# version 03
# baseline (speedup 1.0000x reference)
import math
from contextlib import ExitStack

import numpy as np
import ml_dtypes
import concourse.bass as bass
import concourse.mybir as mybir
from concourse.bass_utils import run_bass_kernel_spmd

F32 = mybir.dt.float32
BF16 = mybir.dt.bfloat16
AF = mybir.ActivationFunctionType
ALU = mybir.AluOpType
AX = mybir.AxisListType

S = 2048
D = 1024
NT = S // 128
NC = D // 128
H, HD = 16, 64
MAGIC = 12582912.0

BF = ml_dtypes.bfloat16


def _host_consts():
    c = {}
    NB, MD = 32, 128
    rel = np.arange(-2047, 2049)
    nb = NB // 2
    buckets = (rel > 0).astype(np.int64) * nb
    rp = np.abs(rel)
    max_exact = nb // 2
    is_small = rp < max_exact
    large = max_exact + (
        np.log(np.maximum(rp, 1) / max_exact) / np.log(MD / max_exact) * (nb - max_exact)
    ).astype(np.int64)
    large = np.minimum(large, nb - 1)
    buckets = buckets + np.where(is_small, rp, large)
    ohrt = np.zeros((NB, 4096), np.float32)
    ohrt[buckets[::-1], np.arange(4096)] = 1.0
    c["ohrt"] = ohrt.astype(BF)
    grp33 = np.zeros((8, 33), np.float32)
    grp33[0:4, 0] = 1.0
    grp33[4:8, 32] = 1.0
    c["grp33"] = grp33.astype(BF)
    c["ident_bf"] = np.eye(128, dtype=np.float32).astype(BF)
    c["ident_f32"] = np.eye(128, dtype=np.float32)
    inv = 1.0 / (10000.0 ** (np.arange(32) / 32.0))
    ang = inv[:, None] * np.arange(S)[None, :]
    c["cos32"] = np.tile(np.cos(ang), (4, 1)).astype(BF)
    c["sin32"] = np.tile(np.sin(ang), (4, 1)).astype(BF)
    caus = np.zeros((128, 4, 512), np.float32)
    for v in range(4):
        off = 128 * v
        caus[:, v, :] = np.where(np.arange(512)[None, :] > off + np.arange(128)[:, None], -1e9, 0.0)
    c["caus"] = caus.reshape(128, 2048).astype(BF)
    perm = np.zeros(D, np.int64)
    for v in range(8):
        for a in range(4):
            for j in range(32):
                perm[v * 128 + a * 32 + j] = (4 * (v // 2) + a) * 64 + 32 * (v % 2) + j
    c["rope_perm"] = perm
    return c


_GNOISE = None


def _gumbel_noise():
    global _GNOISE
    if _GNOISE is None:
        import subprocess, sys, tempfile, os
        fd, path = tempfile.mkstemp(suffix=".npy")
        os.close(fd)
        code = (
            "import jax\n"
            "jax.config.update('jax_platforms', 'cpu')\n"
            "import jax.numpy as jnp, numpy as np, sys\n"
            "n = jax.random.gumbel(jax.random.key(42), (8, 2048, 8), jnp.float32) * 0.05\n"
            f"np.save({path!r}, np.asarray(n))\n"
        )
        env = dict(os.environ)
        env["JAX_PLATFORMS"] = "cpu"
        subprocess.run([sys.executable, "-c", code], check=True, env=env,
                       capture_output=True, timeout=600)
        _GNOISE = np.load(path)
        os.unlink(path)
    return _GNOISE


_CONSTS = _host_consts()


ENGINES = ("sync", "tensor", "vector", "scalar", "gpsimd")


class Emitter:

    def __init__(self, name, fused=False):
        self.nc = bass.Bass("TRN2", target_bir_lowering=False, num_devices=8)
        self.es = ExitStack()
        self.name = name
        self.fused = fused
        self.plan = {e: [] for e in ENGINES}
        self.sems = {}
        self.counts = {}
        self._drams = {}
        self.pool = None
        self._bump = 0
        self._tag = ""
        for e in ENGINES:
            self._mksem(e)
        self._mksem("dma")
        self._mksem("dmag")
        if fused:
            self.pool = self.es.enter_context(self.nc.sbuf_tensor("pool", [128, 212000], mybir.dt.uint8))
            self.ccbuf = self.nc.dram_tensor("ccbuf", [128, 512], F32, kind="Internal").ap()
            self.ccn = 0

    def phase(self, tag):
        self._bump = 0
        self._tag = tag + "_"
        if getattr(self, "_phase_es", None) is not None:
            self._phase_es.close()
        self._phase_es = ExitStack()

    def dr(self, name, shape, dtype=F32, kind="ExternalInput"):
        if name in self._drams:
            return self._drams[name]
        ap = self.nc.dram_tensor(name, shape, dtype, kind=kind).ap()
        self._drams[name] = ap
        return ap

    def allreduce_max(self, src_col_ap, n_vals, after):
        k0 = self.ccn
        self.ccn += n_vals
        assert self.ccn <= 512
        dst = bass.AP(self.ccbuf.tensor, k0, [[1, 1], [1, n_vals]])
        d = self.dma(dst, src_col_ap, after=after, queue="gpsimd")
        cc = self.op("gpsimd", lambda dst=dst: self.nc.gpsimd.collective_compute(
            "AllReduce", ALU.max, replica_groups=[[0, 1, 2, 3, 4, 5, 6, 7]],
            ins=[dst], outs=[dst]), after=[("dmag", d)], inc="gpsimd")
        return dst, cc

    def _mksem(self, n):
        self.sems[n] = self.es.enter_context(self.nc.semaphore("sem_" + n))
        self.counts[n] = 0

    def now(self, sem):
        return self.counts[sem]

    def snap(self):
        return dict(self.counts)

    def op(self, engine, fn, after=(), inc=None):
        if inc is None:
            inc = "dma" if engine == "sync" else engine
        n = 16 if inc in ("dma", "dmag") else 1
        after = tuple(after)
        if engine in ("vector", "scalar", "gpsimd") and self.counts.get(engine, 0) > 0:
            after = after + ((engine, self.counts[engine]),)
        self.counts[inc] += n
        self.plan[engine].append((fn, tuple(after), self.sems[inc], n, inc, self.counts[inc]))
        return self.counts[inc]

    def dma(self, out, in_, after=(), queue="sync"):
        sem = "dma" if queue == "sync" else "dmag"
        eng = self.nc.sync if queue == "sync" else self.nc.gpsimd
        return self.op(queue, lambda o=out, i=in_, g=eng: g.dma_start(out=o, in_=i),
                       after=after, inc=sem)

    def all_now(self):
        return [(s, c) for s, c in self.counts.items() if c > 0]

    def fence(self):
        waits = self.all_now()
        for eng in ENGINES:
            if eng == "sync":
                continue
            self.op(eng, lambda: None, after=waits, inc=eng)

    def verify_acyclic(self):
        prod = {}
        nodes = []
        for eng in ENGINES:
            for i, (fn, after, sem, n, inc, cnt) in enumerate(self.plan[eng]):
                prod[(inc, cnt)] = (eng, i)
                nodes.append((eng, i))
        bysem = {}
        for eng in ENGINES:
            for i, (fn, after, sem, n, inc, cnt) in enumerate(self.plan[eng]):
                bysem.setdefault(inc, []).append((cnt, eng, i))
        for k in bysem:
            bysem[k].sort()
        import bisect as _b
        edges = {nd: [] for nd in nodes}
        indeg = {nd: 0 for nd in nodes}
        def add(a, b):
            edges[a].append(b)
            indeg[b] += 1
        for eng in ENGINES:
            pl = self.plan[eng]
            for i in range(1, len(pl)):
                add((eng, i - 1), (eng, i))
            for i, (fn, after, sem, n, inc, cnt) in enumerate(pl):
                for sname, c in after:
                    if c <= 0:
                        continue
                    lst = bysem.get(sname, [])
                    j = _b.bisect_left(lst, (c, "", -1))
                    assert j < len(lst) and lst[j][0] >= c, (eng, i, sname, c)
                    _, peng, pi = lst[j]
                    add((peng, pi), (eng, i))
        from collections import deque
        q = deque([nd for nd in nodes if indeg[nd] == 0])
        seen = 0
        while q:
            nd = q.popleft()
            seen += 1
            for m in edges[nd]:
                indeg[m] -= 1
                if indeg[m] == 0:
                    q.append(m)
        if seen != len(nodes):
            stuck = [nd for nd in nodes if indeg[nd] > 0]
            print(f"DEADLOCK: {len(nodes)-seen} ops unreachable; first stuck: {stuck[:6]}")
            for nd in stuck[:4]:
                eng, i = nd
                fn, after, sem, n, inc, cnt = self.plan[eng][i]
                print("  stuck op", eng, i, "waits", after)
            return False
        return True

    def emit_and_close(self):
        needed = {}
        for eng in ENGINES:
            for (fn, after, sem, n, inc, cnt) in self.plan[eng]:
                for sname, c in after:
                    if c > 0:
                        needed.setdefault(sname, set()).add(c)
        block = self.es.enter_context(self.nc.Block())
        for eng_name in ENGINES:
            ops = self.plan[eng_name]
            if not ops:
                continue
            last_idx = {}
            for k, op_ in enumerate(ops):
                last_idx[op_[4]] = k

            def section(eng, ops=ops, last_idx=last_idx):
                pending = {}
                for k, (fn, after, sem, n, inc, cnt) in enumerate(ops):
                    for sname, c in after:
                        if c > 0:
                            eng.wait_ge(self.sems[sname], c)
                    inst = fn()
                    pend = pending.get(inc, 0) + n
                    sparse = inc in ("tensor", "vector", "scalar", "gpsimd") and inst is not None
                    take = (not sparse) or (cnt in needed.get(inc, ())) or (k == last_idx[inc])
                    if take:
                        if inst is not None:
                            inst.then_inc(self.sems[inc], pend)
                        else:
                            eng.sem_inc(self.sems[inc], pend)
                        pending[inc] = 0
                    else:
                        pending[inc] = pend
            getattr(block, eng_name)(section)
        self.es.close()
        return self.nc

    def sb(self, name, shape, dtype=F32):
        if self.pool is None:
            return self.es.enter_context(self.nc.sbuf_tensor(name, shape, dtype))
        nbytes = shape[1] * {F32: 4, BF16: 2, mybir.dt.uint8: 1, mybir.dt.int32: 4}[dtype]
        nbytes = (nbytes + 31) // 32 * 32
        assert self._bump + nbytes <= 212000, (self._tag, name, self._bump, nbytes)
        t = self.nc.alloc_sbuf_tensor_at(self._tag + name, shape, dtype,
                                         offset=self.nc.lookup_mloc(self.pool).addr + self._bump)
        self._bump += nbytes
        return t

    def sb_at(self, name, shape, dtype, base, byte_off=0):
        addr = self.nc.lookup_mloc(base).addr + byte_off
        return self.nc.alloc_sbuf_tensor_at(self._tag + name, shape, dtype, offset=addr)

    def psum(self, name, shape, dtype=F32):
        if self.pool is not None and getattr(self, "_phase_es", None) is not None:
            return self._phase_es.enter_context(self.nc.psum_tensor(self._tag + name, shape, dtype))
        return self.es.enter_context(self.nc.psum_tensor(name, shape, dtype))


def rev_free(ap2d, lo, hi):
    pat = [list(d) for d in ap2d.ap]
    fstep = pat[1][0]
    return bass.AP(ap2d.tensor, ap2d.offset + (hi - 1) * fstep, [pat[0], [-fstep, hi - lo]])


def emit_l0attn(e, fused=False, debug=False):
    nc = e.nc
    e.phase("A")
    x = e.dr("x", [S, D], F32)
    WQ = e.dr("wq", [D, D], F32)
    WK = e.dr("wk", [D, D], F32)
    WV = e.dr("wv", [D, D], F32)
    WO = e.dr("wo", [D, D], F32)
    BQ = e.dr("bq", [1, D], F32)
    BK = e.dr("bk", [1, D], F32)
    BV = e.dr("bv", [1, D], F32)
    BO = e.dr("bo", [1, D], F32)
    REL = e.dr("rel", [32, 16], F32)
    GRUW = e.dr("gruw", [64, 8], F32)
    GRUB = e.dr("grub", [1, 8], F32)
    GRUC = e.dr("gruc", [1, 16], F32)
    OHRT = e.dr("ohrt", [32, 4096], BF16)
    GRP33 = e.dr("grp33", [8, 33], BF16)
    IDENT = e.dr("identf", [128, 128], BF16)
    IDF32 = e.dr("identf32", [128, 128], F32)

    ATTN0 = e.dr("attn0", [S, D], F32, kind="Internal" if fused else "ExternalOutput")
    if debug:
        XN_D = e.dr("xn_dbg", [S, D], F32, kind="ExternalOutput")
        QT_D = e.dr("qT_dbg", [D, S], BF16, kind="ExternalOutput")
        KT_D = e.dr("kT_dbg", [D, S], BF16, kind="ExternalOutput")
        VTOK_D = e.dr("vtok_dbg", [S, D], BF16, kind="ExternalOutput")
        GATE_D = e.dr("gate_dbg", [H, S], BF16, kind="ExternalOutput")
        S_D = e.dr("s_dbg", [128, S], F32, kind="ExternalOutput")
        SUM_D = e.dr("sum_dbg", [4, 128], F32, kind="ExternalOutput")
        WN_D = e.dr("wn_dbg", [128, S], BF16, kind="ExternalOutput")
        CTX_D = e.dr("ctx_dbg", [D, S], BF16, kind="ExternalOutput")
        TREV_D = e.dr("trevd_dbg", [16, 4096], BF16, kind="ExternalOutput")
    XNMAX = None if fused else e.dr("xnmax", [1, 1], F32, kind="ExternalOutput")
    TREV = e.dr("trev_d", [16, 4096], BF16, kind="Internal")

    xa = e.sb("xa", [128, D])
    xnT = e.sb("xnT", [128, NC * S], BF16)
    qT = e.sb("qT", [128, NC * S], BF16)
    kT = e.sb("kT", [128, NC * S], BF16)
    vtok_all = e.sb("vtok_all", [128, NT * D], BF16)
    ctxT = e.sb("ctxT", [128, NC * S], BF16)
    w_f32 = e.sb("w_f32", [128, 512])
    w_bf = e.sb("w_bf", [128, NC * 512], BF16)
    scr_bf = e.sb("scr_bf", [128, D], BF16)
    scr32 = e.sb("scr32", [128, D])
    ssb = e.sb("ssb", [128, S])
    wn = e.sb("wn", [128, S], BF16)
    shared = e.sb("shared", [128, S], BF16)
    grow = e.sb("grow", [1, S], BF16)
    gateT = e.sb("gateT", [128, 16])
    mcol = e.sb("mcol", [128, 1])
    vcol = e.sb("vcol", [128, 1])
    negm = e.sb("negm", [128, 1])
    zcol = e.sb("zcol", [128, 1])
    rz = e.sb("rz", [128, 1])
    mx16 = e.sb("mx16", [128, NT])
    mxacc = e.sb("mxacc", [128, 1])
    mx1 = e.sb("mx1", [1, 1])
    ident = e.sb("ident", [128, 128], BF16)
    idf32 = e.sb("idf32", [128, 128])
    rel_sb = e.sb("rel_sb", [32, 16], BF16)
    ohrt_sb = e.sb("ohrt_sb", [32, 128], BF16)
    ttmp = e.sb("ttmp", [128, 16])
    trev_sb = e.sb("trev_sb", [16, 128], BF16)
    gruw_sb = e.sb("gruw_sb", [64, 8], BF16)
    gruwT = e.sb("gruwT", [8, 64], BF16)
    w2d = e.sb("w2d", [128, 33], BF16)
    bias33 = e.sb("bias33", [33, 1])
    grp_sb = e.sb("grp_sb", [8, 33], BF16)
    gcl = e.sb("gcl", [1, 16])
    grub_sb = e.sb("grub_sb", [1, 8])
    grub2 = e.sb("grub2", [1, 2])
    bqc = e.sb("bqc", [128, NC])
    bkc = e.sb("bkc", [128, NC])
    bqs = e.sb("bqs", [128, NC])
    rowbuf_bf = e.sb("rowbuf_bf", [1, D], BF16)
    ones1 = e.sb("ones1", [1, 128], BF16)
    onef = e.sb("onef", [1, 1], BF16)

    ps_k = [e.psum(f"ps_{i}", [128, 512], F32) for i in range(4)]
    ps_t = e.psum("ps_t", [128, 1024], BF16)
    ps_t2 = e.psum("ps_t2", [128, 1024], BF16)
    ps_g = e.psum("ps_g", [128, 512], F32)
    ps_h = e.psum("ps_h", [128, 512], F32)

    T = e.op

    e.dma(ident[:], IDENT[:])
    e.dma(idf32[:], IDF32[:])
    e.dma(rel_sb[:], REL[:], queue="gpsimd")
    e.dma(gruw_sb[:], GRUW[:], queue="gpsimd")
    e.dma(grp_sb[:], GRP33[:])
    e.dma(gcl[:], GRUC[:])
    e.dma(grub_sb[:], GRUB[:])
    for c in range(NC):
        e.dma(bqc[:, c:c + 1], BQ[0:1, c * 128:(c + 1) * 128])
        e.dma(bkc[:, c:c + 1], BK[0:1, c * 128:(c + 1) * 128])
    ld_misc = e.now("dma")
    T("vector", lambda: nc.vector.memset(ones1[:], 1.0))
    T("vector", lambda: nc.vector.memset(onef[:], 1.0))
    T("vector", lambda: nc.vector.memset(bias33[:], 0.0))

    for t in range(NT):
        dx = e.dma(xa[:], x[t * 128:(t + 1) * 128, :], after=[("scalar", e.now("scalar")), ("vector", e.now("vector"))])
        rs = T("vector", lambda: nc.vector.reduce_sum(out=mcol[:], in_=xa[:], axis=AX.X), after=[("dma", dx)])
        if debug and t == 0:
            dtap0 = e.dma(bass.AP(SUM_D.tensor, 0, [[1, 128], [1, 1]]), mcol[:], after=[("vector", rs)])
        sq = T("scalar", lambda: nc.scalar.activation(out=scr32[:], in_=xa[:], func=AF.Square, accum_out=vcol[:]),
               after=[("dma", dx)])
        T("vector", lambda: nc.vector.tensor_scalar(out=mcol[:], in0=mcol[:], scalar1=1.0 / D, scalar2=None, op0=ALU.mult),
          after=[("dma", e.now("dma"))] if (debug and t == 0) else [])
        T("vector", lambda: nc.vector.tensor_scalar(out=vcol[:], in0=vcol[:], scalar1=1.0 / D, scalar2=None, op0=ALU.mult),
          after=[("scalar", sq)])
        T("vector", lambda: nc.vector.tensor_scalar(out=negm[:], in0=mcol[:], scalar1=mcol[:], scalar2=None, op0=ALU.mult))
        T("vector", lambda: nc.vector.tensor_sub(out=vcol[:], in0=vcol[:], in1=negm[:]))
        vv = T("vector", lambda: nc.vector.tensor_scalar(out=vcol[:], in0=vcol[:], scalar1=1e-5, scalar2=None, op0=ALU.add))
        if debug and t == 0:
            e.dma(bass.AP(SUM_D.tensor, 1*128, [[1, 128], [1, 1]]), mcol[:], after=[("vector", vv)])
            e.dma(bass.AP(SUM_D.tensor, 2*128, [[1, 128], [1, 1]]), vcol[:], after=[("vector", vv)])
        sv = T("scalar", lambda: nc.scalar.activation(out=vcol[:], in_=vcol[:], func=AF.Sqrt), after=[("vector", vv)])
        rc = T("vector", lambda: nc.vector.reciprocal(out=vcol[:], in_=vcol[:]), after=[("scalar", sv)])
        if debug and t == 0:
            e.dma(bass.AP(SUM_D.tensor, 3*128, [[1, 128], [1, 1]]), vcol[:], after=[("vector", rc)])
        T("vector", lambda: nc.vector.tensor_scalar(out=xa[:], in0=xa[:], scalar1=mcol[:], scalar2=vcol[:],
                                                    op0=ALU.subtract, op1=ALU.mult),
          after=[("dma", e.now("dma"))] if (debug and t == 0) else [])
        vmx = T("vector", lambda t=t: nc.vector.tensor_reduce(out=mx16[:, t:t + 1], in_=xa[:], axis=AX.X,
                                                              op=ALU.max, apply_absolute_value=True))
        if debug:
            e.dma(XN_D[t * 128:(t + 1) * 128, :], xa[:], after=[("vector", vmx)])
        cs = T("scalar", lambda: nc.scalar.copy(out=scr_bf[:], in_=xa[:]),
               after=[("vector", vmx), ("tensor", e.now("tensor"))])
        for ic in range(NC):
            T("tensor", lambda ic=ic: nc.tensor.transpose(
                out=ps_t[:, ic * 128:(ic + 1) * 128], in_=scr_bf[:, ic * 128:(ic + 1) * 128], identity=ident[:]),
              after=[("scalar", cs), ("dma", e.now("dma"))] if ic == 0 else [])
        tc = e.now("tensor")
        for ic in range(NC):
            T("scalar", lambda t=t, ic=ic: nc.scalar.copy(
                out=xnT[:, ic * S + t * 128: ic * S + (t + 1) * 128], in_=ps_t[:, ic * 128:(ic + 1) * 128]),
              after=[("tensor", tc)] if ic == 0 else [])
    ln_done_v = e.now("vector")

    T("vector", lambda: nc.vector.tensor_reduce(out=mxacc[:], in_=mx16[:], axis=AX.X, op=ALU.max))
    tr1 = T("tensor", lambda: nc.tensor.matmul(out=ps_h[0:1, 0:128], lhsT=mxacc[:], rhs=idf32[:],
                                               start=True, stop=True),
            after=[("vector", e.now("vector")), ("dma", ld_misc)])
    sc1 = T("scalar", lambda: nc.scalar.copy(out=scr32[0:1, 0:128], in_=ps_h[0:1, 0:128]), after=[("tensor", tr1)])
    vm1 = T("vector", lambda: nc.vector.tensor_reduce(out=mx1[:], in_=scr32[0:1, 0:128], axis=AX.X, op=ALU.max),
            after=[("scalar", sc1)])
    if fused:
        e.allreduce_max(mx1[:], 1, after=[("vector", vm1)])
    else:
        e.dma(XNMAX[:], mx1[:], after=[("vector", vm1)])

    for j in range(32):
        dj = e.dma(ohrt_sb[:], OHRT[0:32, j * 128:(j + 1) * 128], after=[("tensor", e.now("tensor"))])
        mm = T("tensor", lambda: nc.tensor.matmul(
            out=ps_h[0:128, 0:16], lhsT=ohrt_sb[:], rhs=rel_sb[:], start=True, stop=True),
            after=[("dma", dj), ("scalar", e.now("scalar"))])
        ev = T("scalar", lambda: nc.scalar.copy(out=ttmp[:], in_=ps_h[0:128, 0:16]), after=[("tensor", mm)])
        tr = T("tensor", lambda: nc.tensor.matmul(out=ps_g[0:16, 0:128], lhsT=ttmp[:], rhs=idf32[:],
                                                  start=True, stop=True), after=[("scalar", ev)])
        sv2 = T("scalar", lambda: nc.scalar.copy(out=trev_sb[:], in_=ps_g[0:16, 0:128]), after=[("tensor", tr)])
        e.dma(TREV[0:16, j * 128:(j + 1) * 128], trev_sb[:], after=[("scalar", sv2)])
    trev_ready = e.now("dma")

    trw = T("tensor", lambda: nc.tensor.matmul(out=ps_g[0:8, 0:64], lhsT=gruw_sb[:], rhs=ident[0:64, 0:64],
                                               start=True, stop=True),
            after=[("dma", ld_misc), ("scalar", e.now("scalar"))])
    ws = T("scalar", lambda: nc.scalar.copy(out=gruwT[:], in_=ps_g[0:8, 0:64]), after=[("tensor", trw)])
    mmw = T("tensor", lambda: nc.tensor.matmul(out=ps_h[0:64, 0:33], lhsT=gruwT[:], rhs=grp_sb[:],
                                               start=True, stop=True), after=[("scalar", ws)])
    w2a = T("scalar", lambda: nc.scalar.copy(out=w2d[0:64, :], in_=ps_h[0:64, 0:33]), after=[("tensor", mmw)])
    T("scalar", lambda: nc.scalar.copy(out=w2d[64:128, :], in_=ps_h[0:64, 0:33]))
    gb = T("vector", lambda: nc.vector.tensor_reduce(
        out=grub2[:], in_=grub_sb[:].rearrange("a (g k) -> a g k", k=4), axis=AX.X, op=ALU.add),
        after=[("dma", ld_misc)])
    e.dma(bias33[0:1, 0:1], grub2[0:1, 0:1], after=[("vector", gb)], queue="gpsimd")
    e.dma(bias33[32:33, 0:1], grub2[0:1, 1:2], after=[("vector", gb)], queue="gpsimd")

    T("vector", lambda: nc.vector.tensor_scalar(out=bqs[:], in0=bqc[:], scalar1=0.125, scalar2=None, op0=ALU.mult),
      after=[("dma", ld_misc)])

    def load_weight_half(W, half):
        for ic in range(NC):
            e.dma(w_bf[:, ic * 512:(ic + 1) * 512], W[ic * 128:(ic + 1) * 128, half * 512:(half + 1) * 512],
                  after=[("tensor", e.now("tensor"))] if ic == 0 else [], queue="gpsimd")
        return e.now("dmag")

    def proj_A(W, outT, bias_cols, scale):
        for half in range(2):
            wld = load_weight_half(W, half)
            for oc4 in range(4):
                oc = half * 4 + oc4
                for nb in range(S // 512):
                    ps = ps_k[nb % 4]
                    deps = [("scalar", e.now("scalar")), ("vector", e.now("vector")), ("dmag", wld)]
                    for ic in range(NC):
                        T("tensor", lambda ic=ic, oc4=oc4, nb=nb, ps=ps: nc.tensor.matmul(
                            out=ps[:], lhsT=w_bf[:, ic * 512 + oc4 * 128: ic * 512 + (oc4 + 1) * 128],
                            rhs=xnT[:, ic * S + nb * 512: ic * S + (nb + 1) * 512],
                            start=(ic == 0), stop=(ic == NC - 1)), after=deps if ic == 0 else [])
                    mmc = e.now("tensor")
                    T("scalar", lambda oc=oc, nb=nb, ps=ps, outT=outT, bias_cols=bias_cols, scale=scale:
                      nc.scalar.activation(out=outT[:, oc * S + nb * 512: oc * S + (nb + 1) * 512], in_=ps[:],
                                           func=AF.Identity, bias=bias_cols[:, oc:oc + 1], scale=scale),
                      after=[("tensor", mmc)])

    proj_A(WQ, qT, bqs, 0.125)
    proj_A(WK, kT, bkc, 1.0)
    dbv = e.dma(rowbuf_bf[:], BV[:], queue="gpsimd")
    for half in range(2):
        wld = load_weight_half(WV, half)
        for t in range(NT):
            ps = ps_k[t % 4]
            deps = [("scalar", e.now("scalar")), ("vector", e.now("vector")), ("dmag", wld)]
            for ic in range(NC):
                T("tensor", lambda t=t, ic=ic, half=half, ps=ps: nc.tensor.matmul(
                    out=ps[:], lhsT=xnT[:, ic * S + t * 128: ic * S + (t + 1) * 128],
                    rhs=w_bf[:, ic * 512:(ic + 1) * 512],
                    start=(ic == 0), stop=False), after=deps if ic == 0 else [])
            T("tensor", lambda half=half, ps=ps: nc.tensor.matmul(
                out=ps[:], lhsT=ones1[:], rhs=rowbuf_bf[0:1, half * 512:(half + 1) * 512],
                start=False, stop=True))
            mmc = e.now("tensor")
            T("scalar", lambda t=t, half=half, ps=ps: nc.scalar.copy(
                out=vtok_all[:, t * D + half * 512: t * D + (half + 1) * 512], in_=ps[:]),
              after=[("tensor", mmc)])

    if debug:
        for c in range(NC):
            e.dma(QT_D[c * 128:(c + 1) * 128, :], qT[:, c * S:(c + 1) * S], after=[("scalar", e.now("scalar"))])
            e.dma(KT_D[c * 128:(c + 1) * 128, :], kT[:, c * S:(c + 1) * S], after=[("scalar", e.now("scalar"))])
        for t in range(NT):
            e.dma(VTOK_D[t * 128:(t + 1) * 128, :], vtok_all[:, t * D:(t + 1) * D], after=[("scalar", e.now("scalar"))])
        e.dma(TREV_D[:], TREV[:], after=[("dma", trev_ready)])

    for h in range(H):
        c2 = h // 2
        pb = 64 * (h % 2)

        for nb in range(4):
            mm = T("tensor", lambda c2=c2, pb=pb, nb=nb: nc.tensor.matmul(
                out=ps_g[0:33, 0:512], lhsT=w2d[pb:pb + 64, :],
                rhs=xnT[pb:pb + 64, c2 * S + nb * 512: c2 * S + (nb + 1) * 512], start=True, stop=True),
                after=[("scalar", e.now("scalar")), ("vector", e.now("vector")), ("dmag", e.now("dmag"))])
            T("scalar", lambda nb=nb: nc.scalar.activation(
                out=shared[0:33, nb * 512:(nb + 1) * 512], in_=ps_g[0:33, 0:512], func=AF.Sigmoid, bias=bias33[:]),
              after=[("tensor", mm)])
        sgc = e.now("scalar")
        mvd = e.dma(grow[:], shared[32:33, :], after=[("scalar", sgc)], queue="gpsimd")
        T("vector", lambda h=h: nc.vector.tensor_scalar(out=grow[:], in0=grow[:], scalar1=gcl[0:1, h:h + 1],
                                                        scalar2=-1.0, op0=ALU.mult, op1=ALU.add),
          after=[("dmag", mvd)])
        T("vector", lambda: nc.vector.tensor_tensor(out=grow[:], in0=shared[0:1, :], in1=grow[:], op=ALU.mult))
        vg = T("vector", lambda: nc.vector.tensor_scalar(out=grow[:], in0=grow[:], scalar1=2.0, scalar2=None, op0=ALU.add))
        if debug:
            e.dma(GATE_D[h:h + 1, :], grow[:], after=[("vector", vg)])
        for qt in range(NT):
            T("tensor", lambda qt=qt: nc.tensor.matmul(
                out=ps_h[0:128, qt:qt + 1], lhsT=grow[0:1, qt * 128:(qt + 1) * 128], rhs=onef[:],
                start=True, stop=True),
              after=[("vector", vg), ("scalar", e.now("scalar"))] if qt == 0 else [])
        gtc = T("scalar", lambda: nc.scalar.copy(out=gateT[:], in_=ps_h[0:128, 0:16]),
                after=[("tensor", e.now("tensor"))])

        for qt in range(NT):
            win = bass.AP(TREV.tensor, h * 4096 + 128 * qt, [[1, 128], [1, S]])
            twd = e.dma(shared[:], win, after=[("dma", trev_ready), ("vector", e.now("vector")),
                                               ("tensor", e.now("tensor"))])
            for kb in range(4):
                T("tensor", lambda c2=c2, pb=pb, qt=qt, kb=kb: nc.tensor.matmul(
                    out=ps_k[kb][:], lhsT=qT[pb:pb + 64, c2 * S + qt * 128: c2 * S + (qt + 1) * 128],
                    rhs=kT[pb:pb + 64, c2 * S + kb * 512: c2 * S + (kb + 1) * 512], start=True, stop=True),
                  after=[("scalar", gtc), ("vector", e.now("vector"))] if kb == 0 else [])
            mmc = e.now("tensor")
            for kb in range(4):
                T("vector", lambda qt=qt, kb=kb: nc.vector.scalar_tensor_tensor(
                    out=ssb[:, kb * 512:(kb + 1) * 512], in0=rev_free(shared[:], S - (kb + 1) * 512, S - kb * 512),
                    scalar=gateT[:, qt:qt + 1], in1=ps_k[kb][:], op0=ALU.mult, op1=ALU.add),
                  after=[("tensor", mmc), ("dma", twd)] if kb == 0 else [])
            if debug and h == 0 and qt == 0:
                e.dma(S_D[:], ssb[:], after=[("vector", e.now("vector"))])
            T("vector", lambda: nc.vector.tensor_reduce(out=mcol[:], in_=ssb[:], axis=AX.X, op=ALU.max))
            T("vector", lambda: nc.vector.tensor_scalar(out=negm[:], in0=mcol[:], scalar1=-1.0, scalar2=None, op0=ALU.mult))
            ex = T("scalar", lambda: nc.scalar.activation(out=wn[:], in_=ssb[:], func=AF.Exp, bias=negm[:], accum_out=zcol[:]),
                   after=[("vector", e.now("vector"))])
            T("vector", lambda: nc.vector.reciprocal(out=rz[:], in_=zcol[:]), after=[("scalar", ex)])
            nw = T("vector", lambda: nc.vector.tensor_scalar(out=wn[:], in0=wn[:], scalar1=rz[:], scalar2=None, op0=ALU.mult))
            if debug and h == 0 and qt == 0:
                e.dma(WN_D[:], wn[:], after=[("vector", nw)])
            for c in range(16):
                pst = ps_t if c < 8 else ps_t2
                T("tensor", lambda c=c, pst=pst: nc.tensor.transpose(
                    out=pst[:, (c % 8) * 128:(c % 8 + 1) * 128], in_=wn[:, c * 128:(c + 1) * 128], identity=ident[:]),
                  after=[("vector", nw), ("scalar", e.now("scalar"))] if c == 0 else [])
            tcn = e.now("tensor")
            T("scalar", lambda: nc.scalar.copy(out=shared[:, 0:1024], in_=ps_t[:]), after=[("tensor", tcn)])
            ev2 = T("scalar", lambda: nc.scalar.copy(out=shared[:, 1024:2048], in_=ps_t2[:]))
            for c in range(16):
                T("tensor", lambda c=c, h=h: nc.tensor.matmul(
                    out=ps_h[0:64, 0:128], lhsT=vtok_all[:, c * D + h * 64: c * D + h * 64 + 64],
                    rhs=shared[:, c * 128:(c + 1) * 128],
                    start=(c == 0), stop=(c == 15)),
                  after=[("scalar", ev2)] if c == 0 else [])
            pvc = e.now("tensor")
            T("scalar", lambda c2=c2, pb=pb, qt=qt: nc.scalar.copy(
                out=ctxT[pb:pb + 64, c2 * S + qt * 128: c2 * S + (qt + 1) * 128], in_=ps_h[0:64, 0:128]),
              after=[("tensor", pvc)])

    if debug:
        for c in range(NC):
            e.dma(CTX_D[c * 128:(c + 1) * 128, :], ctxT[:, c * S:(c + 1) * S], after=[("scalar", e.now("scalar"))])

    dbo = e.dma(rowbuf_bf[:], BO[:], after=[("tensor", e.now("tensor"))], queue="gpsimd")
    for half in range(2):
        wld = load_weight_half(WO, half)
        for t in range(NT):
            nb = half
            ps = ps_k[t % 4]
            deps = [("scalar", e.now("scalar")), ("vector", e.now("vector")), ("dmag", wld)]
            for ic in range(NC):
                T("tensor", lambda t=t, ic=ic, ps=ps: nc.tensor.matmul(
                    out=ps[:], lhsT=ctxT[:, ic * S + t * 128: ic * S + (t + 1) * 128],
                    rhs=w_bf[:, ic * 512:(ic + 1) * 512],
                    start=(ic == 0), stop=False), after=deps if ic == 0 else [])
            T("tensor", lambda half=half, ps=ps: nc.tensor.matmul(
                out=ps[:], lhsT=ones1[:], rhs=rowbuf_bf[0:1, half * 512:(half + 1) * 512],
                start=False, stop=True))
            mmc = e.now("tensor")
            sv3 = T("scalar", lambda ps=ps: nc.scalar.copy(out=scr32[:, 0:512], in_=ps[:]), after=[("tensor", mmc)])
            e.dma(ATTN0[t * 128:(t + 1) * 128, nb * 512:(nb + 1) * 512], scr32[:, 0:512], after=[("scalar", sv3)])

    return


def build_launch1(debug=False):
    e = Emitter("l1")
    emit_l0attn(e, fused=False, debug=debug)
    e.op("gpsimd", lambda: None, after=e.all_now())
    return e.emit_and_close()


def _l1_in_map(inputs, c):
    consts = _CONSTS
    m = {
        "x": inputs["L0_attn_x"][c],
        "wq": inputs["L0_attn_Wq"], "wk": inputs["L0_attn_Wk"],
        "wv": inputs["L0_attn_Wv"], "wo": inputs["L0_attn_Wo"],
        "bq": inputs["L0_attn_bq"].reshape(1, D), "bk": inputs["L0_attn_bk"].reshape(1, D),
        "bv": inputs["L0_attn_bv"].reshape(1, D), "bo": inputs["L0_attn_bo"].reshape(1, D),
        "rel": inputs["L0_attn_rel_embed"], "gruw": inputs["L0_attn_gru_w"],
        "grub": inputs["L0_attn_gru_b"].reshape(1, 8), "gruc": inputs["L0_attn_gru_const"].reshape(1, 16),
        "ohrt": consts["ohrt"], "grp33": consts["grp33"],
        "identf": consts["ident_bf"], "identf32": consts["ident_f32"],
    }
    return {k: np.ascontiguousarray(v) for k, v in m.items()}


def _ffn_in_map(inputs, c, stage, st):
    consts = _CONSTS
    m = {
        "x": inputs["L0_attn_x"][c],
        "xnmax8": st["xnmax8"],
        "wup": inputs["L0_ffn_w_up"], "wgate": inputs["L0_ffn_w_gate"],
        "identf32": consts["ident_f32"], "identf": consts["ident_bf"],
    }
    if stage >= 2:
        m["ugmax8"] = st["ugmax8"]
    if stage >= 3:
        m["g2max8"] = st["g2max8"]
    if stage >= 4:
        m["omax8"] = st["omax8"]
        m["wdown"] = inputs["L0_ffn_w_down"]
        m["attn0"] = st["attn0"][c]
    return {k: np.ascontiguousarray(v) for k, v in m.items()}


def run_l0(inputs):
    st = {}
    nc1 = build_launch1()
    res = run_bass_kernel_spmd(nc1, [_l1_in_map(inputs, c) for c in range(8)], core_ids=list(range(8)))
    st["attn0"] = np.stack([np.asarray(res.results[c]["attn0"]) for c in range(8)])
    st["xnmax8"] = np.concatenate([np.asarray(res.results[c]["xnmax"]) for c in range(8)], axis=1)

    nc = build_l0ffn(1)
    res = run_bass_kernel_spmd(nc, [_ffn_in_map(inputs, c, 1, st) for c in range(8)], core_ids=list(range(8)))
    st["ugmax8"] = np.concatenate([np.asarray(res.results[c]["ugmax"]) for c in range(8)], axis=1)

    nc = build_l0ffn(2)
    res = run_bass_kernel_spmd(nc, [_ffn_in_map(inputs, c, 2, st) for c in range(8)], core_ids=list(range(8)))
    st["g2max8"] = np.concatenate([np.asarray(res.results[c]["g2max"]) for c in range(8)], axis=1)

    nc = build_l0ffn(3)
    res = run_bass_kernel_spmd(nc, [_ffn_in_map(inputs, c, 3, st) for c in range(8)], core_ids=list(range(8)))
    st["omax8"] = np.concatenate([np.asarray(res.results[c]["omax"]) for c in range(8)], axis=1)

    nc = build_l0ffn(4)
    res = run_bass_kernel_spmd(nc, [_ffn_in_map(inputs, c, 4, st) for c in range(8)], core_ids=list(range(8)))
    x1 = np.stack([np.asarray(res.results[c]["x1"]) for c in range(8)])
    return x1


def _l6_in_map(inputs, c, x1):
    consts = _CONSTS
    gn = _gumbel_noise()
    m = {
        "x1": x1[c] if x1 is not None else np.zeros((S, D), np.float32),
        "awq": inputs["L1_attn_Wq"], "awk": inputs["L1_attn_Wk"],
        "awv": inputs["L1_attn_Wv"], "awo": inputs["L1_attn_Wo"],
        "abq": inputs["L1_attn_bq"].reshape(1, D), "abk": inputs["L1_attn_bk"].reshape(1, D),
        "abv": inputs["L1_attn_bv"].reshape(1, D), "abo": inputs["L1_attn_bo"].reshape(1, D),
        "scale1": inputs["L1_ffn_scale1"].reshape(1, D), "scale2": inputs["L1_ffn_scale2"].reshape(1, D),
        "fwq": inputs["L1_ffn_wq"][:, consts["rope_perm"]], "fwk": inputs["L1_ffn_wk"][:, consts["rope_perm"]],
        "fwv": inputs["L1_ffn_wv"], "fwo": inputs["L1_ffn_wo"],
        "router": inputs["L1_ffn_router_w"],
        "w1": inputs["L1_ffn_w1"].reshape(8 * D, D), "w2": inputs["L1_ffn_w2"].reshape(8 * D, D),
        "wout": inputs["L1_ffn_wout"].reshape(8 * D, D),
        "b1": inputs["L1_ffn_b1"], "b2": inputs["L1_ffn_b2"], "bout": inputs["L1_ffn_bout"],
        "gnoise": gn[c], "cos32": consts["cos32"], "sin32": consts["sin32"],
        "identf": consts["ident_bf"], "identf32": consts["ident_f32"], "caus": consts["caus"],
    }
    return {k: np.ascontiguousarray(v) for k, v in m.items()}


def kernel_multi(**inputs):
    inputs = {k: np.asarray(v, dtype=np.float32) for k, v in inputs.items()}
    x1 = run_l0(inputs)
    nc6 = build_l1()
    res = run_bass_kernel_spmd(nc6, [_l6_in_map(inputs, c, x1) for c in range(8)], core_ids=list(range(8)))
    x2 = np.stack([np.asarray(res.results[c]["x2"]) for c in range(8)])
    return x2


def kernel(**inputs):
    return kernel_fused(**inputs)



def build_l0ffn(stage, debug=False):
    e = Emitter(f"ffn{stage}")
    nc = e.nc
    x = nc.dram_tensor("x", [S, D], F32, kind="ExternalInput").ap()
    XNMAX8 = nc.dram_tensor("xnmax8", [1, 8], F32, kind="ExternalInput").ap()
    WUP = nc.dram_tensor("wup", [3072, D], F32, kind="ExternalInput").ap()
    WGATE = nc.dram_tensor("wgate", [3072, D], F32, kind="ExternalInput").ap()
    IDF32 = nc.dram_tensor("identf32", [128, 128], F32, kind="ExternalInput").ap()
    IDENT = nc.dram_tensor("identf", [128, 128], BF16, kind="ExternalInput").ap()
    if stage >= 2:
        UGMAX8 = nc.dram_tensor("ugmax8", [2, 8], F32, kind="ExternalInput").ap()
    if stage >= 3:
        G2MAX8 = nc.dram_tensor("g2max8", [1, 8], F32, kind="ExternalInput").ap()
    if stage >= 4:
        OMAX8 = nc.dram_tensor("omax8", [1, 8], F32, kind="ExternalInput").ap()
        WDOWN = nc.dram_tensor("wdown", [D, 3072], F32, kind="ExternalInput").ap()
        ATTN0 = nc.dram_tensor("attn0", [S, D], F32, kind="ExternalInput").ap()
        X1 = nc.dram_tensor("x1", [S, D], F32, kind="ExternalOutput").ap()
        ODRAM = nc.dram_tensor("od", [3072, S], BF16, kind="Internal").ap()
    if stage == 1:
        UGMAX = nc.dram_tensor("ugmax", [2, 1], F32, kind="ExternalOutput").ap()
    if stage == 2:
        G2MAX = nc.dram_tensor("g2max", [1, 1], F32, kind="ExternalOutput").ap()
    if stage == 3:
        OMAX = nc.dram_tensor("omax", [1, 1], F32, kind="ExternalOutput").ap()
    if debug:
        UPQ_D = nc.dram_tensor("upq_dbg", [3072, S], F32, kind="ExternalOutput").ap()
        G2_D = nc.dram_tensor("g2_dbg", [3072, S], F32, kind="ExternalOutput").ap()

    xa = e.sb("xa", [128, D])
    xqT = e.sb("xqT", [128, NC * S], BF16)
    scr32 = e.sb("scr32", [128, D])
    scr_bf = e.sb("scr_bf", [128, D], BF16)
    w32 = e.sb("w32", [128, D])
    wbf = e.sb("wbf", [128, D], BF16)
    wT = e.sb("wT", [128, NC * 128], BF16)
    bm = e.sb("bm", [128, 32])
    rbm = e.sb("rbm", [128, 32])
    upf = e.sb("upf", [128, S])
    gf = e.sb("gf", [128, S])
    sigf = e.sb("sigf", [128, S])
    obf = e.sb("obf", [128, S], BF16)
    mcol = e.sb("mcol", [128, 1])
    vcol = e.sb("vcol", [128, 1])
    tcol = e.sb("tcol", [128, 1])
    mx_up = e.sb("mx_up", [128, 1])
    mx_g = e.sb("mx_g", [128, 1])
    row8 = e.sb("row8", [2, 8])
    s_sc = e.sb("s_sc", [2, 1])
    scol_xn = e.sb("scol_xn", [128, 1])
    scol_up = e.sb("scol_up", [128, 1])
    scol_g = e.sb("scol_g", [128, 1])
    scol_g2 = e.sb("scol_g2", [128, 1])
    scol_o = e.sb("scol_o", [128, 1])
    onef32 = e.sb("onef32", [1, 128])
    idf32 = e.sb("idf32", [128, 128])
    identb = e.sb("identb", [128, 128], BF16)
    mrow = e.sb("mrow", [1, 128])

    ps_k = [e.psum(f"ps_{i}", [128, 512], F32) for i in range(4)]
    ps_g2 = [e.psum(f"psg_{i}", [128, 512], F32) for i in range(2)]
    ps_t = e.psum("ps_t", [128, 1024], BF16)
    ps_h = e.psum("ps_h", [128, 512], F32)

    T = e.op
    e.dma(idf32[:], IDF32[:])
    e.dma(identb[:], IDENT[:])
    T("vector", lambda: nc.vector.memset(onef32[:], 1.0))

    def bcast_col(src11, dst_col, dep):
        mm = T("tensor", lambda: nc.tensor.matmul(out=ps_h[0:128, 0:1], lhsT=onef32[:], rhs=src11,
                                                  start=True, stop=True),
               after=dep + [("scalar", e.now("scalar")), ("dma", e.now("dma"))])
        return T("scalar", lambda: nc.scalar.copy(out=dst_col[:], in_=ps_h[0:128, 0:1]), after=[("tensor", mm)])

    def scale_from_row(src_row, dst_col):
        d = e.dma(row8[0:1, :], src_row, after=[("vector", e.now("vector")), ("tensor", e.now("tensor"))],
                  queue="gpsimd")
        v1 = T("vector", lambda: nc.vector.tensor_reduce(out=s_sc[0:1, :], in_=row8[0:1, :], axis=AX.X, op=ALU.max),
               after=[("dmag", d), ("tensor", e.now("tensor"))])
        T("vector", lambda: nc.vector.tensor_scalar(out=s_sc[0:1, :], in0=s_sc[0:1, :], scalar1=1.0 / 32767.0,
                                                    scalar2=1e-12, op0=ALU.mult, op1=ALU.max))
        return bcast_col(s_sc[0:1, :], dst_col, [("vector", e.now("vector"))])

    scale_from_row(XNMAX8[0:1, :], scol_xn)
    if stage >= 2:
        scale_from_row(UGMAX8[0:1, :], scol_up)
        scale_from_row(UGMAX8[1:2, :], scol_g)
    if stage >= 3:
        scale_from_row(G2MAX8[0:1, :], scol_g2)
    if stage >= 4:
        scale_from_row(OMAX8[0:1, :], scol_o)

    rcol = e.sb("rcol", [128, 1])

    def qdq_inplace(ap, scol):
        T("vector", lambda: nc.vector.reciprocal(out=rcol[:], in_=scol[:]),
          after=[("scalar", e.now("scalar"))])
        T("vector", lambda: nc.vector.tensor_scalar(out=ap, in0=ap, scalar1=rcol[:], scalar2=None, op0=ALU.mult))
        T("vector", lambda: nc.vector.tensor_scalar(out=ap, in0=ap, scalar1=MAGIC, scalar2=-MAGIC, op0=ALU.add, op1=ALU.add))
        T("vector", lambda: nc.vector.tensor_scalar(out=ap, in0=ap, scalar1=scol[:], scalar2=None, op0=ALU.mult))

    for t in range(NT):
        dx = e.dma(xa[:], x[t * 128:(t + 1) * 128, :], after=[("scalar", e.now("scalar")), ("vector", e.now("vector"))])
        T("vector", lambda: nc.vector.reduce_sum(out=mcol[:], in_=xa[:], axis=AX.X), after=[("dma", dx)])
        sq = T("scalar", lambda: nc.scalar.activation(out=scr32[:], in_=xa[:], func=AF.Square, accum_out=vcol[:]),
               after=[("dma", dx)])
        T("vector", lambda: nc.vector.tensor_scalar(out=mcol[:], in0=mcol[:], scalar1=1.0 / D, scalar2=None, op0=ALU.mult))
        T("vector", lambda: nc.vector.tensor_scalar(out=vcol[:], in0=vcol[:], scalar1=1.0 / D, scalar2=None, op0=ALU.mult),
          after=[("scalar", sq)])
        T("vector", lambda: nc.vector.tensor_scalar(out=tcol[:], in0=mcol[:], scalar1=mcol[:], scalar2=None, op0=ALU.mult))
        T("vector", lambda: nc.vector.tensor_sub(out=vcol[:], in0=vcol[:], in1=tcol[:]))
        vv = T("vector", lambda: nc.vector.tensor_scalar(out=vcol[:], in0=vcol[:], scalar1=1e-5, scalar2=None, op0=ALU.add))
        sv = T("scalar", lambda: nc.scalar.activation(out=vcol[:], in_=vcol[:], func=AF.Sqrt), after=[("vector", vv)])
        T("vector", lambda: nc.vector.reciprocal(out=vcol[:], in_=vcol[:]), after=[("scalar", sv)])
        T("vector", lambda: nc.vector.tensor_scalar(out=xa[:], in0=xa[:], scalar1=mcol[:], scalar2=vcol[:],
                                                    op0=ALU.subtract, op1=ALU.mult))
        qdq_inplace(xa[:], scol_xn)
        cs = T("scalar", lambda: nc.scalar.copy(out=scr_bf[:], in_=xa[:]),
               after=[("vector", e.now("vector")), ("tensor", e.now("tensor"))])
        for ic in range(NC):
            T("tensor", lambda ic=ic: nc.tensor.transpose(
                out=ps_t[:, ic * 128:(ic + 1) * 128], in_=scr_bf[:, ic * 128:(ic + 1) * 128], identity=identb[:]),
              after=[("scalar", cs)] if ic == 0 else [])
        tc = e.now("tensor")
        for ic in range(NC):
            T("scalar", lambda t=t, ic=ic: nc.scalar.copy(
                out=xqT[:, ic * S + t * 128: ic * S + (t + 1) * 128], in_=ps_t[:, ic * 128:(ic + 1) * 128]),
              after=[("tensor", tc)] if ic == 0 else [])
    T("vector", lambda: nc.vector.memset(mx_up[:], 0.0))
    T("vector", lambda: nc.vector.memset(mx_g[:], 0.0))

    def dequant_to_wT(Wsrc, oc, icols=D):
        nblk = icols // 32
        d = e.dma(w32[:, 0:icols], Wsrc[oc * 128:(oc + 1) * 128, :],
                  after=[("vector", e.now("vector")), ("tensor", e.now("tensor"))])
        T("vector", lambda: nc.vector.tensor_reduce(
            out=bm[:, 0:nblk], in_=w32[:, 0:icols].rearrange("p (b k) -> p b k", k=32),
            axis=AX.X, op=ALU.max, apply_absolute_value=True), after=[("dma", d)])
        T("vector", lambda: nc.vector.tensor_scalar(out=bm[:, 0:nblk], in0=bm[:, 0:nblk], scalar1=1.0 / 7.0,
                                                    scalar2=1e-12, op0=ALU.mult, op1=ALU.max))
        w3 = w32[:, 0:icols].rearrange("p (b k) -> p b k", k=32)
        T("vector", lambda: nc.vector.reciprocal(out=rbm[:, 0:nblk], in_=bm[:, 0:nblk]))
        bmap = bm[:, 0:nblk]
        b3 = bass.AP(bmap.tensor, bmap.offset, [list(bmap.ap[0]), [bmap.ap[1][0], nblk], [0, 32]])
        rmap = rbm[:, 0:nblk]
        r3 = bass.AP(rmap.tensor, rmap.offset, [list(rmap.ap[0]), [rmap.ap[1][0], nblk], [0, 32]])
        T("vector", lambda: nc.vector.tensor_tensor(out=w3, in0=w3, in1=r3, op=ALU.mult))
        T("vector", lambda: nc.vector.tensor_scalar(out=w32[:, 0:icols], in0=w32[:, 0:icols], scalar1=MAGIC,
                                                    scalar2=-MAGIC, op0=ALU.add, op1=ALU.add))
        wv = T("vector", lambda: nc.vector.tensor_tensor(out=w3, in0=w3, in1=b3, op=ALU.mult))
        cb = T("scalar", lambda: nc.scalar.copy(out=wbf[:, 0:icols], in_=w32[:, 0:icols]), after=[("vector", wv)])
        nic = icols // 128
        for ic in range(nic):
            T("tensor", lambda ic=ic: nc.tensor.transpose(
                out=ps_t[:, (ic % 8) * 128:(ic % 8 + 1) * 128], in_=wbf[:, ic * 128:(ic + 1) * 128], identity=identb[:]),
              after=[("scalar", cb), ("scalar", e.now("scalar"))] if ic == 0 else [])
        tc = e.now("tensor")
        for ic in range(nic):
            T("scalar", lambda ic=ic: nc.scalar.copy(
                out=wT[:, ic * 128:(ic + 1) * 128], in_=ps_t[:, (ic % 8) * 128:(ic % 8 + 1) * 128]),
              after=[("tensor", tc)] if ic == 0 else [])
        return e.now("scalar")

    for oc in range(24):
        wld = dequant_to_wT(WGATE, oc)
        for nb in range(4):
            ps = ps_g2[nb % 2]
            deps = [("scalar", e.now("scalar")), ("vector", e.now("vector"))]
            for ic in range(NC):
                T("tensor", lambda ic=ic, nb=nb, ps=ps: nc.tensor.matmul(
                    out=ps[:], lhsT=wT[:, ic * 128:(ic + 1) * 128],
                    rhs=xqT[:, ic * S + nb * 512: ic * S + (nb + 1) * 512],
                    start=(ic == 0), stop=(ic == NC - 1)), after=deps if ic == 0 else [])
            mmc = e.now("tensor")
            if stage == 1:
                T("vector", lambda ps=ps: nc.vector.tensor_reduce(out=tcol[:], in_=ps[:], axis=AX.X,
                                                                  op=ALU.max, apply_absolute_value=True),
                  after=[("tensor", mmc)])
                T("vector", lambda: nc.vector.tensor_max(out=mx_g[:], in0=mx_g[:], in1=tcol[:]))
            else:
                T("scalar", lambda nb=nb, ps=ps: nc.scalar.copy(out=gf[:, nb * 512:(nb + 1) * 512], in_=ps[:]),
                  after=[("tensor", mmc)])
        if stage >= 2:
            qdq_inplace(gf[:], scol_g)
            sg = T("scalar", lambda: nc.scalar.activation(out=sigf[:], in_=gf[:], func=AF.Sigmoid),
                   after=[("vector", e.now("vector"))])
            T("vector", lambda: nc.vector.tensor_scalar(out=sigf[:], in0=sigf[:], scalar1=65536.0, scalar2=MAGIC,
                                                        op0=ALU.mult, op1=ALU.add), after=[("scalar", sg)])
            T("vector", lambda: nc.vector.tensor_scalar(out=sigf[:], in0=sigf[:], scalar1=-MAGIC, scalar2=65535.0,
                                                        op0=ALU.add, op1=ALU.min))
            g2v = T("vector", lambda: nc.vector.scalar_tensor_tensor(out=gf[:], in0=sigf[:], scalar=1.0 / 65536.0,
                                                                     in1=gf[:], op0=ALU.mult, op1=ALU.mult))
            if stage == 2:
                T("vector", lambda: nc.vector.tensor_reduce(out=tcol[:], in_=gf[:], axis=AX.X,
                                                            op=ALU.max, apply_absolute_value=True))
                T("vector", lambda: nc.vector.tensor_max(out=mx_g[:], in0=mx_g[:], in1=tcol[:]))
        if stage >= 3:
            qdq_inplace(gf[:], scol_g2)
        if stage == 1 or stage >= 3:
            wld = dequant_to_wT(WUP, oc)
            for nb in range(4):
                ps = ps_k[nb % 2 + 2]
                deps = [("scalar", e.now("scalar")), ("vector", e.now("vector"))]
                for ic in range(NC):
                    T("tensor", lambda ic=ic, nb=nb, ps=ps: nc.tensor.matmul(
                        out=ps[:], lhsT=wT[:, ic * 128:(ic + 1) * 128],
                        rhs=xqT[:, ic * S + nb * 512: ic * S + (nb + 1) * 512],
                        start=(ic == 0), stop=(ic == NC - 1)), after=deps if ic == 0 else [])
                mmc = e.now("tensor")
                if stage == 1:
                    T("vector", lambda ps=ps: nc.vector.tensor_reduce(out=tcol[:], in_=ps[:], axis=AX.X,
                                                                      op=ALU.max, apply_absolute_value=True),
                      after=[("tensor", mmc)])
                    T("vector", lambda: nc.vector.tensor_max(out=mx_up[:], in0=mx_up[:], in1=tcol[:]))
                else:
                    T("scalar", lambda nb=nb, ps=ps: nc.scalar.copy(out=upf[:, nb * 512:(nb + 1) * 512], in_=ps[:]),
                      after=[("tensor", mmc)])
        if stage >= 3:
            qdq_inplace(upf[:], scol_up)
            if debug:
                dmp = e.dma(UPQ_D[oc * 128:(oc + 1) * 128, :], upf[:], after=[("vector", e.now("vector"))])
                T("vector", lambda: nc.vector.tensor_copy(out=tcol[:], in_=tcol[:]), after=[("dma", dmp)])
            ov = T("vector", lambda: nc.vector.tensor_mul(out=gf[:], in0=gf[:], in1=upf[:]))
            if stage == 3:
                T("vector", lambda: nc.vector.tensor_reduce(out=tcol[:], in_=gf[:], axis=AX.X,
                                                            op=ALU.max, apply_absolute_value=True))
                T("vector", lambda: nc.vector.tensor_max(out=mx_g[:], in0=mx_g[:], in1=tcol[:]))
            if debug:
                dmp = e.dma(G2_D[oc * 128:(oc + 1) * 128, :], gf[:], after=[("vector", e.now("vector"))])
                T("vector", lambda: nc.vector.tensor_copy(out=tcol[:], in_=tcol[:]), after=[("dma", dmp)])
        if stage >= 4:
            qdq_inplace(gf[:], scol_o)
            ob = T("vector", lambda: nc.vector.tensor_copy(out=obf[:], in_=gf[:]), after=[("dma", e.now("dma"))])
            e.dma(ODRAM[oc * 128:(oc + 1) * 128, :], obf[:], after=[("vector", ob)])

    def col_to_scalar_max(col, dst11):
        mm = T("tensor", lambda: nc.tensor.matmul(out=ps_h[0:1, 0:128], lhsT=col[:], rhs=idf32[:],
                                                  start=True, stop=True),
               after=[("vector", e.now("vector")), ("scalar", e.now("scalar"))])
        sc = T("scalar", lambda: nc.scalar.copy(out=mrow[:], in_=ps_h[0:1, 0:128]), after=[("tensor", mm)])
        return T("vector", lambda: nc.vector.tensor_reduce(out=dst11, in_=mrow[:], axis=AX.X, op=ALU.max),
                 after=[("scalar", sc)])

    if stage == 1:
        v1 = col_to_scalar_max(mx_up, s_sc[0:1, 0:1])
        e.dma(UGMAX[0:1, :], s_sc[0:1, 0:1], after=[("vector", v1)])
        v2 = col_to_scalar_max(mx_g, mrow[0:1, 0:1])
        e.dma(UGMAX[1:2, :], mrow[0:1, 0:1], after=[("vector", v2)])
    if stage == 2:
        v2 = col_to_scalar_max(mx_g, s_sc[0:1, 0:1])
        e.dma(G2MAX[:], s_sc[0:1, 0:1], after=[("vector", v2)])
    if stage == 3:
        v2 = col_to_scalar_max(mx_g, s_sc[0:1, 0:1])
        e.dma(OMAX[:], s_sc[0:1, 0:1], after=[("vector", v2)])

    if stage >= 4:
        wdT = e.sb("wdT", [128, 24 * 1024], BF16)
        for c in range(NC):
            d = e.dma(w32[:, 0:D] if False else w32[:], WDOWN[c * 128:(c + 1) * 128, 0:1024],
                      after=[("vector", e.now("vector")), ("tensor", e.now("tensor"))])
            for g3 in range(3):
                dg = e.dma(w32[:], WDOWN[c * 128:(c + 1) * 128, g3 * 1024:(g3 + 1) * 1024],
                           after=[("vector", e.now("vector")), ("tensor", e.now("tensor"))]) if g3 > 0 else d
                nblk = 32
                T("vector", lambda: nc.vector.tensor_reduce(
                    out=bm[:, 0:nblk], in_=w32[:].rearrange("p (b k) -> p b k", k=32),
                    axis=AX.X, op=ALU.max, apply_absolute_value=True), after=[("dma", dg)])
                T("vector", lambda: nc.vector.tensor_scalar(out=bm[:, 0:nblk], in0=bm[:, 0:nblk], scalar1=1.0 / 7.0,
                                                            scalar2=1e-12, op0=ALU.mult, op1=ALU.max))
                w3 = w32[:].rearrange("p (b k) -> p b k", k=32)
                T("vector", lambda: nc.vector.reciprocal(out=rbm[:, 0:nblk], in_=bm[:, 0:nblk]))
                bmap = bm[:, 0:nblk]
                b3 = bass.AP(bmap.tensor, bmap.offset, [list(bmap.ap[0]), [bmap.ap[1][0], nblk], [0, 32]])
                rmap = rbm[:, 0:nblk]
                r3 = bass.AP(rmap.tensor, rmap.offset, [list(rmap.ap[0]), [rmap.ap[1][0], nblk], [0, 32]])
                T("vector", lambda w3=w3, r3=r3: nc.vector.tensor_tensor(out=w3, in0=w3, in1=r3, op=ALU.mult))
                T("vector", lambda: nc.vector.tensor_scalar(out=w32[:], in0=w32[:], scalar1=MAGIC,
                                                            scalar2=-MAGIC, op0=ALU.add, op1=ALU.add))
                wv = T("vector", lambda w3=w3, b3=b3: nc.vector.tensor_tensor(out=w3, in0=w3, in1=b3, op=ALU.mult))
                cb = T("scalar", lambda: nc.scalar.copy(out=wbf[:], in_=w32[:]), after=[("vector", wv)])
                for ic in range(8):
                    T("tensor", lambda ic=ic: nc.tensor.transpose(
                        out=ps_t[:, ic * 128:(ic + 1) * 128], in_=wbf[:, ic * 128:(ic + 1) * 128], identity=identb[:]),
                      after=[("scalar", cb), ("scalar", e.now("scalar"))] if ic == 0 else [])
                tc = e.now("tensor")
                for ic in range(8):
                    icg = g3 * 8 + ic
                    T("scalar", lambda icg=icg, ic=ic, c=c: nc.scalar.copy(
                        out=wdT[:, icg * 1024 + c * 128: icg * 1024 + (c + 1) * 128],
                        in_=ps_t[:, ic * 128:(ic + 1) * 128]),
                      after=[("tensor", tc)] if ic == 0 else [])
        wd_done = e.now("scalar")
        oT_t = e.sb("oT_t", [128, 24 * 128], BF16)
        res32 = e.sb("res32", [128, D])
        for t in range(NT):
            dl = []
            for icg in range(24):
                dl.append(e.dma(oT_t[:, icg * 128:(icg + 1) * 128],
                                ODRAM[icg * 128:(icg + 1) * 128, t * 128:(t + 1) * 128],
                                after=[("tensor", e.now("tensor"))] if icg == 0 else []))
            dx = e.dma(res32[:], x[t * 128:(t + 1) * 128, :], after=[("vector", e.now("vector"))])
            dat = e.dma(xa[:], ATTN0[t * 128:(t + 1) * 128, :])
            av = T("vector", lambda: nc.vector.tensor_add(out=res32[:], in0=res32[:], in1=xa[:]),
                   after=[("dma", e.now("dma"))])
            for nb in range(2):
                ps = ps_k[nb]
                deps = [("scalar", wd_done), ("vector", av), ("dma", dl[-1])]
                for icg in range(24):
                    T("tensor", lambda icg=icg, nb=nb, ps=ps: nc.tensor.matmul(
                        out=ps[:], lhsT=oT_t[:, icg * 128:(icg + 1) * 128],
                        rhs=wdT[:, icg * 1024 + nb * 512: icg * 1024 + (nb + 1) * 512],
                        start=(icg == 0), stop=(icg == 23)), after=deps if icg == 0 else [])
                mmc = e.now("tensor")
                fv = T("vector", lambda t=t, nb=nb, ps=ps: nc.vector.tensor_add(
                    out=res32[:, nb * 512:(nb + 1) * 512], in0=res32[:, nb * 512:(nb + 1) * 512], in1=ps[:]),
                    after=[("tensor", mmc)])
                e.dma(X1[t * 128:(t + 1) * 128, nb * 512:(nb + 1) * 512], res32[:, nb * 512:(nb + 1) * 512],
                      after=[("vector", fv)])

    T("gpsimd", lambda: None, after=e.all_now())
    return e.emit_and_close()


def emit_l1(e, fused=False, debug=False):
    nc = e.nc
    e.phase("C")
    X1I = e.dr("x1", [S, D], F32, kind="Internal" if fused else "ExternalInput")
    AWQ = e.dr("awq", [D, D], F32)
    AWK = e.dr("awk", [D, D], F32)
    AWV = e.dr("awv", [D, D], F32)
    AWO = e.dr("awo", [D, D], F32)
    ABQ = e.dr("abq", [1, D], F32)
    ABK = e.dr("abk", [1, D], F32)
    ABV = e.dr("abv", [1, D], F32)
    ABO = e.dr("abo", [1, D], F32)
    SC1 = e.dr("scale1", [1, D], F32)
    SC2 = e.dr("scale2", [1, D], F32)
    FWQ = e.dr("fwq", [D, D], F32)
    FWK = e.dr("fwk", [D, D], F32)
    FWV = e.dr("fwv", [D, D], F32)
    FWO = e.dr("fwo", [D, D], F32)
    ROUT = e.dr("router", [D, 8], F32)
    W1 = e.dr("w1", [8 * D, D], F32)
    W2 = e.dr("w2", [8 * D, D], F32)
    WOUT = e.dr("wout", [8 * D, D], F32)
    B1 = e.dr("b1", [8, D], F32)
    B2 = e.dr("b2", [8, D], F32)
    BOUT = e.dr("bout", [8, D], F32)
    GN = e.dr("gnoise", [S, 8], F32)
    COS32 = e.dr("cos32", [128, S], BF16)
    SIN32 = e.dr("sin32", [128, S], BF16)
    IDENT = e.dr("identf", [128, 128], BF16)
    IDF32 = e.dr("identf32", [128, 128], F32)
    CAUS = e.dr("caus", [128, 4 * 512], BF16)

    X2O = e.dr("x2", [S, D], F32, kind="ExternalOutput")
    XN1 = e.dr("xn1_d", [S, D], F32, kind="Internal")
    A1 = e.dr("a1_d", [S, D], F32, kind="Internal")
    H1 = e.dr("h1_d", [S, D], F32, kind="Internal")
    MOE = e.dr("moe_d", [S, D], F32, kind="Internal")
    if debug:
        A1_D = e.dr("a1_dbg", [S, D], F32, kind="ExternalOutput")
        H1_D = e.dr("h1_dbg", [S, D], F32, kind="ExternalOutput")
        MOE_D = e.dr("moe_dbg", [S, D], F32, kind="ExternalOutput")
        V38_D = e.dr("v38_dbg", [2, 128], F32, kind="ExternalOutput")
        WN_D = e.dr("wn_dbg", [128, S], BF16, kind="ExternalOutput")
        CTX_D = e.dr("ctx_dbg", [D, S], BF16, kind="ExternalOutput")
        PRB_D = e.dr("prb_dbg", [128, 8], F32, kind="ExternalOutput")
        WTS_D = e.dr("wts_dbg", [128, 128], F32, kind="ExternalOutput")

    xa = e.sb("xa", [128, D])
    xb2 = e.sb("xb2", [128, D])
    scr32 = e.sb("scr32", [128, D])
    scr_bf = e.sb("scr_bf", [128, D], BF16)
    xnT = e.sb("xnT", [128, NC * S], BF16)
    qT = e.sb("qT", [128, NC * S], BF16)
    kT = e.sb("kT", [128, NC * S], BF16)
    vtok = e.sb("vtok", [128, NT * D], BF16)
    ctxT = e.sb("ctxT", [128, NC * S], BF16)
    wfull = e.sb("wfull", [128, NC * D], BF16)
    ssb = e.sb("ssb", [128, S])
    sel = e.sb_at("sel", [128, S], F32, xnT)
    wn = e.sb_at("wn", [128, S], BF16, wfull, 0)
    shared = e.sb_at("shared", [128, S], BF16, wfull, 8192)
    causb = e.sb("causb", [128, 4 * 512], BF16)
    ropeA = e.sb_at("ropeA", [128, S], BF16, wfull, 0)
    ropeB = e.sb_at("ropeB", [128, S], BF16, wfull, 4096)
    ropeC = e.sb_at("ropeC", [128, S], BF16, wfull, 8192)
    ropeD = e.sb_at("ropeD", [128, S], BF16, wfull, 12288)
    cosb = e.sb_at("cosb", [128, S], BF16, ctxT, 0)
    sinb = e.sb_at("sinb", [128, S], BF16, ctxT, 4096)
    m8 = e.sb("m8", [128, 8])
    mcol = e.sb("mcol", [128, 1])
    vcol = e.sb("vcol", [128, 1])
    tcol = e.sb("tcol", [128, 1])
    zcol = e.sb("zcol", [128, 1])
    rz = e.sb("rz", [128, 1])
    negm = e.sb("negm", [128, 1])
    bqc = e.sb("bqc", [128, NC])
    bkc = e.sb("bkc", [128, NC])
    s1b = e.sb_at("s1b", [128, D], F32, qT, 0)
    wts_all = e.sb("wts_all", [128, NT * 8])
    prb = e.sb("prb", [128, 8])
    gn_t = e.sb("gn_t", [128, 8])
    rowbuf_bf = e.sb_at("rowbuf_bf", [1, D], BF16, scr_bf, 0)
    rowb2 = e.sb_at("rowb2", [1, D], BF16, xb2, 0)
    boutall = e.sb_at("boutall", [8, D], BF16, xb2, 2048)
    wtsT = e.sb("wtsT", [8, 128], BF16)
    a1s = e.sb_at("a1s", [128, D], BF16, ssb, 0)
    a2s = e.sb_at("a2s", [128, D], BF16, ssb, 2048)
    heT = e.sb_at("heT", [128, D], BF16, ssb, 4096)
    ident = e.sb("ident", [128, 128], BF16)
    idf32 = e.sb("idf32", [128, 128])
    ones1 = e.sb("ones1", [1, 128], BF16)
    onef32 = e.sb("onef32", [1, 128])

    ps_k = [e.psum(f"ps_{i}", [128, 512], F32) for i in range(4)]
    ps_t = e.psum("ps_t", [128, 1024], BF16)
    ps_t2 = e.psum("ps_t2", [128, 1024], BF16)
    ps_g2 = [e.psum(f"psg_{i}", [128, 512], F32) for i in range(2)]

    T = e.op
    e.dma(ident[:], IDENT[:])
    e.dma(idf32[:], IDF32[:])
    ld0 = e.now("dma")
    T("vector", lambda: nc.vector.memset(ones1[:], 1.0))
    T("vector", lambda: nc.vector.memset(onef32[:], 1.0))
    for c in range(NC):
        e.dma(bqc[:, c:c + 1], ABQ[0:1, c * 128:(c + 1) * 128])
        e.dma(bkc[:, c:c + 1], ABK[0:1, c * 128:(c + 1) * 128])
    ld_b = e.now("dma")

    def ln_stream(SRC, DSTDRAM, eps, rms=False, scale_b=None):
        for t in range(NT):
            dx = e.dma(xa[:], SRC[t * 128:(t + 1) * 128, :],
                       after=[("scalar", e.now("scalar")), ("vector", e.now("vector")), ("dma", e.now("dma"))])
            if not rms:
                T("vector", lambda: nc.vector.reduce_sum(out=mcol[:], in_=xa[:], axis=AX.X), after=[("dma", dx)])
            sq = T("scalar", lambda: nc.scalar.activation(out=scr32[:], in_=xa[:], func=AF.Square, accum_out=vcol[:]),
                   after=[("dma", dx)])
            if not rms:
                T("vector", lambda: nc.vector.tensor_scalar(out=mcol[:], in0=mcol[:], scalar1=1.0 / D, scalar2=None, op0=ALU.mult))
            T("vector", lambda: nc.vector.tensor_scalar(out=vcol[:], in0=vcol[:], scalar1=1.0 / D, scalar2=None, op0=ALU.mult),
              after=[("scalar", sq)])
            if not rms:
                T("vector", lambda: nc.vector.tensor_scalar(out=tcol[:], in0=mcol[:], scalar1=mcol[:], scalar2=None, op0=ALU.mult))
                T("vector", lambda: nc.vector.tensor_sub(out=vcol[:], in0=vcol[:], in1=tcol[:]))
            vv = T("vector", lambda: nc.vector.tensor_scalar(out=vcol[:], in0=vcol[:], scalar1=eps, scalar2=None, op0=ALU.add))
            sv = T("scalar", lambda: nc.scalar.activation(out=vcol[:], in_=vcol[:], func=AF.Sqrt), after=[("vector", vv)])
            T("vector", lambda: nc.vector.reciprocal(out=vcol[:], in_=vcol[:]), after=[("scalar", sv)])
            if rms:
                T("vector", lambda: nc.vector.tensor_scalar(out=xa[:], in0=xa[:], scalar1=vcol[:], scalar2=None, op0=ALU.mult))
                T("vector", lambda: nc.vector.tensor_mul(out=xa[:], in0=xa[:], in1=scale_b[:]))
            else:
                T("vector", lambda: nc.vector.tensor_scalar(out=xa[:], in0=xa[:], scalar1=mcol[:], scalar2=vcol[:],
                                                            op0=ALU.subtract, op1=ALU.mult))
            lv = e.now("vector")
            if DSTDRAM is not None:
                e.dma(DSTDRAM[t * 128:(t + 1) * 128, :], xa[:], after=[("vector", lv)])
            cs = T("scalar", lambda: nc.scalar.copy(out=scr_bf[:], in_=xa[:]),
                   after=[("vector", lv), ("tensor", e.now("tensor"))])
            for ic in range(NC):
                T("tensor", lambda ic=ic: nc.tensor.transpose(
                    out=ps_t[:, ic * 128:(ic + 1) * 128], in_=scr_bf[:, ic * 128:(ic + 1) * 128], identity=ident[:]),
                  after=[("scalar", cs)] if ic == 0 else [])
            tc = e.now("tensor")
            for ic in range(NC):
                T("scalar", lambda t=t, ic=ic: nc.scalar.copy(
                    out=xnT[:, ic * S + t * 128: ic * S + (t + 1) * 128], in_=ps_t[:, ic * 128:(ic + 1) * 128]),
                  after=[("tensor", tc)] if ic == 0 else [])

    ln_stream(X1I, XN1, 1e-5, rms=False)

    def load_w_full(W):
        for r in range(NC):
            e.dma(wfull[:, r * D:(r + 1) * D], W[r * 128:(r + 1) * 128, :],
                  after=[("tensor", e.now("tensor"))] if r == 0 else [], queue="gpsimd")
        return e.now("dmag")

    def transpose_w_to(WDST):
        dep = [("dmag", e.now("dmag"))]
        for oc in range(NC):
            for ic in range(NC):
                T("tensor", lambda oc=oc, ic=ic: nc.tensor.transpose(
                    out=ps_t[:, (ic % 8) * 128:(ic % 8 + 1) * 128],
                    in_=wfull[:, oc * D + ic * 128: oc * D + (ic + 1) * 128], identity=ident[:]),
                  after=dep + [("scalar", e.now("scalar"))] if ic == 0 else [])
            tc = e.now("tensor")
            for ic in range(NC):
                T("scalar", lambda oc=oc, ic=ic: nc.scalar.copy(
                    out=WDST[:, ic * D + oc * 128: ic * D + (oc + 1) * 128],
                    in_=ps_t[:, (ic % 8) * 128:(ic % 8 + 1) * 128]),
                  after=[("tensor", tc)] if ic == 0 else [])
        return e.now("scalar")

    def proj_from_pieces(PIECES, outT, bias_cols, ldep):
        for oc in range(NC):
            for nb in range(S // 512):
                ps = ps_k[nb % 4]
                deps = [("scalar", e.now("scalar")), ("vector", e.now("vector"))] + ldep
                for ic in range(NC):
                    T("tensor", lambda ic=ic, oc=oc, nb=nb, ps=ps: nc.tensor.matmul(
                        out=ps[:], lhsT=PIECES[:, ic * D + oc * 128: ic * D + (oc + 1) * 128],
                        rhs=xnT[:, ic * S + nb * 512: ic * S + (nb + 1) * 512],
                        start=(ic == 0), stop=(ic == NC - 1)), after=deps if ic == 0 else [])
                mmc = e.now("tensor")
                if bias_cols is not None:
                    T("scalar", lambda oc=oc, nb=nb, ps=ps, outT=outT: nc.scalar.activation(
                        out=outT[:, oc * S + nb * 512: oc * S + (nb + 1) * 512], in_=ps[:],
                        func=AF.Identity, bias=bias_cols[:, oc:oc + 1], scale=1.0), after=[("tensor", mmc)])
                else:
                    T("scalar", lambda oc=oc, nb=nb, ps=ps, outT=outT: nc.scalar.copy(
                        out=outT[:, oc * S + nb * 512: oc * S + (nb + 1) * 512], in_=ps[:]), after=[("tensor", mmc)])

    def proj_tokmajor(RHS_PIECES, dst_tok, brow_bf, ldep):
        for t in range(NT):
            for nb in range(2):
                ps = ps_k[(2 * t + nb) % 4]
                deps = [("scalar", e.now("scalar")), ("vector", e.now("vector"))] + ldep
                last = NC - 1 if brow_bf is None else -1
                for ic in range(NC):
                    T("tensor", lambda t=t, ic=ic, nb=nb, ps=ps, last=last: nc.tensor.matmul(
                        out=ps[:], lhsT=xnT[:, ic * S + t * 128: ic * S + (t + 1) * 128],
                        rhs=RHS_PIECES[:, ic * D + nb * 512: ic * D + (nb + 1) * 512],
                        start=(ic == 0), stop=(ic == last)), after=deps if ic == 0 else [])
                if brow_bf is not None:
                    T("tensor", lambda nb=nb, ps=ps, brow_bf=brow_bf: nc.tensor.matmul(
                        out=ps[:], lhsT=ones1[:], rhs=brow_bf[0:1, nb * 512:(nb + 1) * 512], start=False, stop=True))
                mmc = e.now("tensor")
                T("scalar", lambda t=t, nb=nb, ps=ps, dst_tok=dst_tok: nc.scalar.copy(
                    out=dst_tok[:, t * D + nb * 512: t * D + (nb + 1) * 512], in_=ps[:]), after=[("tensor", mmc)])

    bqs = e.sb("bqs", [128, NC])
    T("vector", lambda: nc.vector.tensor_scalar(out=bqs[:], in0=bqc[:], scalar1=0.125, scalar2=None, op0=ALU.mult),
      after=[("dma", ld_b)])
    load_w_full(AWQ)
    wdep = transpose_w_to(ctxT)
    for oc in range(NC):
        pass
    for oc in range(NC):
        for nb in range(S // 512):
            ps = ps_k[nb % 4]
            deps = [("scalar", e.now("scalar")), ("vector", e.now("vector"))]
            for ic in range(NC):
                T("tensor", lambda ic=ic, oc=oc, nb=nb, ps=ps: nc.tensor.matmul(
                    out=ps[:], lhsT=ctxT[:, ic * D + oc * 128: ic * D + (oc + 1) * 128],
                    rhs=xnT[:, ic * S + nb * 512: ic * S + (nb + 1) * 512],
                    start=(ic == 0), stop=(ic == NC - 1)), after=deps if ic == 0 else [])
            mmc = e.now("tensor")
            T("scalar", lambda oc=oc, nb=nb, ps=ps: nc.scalar.activation(
                out=qT[:, oc * S + nb * 512: oc * S + (nb + 1) * 512], in_=ps[:],
                func=AF.Identity, bias=bqs[:, oc:oc + 1], scale=0.125), after=[("tensor", mmc)])
    load_w_full(AWK)
    transpose_w_to(ctxT)
    proj_from_pieces(ctxT, kT, bkc, [])
    dbv = e.dma(rowbuf_bf[:], ABV[:], queue="gpsimd")
    load_w_full(AWV)
    transpose_w_to(ctxT)
    proj_tokmajor(ctxT, vtok, rowbuf_bf, [])

    U = 38
    for h in range(H):
        c2 = h // 2
        pb = 64 * (h % 2)
        for qt in range(NT):
            for kb in range(4):
                T("tensor", lambda c2=c2, pb=pb, qt=qt, kb=kb: nc.tensor.matmul(
                    out=ps_k[kb][:], lhsT=qT[pb:pb + 64, c2 * S + qt * 128: c2 * S + (qt + 1) * 128],
                    rhs=kT[pb:pb + 64, c2 * S + kb * 512: c2 * S + (kb + 1) * 512], start=True, stop=True),
                  after=[("scalar", e.now("scalar")), ("vector", e.now("vector"))] if kb == 0 else [])
            mmc = e.now("tensor")
            for kb in range(4):
                T("scalar", lambda kb=kb: nc.scalar.copy(out=ssb[:, kb * 512:(kb + 1) * 512], in_=ps_k[kb][:]),
                  after=[("tensor", mmc)] if kb == 0 else [])
            sevac = e.now("scalar")
            T("vector", lambda: nc.vector.tensor_copy(out=sel[:], in_=ssb[:]), after=[("scalar", sevac)])
            for rnd in range(5):
                T("vector", lambda: nc.vector.max(out=m8[:], in_=sel[:]))
                if rnd == 0:
                    T("vector", lambda: nc.vector.tensor_copy(out=mcol[:], in_=m8[:, 0:1]))
                if rnd < 4:
                    T("vector", lambda: nc.vector.match_replace(out=sel[:], in_to_replace=m8[:], in_values=sel[:],
                                                                imm_value=-1e30))
            T("vector", lambda: nc.vector.tensor_copy(out=vcol[:], in_=m8[:, 5:6]))
            if debug and h == 0 and qt == 0:
                e.dma(bass.AP(V38_D.tensor, 0, [[1, 128], [1, 1]]), vcol[:], after=[("vector", e.now("vector"))])
                e.dma(bass.AP(V38_D.tensor, 128, [[1, 128], [1, 1]]), mcol[:], after=[("vector", e.now("vector"))])
            T("vector", lambda: nc.vector.tensor_single_scalar(out=wn[:], in_=ssb[:], scalar=vcol[:], op=ALU.is_ge))
            T("vector", lambda: nc.vector.tensor_scalar(out=negm[:], in0=mcol[:], scalar1=-1.0, scalar2=None, op0=ALU.mult))
            ex = T("scalar", lambda: nc.scalar.activation(out=shared[:], in_=ssb[:], func=AF.Exp, bias=negm[:]),
                   after=[("vector", e.now("vector"))])
            T("vector", lambda: nc.vector.tensor_mul(out=wn[:], in0=wn[:], in1=shared[:]), after=[("scalar", ex)])
            T("vector", lambda: nc.vector.reduce_sum(out=zcol[:], in_=wn[:], axis=AX.X))
            T("vector", lambda: nc.vector.reciprocal(out=rz[:], in_=zcol[:]))
            nw = T("vector", lambda: nc.vector.tensor_scalar(out=wn[:], in0=wn[:], scalar1=rz[:], scalar2=None, op0=ALU.mult))
            if debug and h == 0 and qt == 0:
                dwn = e.dma(WN_D[:], wn[:], after=[("vector", nw)])
                T("vector", lambda: nc.vector.tensor_copy(out=tcol[:], in_=tcol[:]), after=[("dma", dwn)])
            for c in range(16):
                pst = ps_t if c < 8 else ps_t2
                T("tensor", lambda c=c, pst=pst: nc.tensor.transpose(
                    out=pst[:, (c % 8) * 128:(c % 8 + 1) * 128], in_=wn[:, c * 128:(c + 1) * 128], identity=ident[:]),
                  after=[("vector", nw), ("scalar", e.now("scalar"))] if c == 0 else [])
            tcn = e.now("tensor")
            T("scalar", lambda: nc.scalar.copy(out=shared[:, 0:1024], in_=ps_t[:]), after=[("tensor", tcn)])
            ev2 = T("scalar", lambda: nc.scalar.copy(out=shared[:, 1024:2048], in_=ps_t2[:]))
            for c in range(16):
                T("tensor", lambda c=c, h=h: nc.tensor.matmul(
                    out=ps_g2[0][0:64, 0:128], lhsT=vtok[:, c * D + h * 64: c * D + h * 64 + 64],
                    rhs=shared[:, c * 128:(c + 1) * 128], start=(c == 0), stop=(c == 15)),
                  after=[("scalar", ev2)] if c == 0 else [])
            pvc = e.now("tensor")
            T("scalar", lambda c2=c2, pb=pb, qt=qt: nc.scalar.copy(
                out=ctxT[pb:pb + 64, c2 * S + qt * 128: c2 * S + (qt + 1) * 128], in_=ps_g2[0][0:64, 0:128]),
              after=[("tensor", pvc)])

    if debug:
        for cdf in range(NC):
            e.dma(CTX_D[cdf * 128:(cdf + 1) * 128, :], ctxT[:, cdf * S:(cdf + 1) * S],
                  after=[("scalar", e.now("scalar"))])
    dbo = e.dma(rowbuf_bf[:], ABO[:], after=[("tensor", e.now("tensor"))], queue="gpsimd")
    load_w_full(AWO)
    wot = transpose_w_to(qT)
    for t in range(NT):
        for nb in range(2):
            ps = ps_k[(2 * t + nb) % 4]
            deps = [("scalar", e.now("scalar")), ("vector", e.now("vector")), ("dmag", e.now("dmag"))]
            for ic in range(NC):
                T("tensor", lambda t=t, ic=ic, nb=nb, ps=ps: nc.tensor.matmul(
                    out=ps[:], lhsT=ctxT[:, ic * S + t * 128: ic * S + (t + 1) * 128],
                    rhs=qT[:, ic * D + nb * 512: ic * D + (nb + 1) * 512],
                    start=(ic == 0), stop=False), after=deps if ic == 0 else [])
            T("tensor", lambda nb=nb, ps=ps: nc.tensor.matmul(
                out=ps[:], lhsT=ones1[:], rhs=rowbuf_bf[0:1, nb * 512:(nb + 1) * 512], start=False, stop=True))
            mmc = e.now("tensor")
            sv = T("scalar", lambda nb=nb, ps=ps: nc.scalar.copy(out=scr32[:, nb * 512:(nb + 1) * 512], in_=ps[:]),
                   after=[("tensor", mmc), ("dma", e.now("dma"))])
        e.dma(A1[t * 128:(t + 1) * 128, :], scr32[:], after=[("scalar", e.now("scalar"))])
        if debug:
            e.dma(A1_D[t * 128:(t + 1) * 128, :], scr32[:], after=[("scalar", e.now("scalar"))])

    ds1 = e.dma(scr32[0:1, :], SC1[:], after=[("scalar", e.now("scalar")), ("vector", e.now("vector")),
                                              ("dma", e.now("dma"))])
    cs1 = T("vector", lambda: nc.vector.tensor_copy(out=rowb2[:], in_=scr32[0:1, :]), after=[("dma", ds1)])
    for nb in range(2):
        mm = T("tensor", lambda nb=nb: nc.tensor.matmul(
            out=ps_g2[0][:, 0:512], lhsT=ones1[:], rhs=rowb2[0:1, nb * 512:(nb + 1) * 512], start=True, stop=True),
            after=[("vector", cs1), ("scalar", e.now("scalar"))])
        T("scalar", lambda nb=nb: nc.scalar.copy(out=s1b[:, nb * 512:(nb + 1) * 512], in_=ps_g2[0][:, 0:512]),
          after=[("tensor", mm)])
    sb_done = e.now("scalar")
    T("vector", lambda: nc.vector.tensor_copy(out=tcol[:], in_=tcol[:]), after=[("scalar", sb_done)])
    ln_stream(XN1, None, 1e-6, rms=True, scale_b=s1b)

    def rope_chunks(tT):
        for c in range(4):
            x1 = tT[:, (2 * c) * S:(2 * c + 1) * S]
            x2 = tT[:, (2 * c + 1) * S:(2 * c + 2) * S]
            T("vector", lambda x1=x1: nc.vector.tensor_mul(out=ropeA[:], in0=x1, in1=cosb[:]),
              after=[("scalar", e.now("scalar"))])
            T("vector", lambda x1=x1: nc.vector.tensor_mul(out=ropeB[:], in0=x1, in1=sinb[:]))
            T("vector", lambda x2=x2: nc.vector.tensor_mul(out=ropeC[:], in0=x2, in1=sinb[:]))
            T("vector", lambda x2=x2: nc.vector.tensor_mul(out=ropeD[:], in0=x2, in1=cosb[:]))
            T("vector", lambda x1=x1: nc.vector.tensor_sub(out=x1, in0=ropeA[:], in1=ropeC[:]))
            T("vector", lambda x2=x2: nc.vector.tensor_add(out=x2, in0=ropeB[:], in1=ropeD[:]))

    ldq = load_w_full(FWQ)
    for oc in range(NC):
        for nb in range(S // 512):
            ps = ps_k[nb % 4]
            deps = [("scalar", e.now("scalar")), ("vector", e.now("vector")), ("dmag", ldq)]
            for ic in range(NC):
                T("tensor", lambda ic=ic, oc=oc, nb=nb, ps=ps: nc.tensor.matmul(
                    out=ps[:], lhsT=wfull[:, ic * D + oc * 128: ic * D + (oc + 1) * 128],
                    rhs=xnT[:, ic * S + nb * 512: ic * S + (nb + 1) * 512],
                    start=(ic == 0), stop=(ic == NC - 1)), after=deps if ic == 0 else [])
            mmc = e.now("tensor")
            T("scalar", lambda oc=oc, nb=nb, ps=ps: nc.scalar.activation(
                out=qT[:, oc * S + nb * 512: oc * S + (nb + 1) * 512], in_=ps[:],
                func=AF.Identity, scale=0.125), after=[("tensor", mmc)])
    ldk = load_w_full(FWK)
    for oc in range(NC):
        for nb in range(S // 512):
            ps = ps_k[nb % 4]
            deps = [("scalar", e.now("scalar")), ("vector", e.now("vector")), ("dmag", ldk)]
            for ic in range(NC):
                T("tensor", lambda ic=ic, oc=oc, nb=nb, ps=ps: nc.tensor.matmul(
                    out=ps[:], lhsT=wfull[:, ic * D + oc * 128: ic * D + (oc + 1) * 128],
                    rhs=xnT[:, ic * S + nb * 512: ic * S + (nb + 1) * 512],
                    start=(ic == 0), stop=(ic == NC - 1)), after=deps if ic == 0 else [])
            mmc = e.now("tensor")
            T("scalar", lambda oc=oc, nb=nb, ps=ps: nc.scalar.copy(
                out=kT[:, oc * S + nb * 512: oc * S + (nb + 1) * 512], in_=ps[:]), after=[("tensor", mmc)])
    ldv = load_w_full(FWV)
    proj_tokmajor(wfull, vtok, None, [("dmag", ldv)])
    dcs = e.dma(cosb[:], COS32[:], after=[("tensor", e.now("tensor")), ("scalar", e.now("scalar"))])
    dsn = e.dma(sinb[:], SIN32[:])
    T("vector", lambda: nc.vector.tensor_copy(out=tcol[:], in_=tcol[:]), after=[("dma", dsn)])
    rope_chunks(qT)
    rope_chunks(kT)
    dca = e.dma(causb[:], CAUS[:], after=[("tensor", e.now("tensor")), ("scalar", e.now("scalar")),
                                          ("vector", e.now("vector"))])
    T("vector", lambda: nc.vector.tensor_copy(out=tcol[:], in_=tcol[:]), after=[("dma", dca)])

    for h in range(H):
        c2 = h // 2
        pb = 64 * (h % 2)
        for qt in range(NT):
            nkc = qt + 1
            nkb = (nkc * 128 + 511) // 512
            cc = h // 4
            aa = h % 4
            rb = 32 * aa
            for kb in range(nkb):
                for xh in range(2):
                    cblk = (2 * cc + xh) * S
                    T("tensor", lambda qt=qt, kb=kb, rb=rb, cblk=cblk, xh=xh: nc.tensor.matmul(
                        out=ps_k[kb][:], lhsT=qT[rb:rb + 32, cblk + qt * 128: cblk + (qt + 1) * 128],
                        rhs=kT[rb:rb + 32, cblk + kb * 512: cblk + (kb + 1) * 512],
                        start=(xh == 0), stop=(xh == 1), tile_position=(rb, 0)),
                      after=[("scalar", e.now("scalar")), ("vector", e.now("vector"))] if (kb == 0 and xh == 0) else [])
            mmc = e.now("tensor")
            for kb in range(nkb):
                T("scalar", lambda kb=kb: nc.scalar.copy(out=ssb[:, kb * 512:(kb + 1) * 512], in_=ps_k[kb][:]),
                  after=[("tensor", mmc)] if kb == 0 else [])
            sevac = e.now("scalar")
            dkb = (qt * 128) // 512
            dofs = qt * 128 - dkb * 512
            T("vector", lambda dkb=dkb, dofs=dofs: nc.vector.tensor_add(
                out=ssb[:, dkb * 512:(dkb + 1) * 512], in0=ssb[:, dkb * 512:(dkb + 1) * 512],
                in1=causb[:, (dofs // 128) * 512:(dofs // 128 + 1) * 512]),
              after=[("scalar", sevac)])
            nv = nkc * 128
            T("vector", lambda nv=nv: nc.vector.tensor_reduce(out=mcol[:], in_=ssb[:, 0:nv], axis=AX.X, op=ALU.max))
            T("vector", lambda: nc.vector.tensor_scalar(out=negm[:], in0=mcol[:], scalar1=-1.0, scalar2=None, op0=ALU.mult))
            ex = T("scalar", lambda nv=nv: nc.scalar.activation(out=wn[:, 0:nv], in_=ssb[:, 0:nv], func=AF.Exp,
                                                                bias=negm[:], accum_out=zcol[:]),
                   after=[("vector", e.now("vector"))])
            T("vector", lambda: nc.vector.reciprocal(out=rz[:], in_=zcol[:]), after=[("scalar", ex)])
            nw = T("vector", lambda nv=nv: nc.vector.tensor_scalar(out=wn[:, 0:nv], in0=wn[:, 0:nv], scalar1=rz[:],
                                                                   scalar2=None, op0=ALU.mult))
            for c in range(nkc):
                pst = ps_t if (c % 16) < 8 else ps_t2
                T("tensor", lambda c=c, pst=pst: nc.tensor.transpose(
                    out=pst[:, (c % 8) * 128:(c % 8 + 1) * 128], in_=wn[:, c * 128:(c + 1) * 128], identity=ident[:]),
                  after=[("vector", nw), ("scalar", e.now("scalar"))] if c == 0 else [])
            tcn = e.now("tensor")
            T("scalar", lambda w=min(nkc, 8): nc.scalar.copy(out=shared[:, 0:w * 128], in_=ps_t[:, 0:w * 128]),
              after=[("tensor", tcn)])
            if nkc > 8:
                T("scalar", lambda w=nkc - 8: nc.scalar.copy(out=shared[:, 1024:1024 + w * 128], in_=ps_t2[:, 0:w * 128]))
            ev2 = e.now("scalar")
            for c in range(nkc):
                T("tensor", lambda c=c, h=h: nc.tensor.matmul(
                    out=ps_g2[0][0:64, 0:128], lhsT=vtok[:, c * D + h * 64: c * D + h * 64 + 64],
                    rhs=shared[:, c * 128:(c + 1) * 128], start=(c == 0), stop=(c == nkc - 1)),
                  after=[("scalar", ev2)] if c == 0 else [])
            pvc = e.now("tensor")
            T("scalar", lambda c2=c2, pb=pb, qt=qt: nc.scalar.copy(
                out=ctxT[pb:pb + 64, c2 * S + qt * 128: c2 * S + (qt + 1) * 128], in_=ps_g2[0][0:64, 0:128]),
              after=[("tensor", pvc)])

    ldo = load_w_full(FWO)
    for t in range(NT):
        dxn = e.dma(xb2[:], XN1[t * 128:(t + 1) * 128, :], after=[("vector", e.now("vector"))])
        for nb in range(2):
            ps = ps_k[(2 * t + nb) % 4]
            deps = [("scalar", e.now("scalar")), ("vector", e.now("vector")), ("dmag", ldo), ("dma", dxn)]
            for ic in range(NC):
                T("tensor", lambda t=t, ic=ic, nb=nb, ps=ps: nc.tensor.matmul(
                    out=ps[:], lhsT=ctxT[:, ic * S + t * 128: ic * S + (t + 1) * 128],
                    rhs=wfull[:, ic * D + nb * 512: ic * D + (nb + 1) * 512],
                    start=(ic == 0), stop=(ic == NC - 1)), after=deps if ic == 0 else [])
            mmc = e.now("tensor")
            T("vector", lambda t=t, nb=nb, ps=ps: nc.vector.tensor_add(
                out=scr32[:, nb * 512:(nb + 1) * 512], in0=xb2[:, nb * 512:(nb + 1) * 512], in1=ps[:]),
              after=[("tensor", mmc), ("dma", e.now("dma"))])
        hv = e.now("vector")
        e.dma(H1[t * 128:(t + 1) * 128, :], scr32[:], after=[("vector", hv)])
        if debug:
            e.dma(H1_D[t * 128:(t + 1) * 128, :], scr32[:], after=[("vector", hv)])

    ds2 = e.dma(scr32[0:1, :], SC2[:], after=[("scalar", e.now("scalar")), ("vector", e.now("vector")),
                                              ("dma", e.now("dma"))])
    cs2 = T("vector", lambda: nc.vector.tensor_copy(out=rowb2[:], in_=scr32[0:1, :]), after=[("dma", ds2)])
    for nb in range(2):
        mm = T("tensor", lambda nb=nb: nc.tensor.matmul(
            out=ps_g2[0][:, 0:512], lhsT=ones1[:], rhs=rowb2[0:1, nb * 512:(nb + 1) * 512], start=True, stop=True),
            after=[("vector", cs2), ("scalar", e.now("scalar"))])
        T("scalar", lambda nb=nb: nc.scalar.copy(out=s1b[:, nb * 512:(nb + 1) * 512], in_=ps_g2[0][:, 0:512]),
          after=[("tensor", mm)])
    T("vector", lambda: nc.vector.tensor_copy(out=tcol[:], in_=tcol[:]), after=[("scalar", e.now("scalar"))])
    ln_stream(H1, None, 1e-6, rms=True, scale_b=s1b)

    for ic in range(NC):
        e.dma(wfull[:, ic * 8:(ic + 1) * 8], ROUT[ic * 128:(ic + 1) * 128, :],
              after=[("tensor", e.now("tensor"))] if ic == 0 else [], queue="gpsimd")
    ldr = e.now("dmag")
    for t in range(NT):
        dgn = e.dma(gn_t[:], GN[t * 128:(t + 1) * 128, :], after=[("vector", e.now("vector"))])
        deps = [("scalar", e.now("scalar")), ("vector", e.now("vector")), ("dmag", ldr)]
        for ic in range(NC):
            T("tensor", lambda t=t, ic=ic: nc.tensor.matmul(
                out=ps_g2[1][0:128, 0:8], lhsT=xnT[:, ic * S + t * 128: ic * S + (t + 1) * 128],
                rhs=wfull[:, ic * 8:(ic + 1) * 8], start=(ic == 0), stop=(ic == NC - 1)),
              after=deps if ic == 0 else [])
        mmc = e.now("tensor")
        T("vector", lambda: nc.vector.tensor_add(out=prb[:], in0=ps_g2[1][0:128, 0:8], in1=gn_t[:]),
          after=[("tensor", mmc), ("dma", dgn)])
        T("vector", lambda: nc.vector.tensor_reduce(out=mcol[:], in_=prb[:], axis=AX.X, op=ALU.max))
        T("vector", lambda: nc.vector.tensor_scalar(out=negm[:], in0=mcol[:], scalar1=-1.0, scalar2=None, op0=ALU.mult))
        exr = T("scalar", lambda: nc.scalar.activation(out=prb[:], in_=prb[:], func=AF.Exp, bias=negm[:],
                                                       accum_out=zcol[:]),
                after=[("vector", e.now("vector"))])
        T("vector", lambda: nc.vector.reciprocal(out=rz[:], in_=zcol[:]), after=[("scalar", exr)])
        T("vector", lambda: nc.vector.tensor_scalar(out=prb[:], in0=prb[:], scalar1=rz[:], scalar2=None, op0=ALU.mult))
        T("vector", lambda: nc.vector.max(out=m8[:], in_=prb[:]))
        T("vector", lambda: nc.vector.tensor_copy(out=vcol[:], in_=m8[:, 1:2]))
        T("vector", lambda: nc.vector.tensor_single_scalar(out=m8[:], in_=prb[:], scalar=vcol[:], op=ALU.is_ge))
        T("vector", lambda t=t: nc.vector.tensor_tensor(out=wts_all[:, t * 8:(t + 1) * 8], in0=prb[:],
                                                        in1=m8[:], op=ALU.mult))
        if debug and t == 0:
            e.dma(PRB_D[:], prb[:], after=[("vector", e.now("vector"))])
    if debug:
        e.dma(WTS_D[:], wts_all[:], after=[("vector", e.now("vector"))])

    e.dma(boutall[:], BOUT[:], after=[("tensor", e.now("tensor"))], queue="gpsimd")
    bout_ld = e.now("dmag")

    for ex in range(8):
        for r in range(NC):
            e.dma(qT[:, r * D:(r + 1) * D], W1[ex * D + r * 128: ex * D + (r + 1) * 128, :],
                  after=[("tensor", e.now("tensor"))] if r == 0 else [], queue="gpsimd")
            e.dma(kT[:, r * D:(r + 1) * D], W2[ex * D + r * 128: ex * D + (r + 1) * 128, :], queue="gpsimd")
            e.dma(vtok[:, r * D:(r + 1) * D], WOUT[ex * D + r * 128: ex * D + (r + 1) * 128, :], queue="gpsimd")
        e.dma(rowbuf_bf[:], B1[ex:ex + 1, :], queue="gpsimd")
        e.dma(rowb2[:], B2[ex:ex + 1, :], queue="gpsimd")
        wld = e.now("dmag")
        for t in range(NT):
            for nb in range(2):
                deps = [("scalar", e.now("scalar")), ("vector", e.now("vector")), ("dmag", wld)]
                for ic in range(NC):
                    T("tensor", lambda t=t, ic=ic, nb=nb: nc.tensor.matmul(
                        out=ps_k[nb][:], lhsT=xnT[:, ic * S + t * 128: ic * S + (t + 1) * 128],
                        rhs=qT[:, ic * D + nb * 512: ic * D + (nb + 1) * 512],
                        start=(ic == 0), stop=False), after=deps if ic == 0 else [])
                T("tensor", lambda nb=nb: nc.tensor.matmul(
                    out=ps_k[nb][:], lhsT=ones1[:], rhs=rowbuf_bf[0:1, nb * 512:(nb + 1) * 512],
                    start=False, stop=True))
                for ic in range(NC):
                    T("tensor", lambda t=t, ic=ic, nb=nb: nc.tensor.matmul(
                        out=ps_k[2 + nb][:], lhsT=xnT[:, ic * S + t * 128: ic * S + (t + 1) * 128],
                        rhs=kT[:, ic * D + nb * 512: ic * D + (nb + 1) * 512],
                        start=(ic == 0), stop=False))
                T("tensor", lambda nb=nb: nc.tensor.matmul(
                    out=ps_k[2 + nb][:], lhsT=ones1[:], rhs=rowb2[0:1, nb * 512:(nb + 1) * 512],
                    start=False, stop=True))
            mmc = e.now("tensor")
            for nb in range(2):
                T("scalar", lambda nb=nb: nc.scalar.activation(out=a1s[:, nb * 512:(nb + 1) * 512],
                                                               in_=ps_k[nb][:], func=AF.Silu),
                  after=[("tensor", mmc)] if nb == 0 else [])
                T("scalar", lambda nb=nb: nc.scalar.copy(out=a2s[:, nb * 512:(nb + 1) * 512], in_=ps_k[2 + nb][:]))
            aev = e.now("scalar")
            hv = T("vector", lambda t=t, ex=ex: nc.vector.scalar_tensor_tensor(
                out=a1s[:], in0=a1s[:], scalar=wts_all[:, t * 8 + ex: t * 8 + ex + 1], in1=a2s[:],
                op0=ALU.mult, op1=ALU.mult), after=[("scalar", aev)])
            for ic in range(NC):
                T("tensor", lambda ic=ic: nc.tensor.transpose(
                    out=ps_t[:, ic * 128:(ic + 1) * 128], in_=a1s[:, ic * 128:(ic + 1) * 128], identity=ident[:]),
                  after=[("vector", hv), ("scalar", e.now("scalar"))] if ic == 0 else [])
            tcn = e.now("tensor")
            T("scalar", lambda: nc.scalar.copy(out=heT[:], in_=ps_t[:, 0:1024]), after=[("tensor", tcn)])
            hev = e.now("scalar")
            for nb in range(2):
                deps2 = [("scalar", hev)]
                last = NC - 1 if ex != 7 else -1
                for ic in range(NC):
                    T("tensor", lambda ic=ic, nb=nb, last=last: nc.tensor.matmul(
                        out=ps_g2[nb][:], lhsT=heT[:, ic * 128:(ic + 1) * 128],
                        rhs=vtok[:, ic * D + nb * 512: ic * D + (nb + 1) * 512],
                        start=(ic == 0), stop=(ic == last)), after=deps2 if ic == 0 else [])
                if ex == 7:
                    T("tensor", lambda t=t, nb=nb: nc.tensor.matmul(
                        out=ps_g2[nb][:], lhsT=wtsT[:], rhs=boutall[:, nb * 512:(nb + 1) * 512],
                        start=False, stop=True))
            mmc = e.now("tensor")
            sv = T("scalar", lambda: nc.scalar.copy(out=scr32[:, 0:512], in_=ps_g2[0][:]),
                   after=[("tensor", mmc), ("dmag", e.now("dmag"))])
            sv2 = T("scalar", lambda: nc.scalar.copy(out=scr32[:, 512:1024], in_=ps_g2[1][:]))
            op = ALU.bypass if ex == 0 else ALU.add
            md = e.op("gpsimd", lambda t=t, op=op: nc.gpsimd.dma_start(
                out=MOE[t * 128:(t + 1) * 128, :], in_=scr32[:], accum_op=op), after=[("scalar", sv2)], inc="dmag")
            if ex == 6:
                wm = T("tensor", lambda t=t: nc.tensor.matmul(
                    out=ps_g2[0][0:8, 0:128], lhsT=wts_all[:, t * 8:(t + 1) * 8], rhs=idf32[:],
                    start=True, stop=True),
                    after=[("scalar", e.now("scalar")), ("vector", e.now("vector"))])
                T("scalar", lambda: nc.scalar.copy(out=wtsT[:], in_=ps_g2[0][0:8, 0:128]), after=[("tensor", wm)])

    for t in range(NT):
        d1 = e.dma(xa[:], X1I[t * 128:(t + 1) * 128, :], after=[("vector", e.now("vector")), ("scalar", e.now("scalar"))])
        d2 = e.dma(xb2[:], A1[t * 128:(t + 1) * 128, :], after=[("vector", e.now("vector"))])
        v1 = T("vector", lambda: nc.vector.tensor_add(out=xa[:], in0=xa[:], in1=xb2[:]), after=[("dma", d2)])
        d3 = e.dma(xb2[:], H1[t * 128:(t + 1) * 128, :], after=[("vector", v1)])
        v2 = T("vector", lambda: nc.vector.tensor_add(out=xa[:], in0=xa[:], in1=xb2[:]), after=[("dma", d3)])
        d4 = e.dma(xb2[:], MOE[t * 128:(t + 1) * 128, :], after=[("vector", v2), ("dmag", e.now("dmag"))])
        v3 = T("vector", lambda: nc.vector.tensor_add(out=xa[:], in0=xa[:], in1=xb2[:]), after=[("dma", d4)])
        e.dma(X2O[t * 128:(t + 1) * 128, :], xa[:], after=[("vector", v3)])
        if debug:
            dm = e.dma(xb2[:], MOE[t * 128:(t + 1) * 128, :], after=[("vector", e.now("vector"))])
            vv = T("vector", lambda: nc.vector.tensor_copy(out=scr32[:], in_=xb2[:]), after=[("dma", dm)])
            e.dma(MOE_D[t * 128:(t + 1) * 128, :], scr32[:], after=[("vector", vv)])

    return


def build_l1(debug=False):
    e = Emitter("l6")
    emit_l1(e, fused=False, debug=debug)
    e.op("gpsimd", lambda: None, after=e.all_now())
    return e.emit_and_close()


def emit_ffn_fused(e, debug=False):
    nc = e.nc
    e.phase("B")
    T = e.op
    x = e.dr("x", [S, D])
    WUP = e.dr("wup", [3072, D])
    WGATE = e.dr("wgate", [3072, D])
    WDOWN = e.dr("wdown", [D, 3072])
    IDF32 = e.dr("identf32", [128, 128], F32)
    IDENT = e.dr("identf", [128, 128], BF16)
    ATTN0 = e.dr("attn0", [S, D], F32, kind="Internal")
    X1 = e.dr("x1", [S, D], F32, kind="Internal")
    GD = e.dr("gd", [3072, S], F32, kind="Internal")
    UD = e.dr("ud", [3072, S], F32, kind="Internal")
    OD = e.dr("od", [3072, S], BF16, kind="Internal")

    xa = e.sb("xa", [128, D])
    xqT = e.sb("xqT", [128, NC * S], BF16)
    scr32 = e.sb("scr32", [128, D])
    scr_bf = e.sb("scr_bf", [128, D], BF16)
    w32 = e.sb("w32", [128, 3072])
    wbf = e.sb("wbf", [128, 3072], BF16)
    wT = e.sb("wT", [128, NC * 128], BF16)
    bm = e.sb("bm", [128, 96])
    rbm = e.sb("rbm", [128, 96])
    upf = e.sb("upf", [128, S])
    gf = e.sb("gf", [128, S])
    sigf = e.sb("sigf", [128, S])
    obf = e.sb("obf", [128, S], BF16)
    mcol = e.sb("mcol", [128, 1])
    vcol = e.sb("vcol", [128, 1])
    tcol = e.sb("tcol", [128, 1])
    rcol = e.sb("rcol", [128, 1])
    mx_up = e.sb("mx_up", [128, 1])
    mx_g = e.sb("mx_g", [128, 1])
    s_sc = e.sb("s_sc", [1, 8])
    scol_xn = e.sb("scol_xn", [128, 1])
    scol_up = e.sb("scol_up", [128, 1])
    scol_g = e.sb("scol_g", [128, 1])
    scol_g2 = e.sb("scol_g2", [128, 1])
    scol_o = e.sb("scol_o", [128, 1])
    onef32 = e.sb("onef32", [1, 128])
    idf32 = e.sb("idf32", [128, 128])
    identb = e.sb("identb", [128, 128], BF16)
    mrow = e.sb("mrow", [1, 128])
    wdT = e.sb("wdT", [128, 24 * 1024], BF16)
    oT_t = e.sb("oT_t", [128, 24 * 128], BF16)
    res32 = e.sb("res32", [128, D])

    ps_k = [e.psum(f"psB_{i}", [128, 512], F32) for i in range(4)]
    ps_g2 = [e.psum(f"psgB_{i}", [128, 512], F32) for i in range(2)]
    ps_t = e.psum("ps_tB", [128, 1024], BF16)
    ps_h = e.psum("ps_hB", [128, 512], F32)

    e.dma(idf32[:], IDF32[:])
    e.dma(identb[:], IDENT[:])
    T("vector", lambda: nc.vector.memset(onef32[:], 1.0))

    def bcast_col(src11, dst_col, dep):
        mm = T("tensor", lambda src11=src11: nc.tensor.matmul(
            out=ps_h[0:128, 0:1], lhsT=onef32[:], rhs=src11, start=True, stop=True),
            after=dep + [("scalar", e.now("scalar")), ("dma", e.now("dma"))])
        return T("scalar", lambda dst_col=dst_col: nc.scalar.copy(out=dst_col[:], in_=ps_h[0:128, 0:1]),
                 after=[("tensor", mm)])

    def scale_from_cc(slot, dst_col, cc_dep):
        flat = bass.AP(e.ccbuf.tensor, slot, [[1, 1], [1, 1]])
        d = e.dma(s_sc[0:1, 0:1], flat, after=cc_dep + [("vector", e.now("vector")), ("tensor", e.now("tensor"))],
                  queue="gpsimd")
        T("vector", lambda: nc.vector.tensor_scalar(out=s_sc[0:1, 0:1], in0=s_sc[0:1, 0:1], scalar1=1.0 / 32767.0,
                                                    scalar2=1e-12, op0=ALU.mult, op1=ALU.max),
          after=[("dmag", d)])
        return bcast_col(s_sc[0:1, 0:1], dst_col, [("vector", e.now("vector"))])

    def qdq_inplace(ap, scol):
        T("vector", lambda: nc.vector.reciprocal(out=rcol[:], in_=scol[:]), after=[("scalar", e.now("scalar"))])
        T("vector", lambda ap=ap: nc.vector.tensor_scalar(out=ap, in0=ap, scalar1=rcol[:], scalar2=None, op0=ALU.mult))
        T("vector", lambda ap=ap: nc.vector.tensor_scalar(out=ap, in0=ap, scalar1=MAGIC, scalar2=-MAGIC, op0=ALU.add, op1=ALU.add))
        T("vector", lambda ap=ap, scol=scol: nc.vector.tensor_scalar(out=ap, in0=ap, scalar1=scol[:], scalar2=None, op0=ALU.mult))

    scale_from_cc(0, scol_xn, [("gpsimd", e.now("gpsimd"))])

    for t in range(NT):
        dx = e.dma(xa[:], x[t * 128:(t + 1) * 128, :], after=[("scalar", e.now("scalar")), ("vector", e.now("vector"))])
        T("vector", lambda: nc.vector.reduce_sum(out=mcol[:], in_=xa[:], axis=AX.X), after=[("dma", dx)])
        sq = T("scalar", lambda: nc.scalar.activation(out=scr32[:], in_=xa[:], func=AF.Square, accum_out=vcol[:]),
               after=[("dma", dx)])
        T("vector", lambda: nc.vector.tensor_scalar(out=mcol[:], in0=mcol[:], scalar1=1.0 / D, scalar2=None, op0=ALU.mult))
        T("vector", lambda: nc.vector.tensor_scalar(out=vcol[:], in0=vcol[:], scalar1=1.0 / D, scalar2=None, op0=ALU.mult),
          after=[("scalar", sq)])
        T("vector", lambda: nc.vector.tensor_scalar(out=tcol[:], in0=mcol[:], scalar1=mcol[:], scalar2=None, op0=ALU.mult))
        T("vector", lambda: nc.vector.tensor_sub(out=vcol[:], in0=vcol[:], in1=tcol[:]))
        vv = T("vector", lambda: nc.vector.tensor_scalar(out=vcol[:], in0=vcol[:], scalar1=1e-5, scalar2=None, op0=ALU.add))
        sv = T("scalar", lambda: nc.scalar.activation(out=vcol[:], in_=vcol[:], func=AF.Sqrt), after=[("vector", vv)])
        T("vector", lambda: nc.vector.reciprocal(out=vcol[:], in_=vcol[:]), after=[("scalar", sv)])
        T("vector", lambda: nc.vector.tensor_scalar(out=xa[:], in0=xa[:], scalar1=mcol[:], scalar2=vcol[:],
                                                    op0=ALU.subtract, op1=ALU.mult))
        qdq_inplace(xa[:], scol_xn)
        cs = T("scalar", lambda: nc.scalar.copy(out=scr_bf[:], in_=xa[:]),
               after=[("vector", e.now("vector")), ("tensor", e.now("tensor"))])
        for ic in range(NC):
            T("tensor", lambda ic=ic: nc.tensor.transpose(
                out=ps_t[:, ic * 128:(ic + 1) * 128], in_=scr_bf[:, ic * 128:(ic + 1) * 128], identity=identb[:]),
              after=[("scalar", cs)] if ic == 0 else [])
        tc = e.now("tensor")
        for ic in range(NC):
            T("scalar", lambda t=t, ic=ic: nc.scalar.copy(
                out=xqT[:, ic * S + t * 128: ic * S + (t + 1) * 128], in_=ps_t[:, ic * 128:(ic + 1) * 128]),
              after=[("tensor", tc)] if ic == 0 else [])
    T("vector", lambda: nc.vector.memset(mx_up[:], 0.0))
    T("vector", lambda: nc.vector.memset(mx_g[:], 0.0))

    def dequant_to_wT(Wsrc, oc):
        nblk = 32
        d = e.dma(w32[:, 0:D], Wsrc[oc * 128:(oc + 1) * 128, :],
                  after=[("vector", e.now("vector")), ("tensor", e.now("tensor"))])
        T("vector", lambda: nc.vector.tensor_reduce(
            out=bm[:, 0:nblk], in_=w32[:, 0:D].rearrange("p (b k) -> p b k", k=32),
            axis=AX.X, op=ALU.max, apply_absolute_value=True), after=[("dma", d)])
        T("vector", lambda: nc.vector.tensor_scalar(out=bm[:, 0:nblk], in0=bm[:, 0:nblk], scalar1=1.0 / 7.0,
                                                    scalar2=1e-12, op0=ALU.mult, op1=ALU.max))
        w3 = w32[:, 0:D].rearrange("p (b k) -> p b k", k=32)
        T("vector", lambda: nc.vector.reciprocal(out=rbm[:, 0:nblk], in_=bm[:, 0:nblk]))
        bmap = bm[:, 0:nblk]
        b3 = bass.AP(bmap.tensor, bmap.offset, [list(bmap.ap[0]), [bmap.ap[1][0], nblk], [0, 32]])
        rmap = rbm[:, 0:nblk]
        r3 = bass.AP(rmap.tensor, rmap.offset, [list(rmap.ap[0]), [rmap.ap[1][0], nblk], [0, 32]])
        T("vector", lambda: nc.vector.tensor_tensor(out=w3, in0=w3, in1=r3, op=ALU.mult))
        T("vector", lambda: nc.vector.tensor_scalar(out=w32[:, 0:D], in0=w32[:, 0:D], scalar1=MAGIC,
                                                    scalar2=-MAGIC, op0=ALU.add, op1=ALU.add))
        wv = T("vector", lambda: nc.vector.tensor_tensor(out=w3, in0=w3, in1=b3, op=ALU.mult))
        cb = T("scalar", lambda: nc.scalar.copy(out=wbf[:, 0:D], in_=w32[:, 0:D]), after=[("vector", wv)])
        for ic in range(NC):
            T("tensor", lambda ic=ic: nc.tensor.transpose(
                out=ps_t[:, ic * 128:(ic + 1) * 128], in_=wbf[:, ic * 128:(ic + 1) * 128], identity=identb[:]),
              after=[("scalar", cb), ("scalar", e.now("scalar"))] if ic == 0 else [])
        tc = e.now("tensor")
        for ic in range(NC):
            T("scalar", lambda ic=ic: nc.scalar.copy(
                out=wT[:, ic * 128:(ic + 1) * 128], in_=ps_t[:, ic * 128:(ic + 1) * 128]),
              after=[("tensor", tc)] if ic == 0 else [])
        return e.now("scalar")

    def hidden_matmul(dst_sb, oc, psl):
        for nb in range(4):
            ps = psl[nb % 2]
            deps = [("scalar", e.now("scalar")), ("vector", e.now("vector"))]
            for ic in range(NC):
                T("tensor", lambda ic=ic, nb=nb, ps=ps: nc.tensor.matmul(
                    out=ps[:], lhsT=wT[:, ic * 128:(ic + 1) * 128],
                    rhs=xqT[:, ic * S + nb * 512: ic * S + (nb + 1) * 512],
                    start=(ic == 0), stop=(ic == NC - 1)), after=deps if ic == 0 else [])
            mmc = e.now("tensor")
            T("scalar", lambda nb=nb, ps=ps, dst_sb=dst_sb: nc.scalar.copy(
                out=dst_sb[:, nb * 512:(nb + 1) * 512], in_=ps[:]), after=[("tensor", mmc)])
        return e.now("scalar")

    def maxacc(src_sb, acc_col):
        T("vector", lambda src_sb=src_sb: nc.vector.tensor_reduce(
            out=tcol[:], in_=src_sb[:], axis=AX.X, op=ALU.max, apply_absolute_value=True),
          after=[("scalar", e.now("scalar"))])
        return T("vector", lambda acc_col=acc_col: nc.vector.tensor_max(out=acc_col[:], in0=acc_col[:], in1=tcol[:]))

    def col_to_val(col, dst11):
        mm = T("tensor", lambda col=col: nc.tensor.matmul(out=ps_h[0:1, 0:128], lhsT=col[:], rhs=idf32[:],
                                                          start=True, stop=True),
               after=[("vector", e.now("vector")), ("scalar", e.now("scalar"))])
        sc = T("scalar", lambda: nc.scalar.copy(out=mrow[:], in_=ps_h[0:1, 0:128]), after=[("tensor", mm)])
        return T("vector", lambda dst11=dst11: nc.vector.tensor_reduce(out=dst11, in_=mrow[:], axis=AX.X, op=ALU.max),
                 after=[("scalar", sc)])

    for oc in range(24):
        dequant_to_wT(WGATE, oc)
        gev = hidden_matmul(gf, oc, ps_g2)
        e.dma(GD[oc * 128:(oc + 1) * 128, :], gf[:], after=[("scalar", gev)])
        maxacc(gf, mx_g)
        dequant_to_wT(WUP, oc)
        uev = hidden_matmul(upf, oc, [ps_k[2], ps_k[3]])
        e.dma(UD[oc * 128:(oc + 1) * 128, :], upf[:], after=[("scalar", uev)])
        maxacc(upf, mx_up)
    col_to_val(mx_up, s_sc[0:1, 0:1])
    v2 = col_to_val(mx_g, s_sc[0:1, 1:2])
    _, cc12 = e.allreduce_max(s_sc[0:1, 0:2], 2, after=[("vector", v2)])
    scale_from_cc(1, scol_up, [("gpsimd", cc12)])
    scale_from_cc(2, scol_g, [("gpsimd", cc12)])

    T("vector", lambda: nc.vector.memset(mx_g[:], 0.0))
    for oc in range(24):
        d = e.dma(gf[:], GD[oc * 128:(oc + 1) * 128, :],
                  after=[("vector", e.now("vector")), ("scalar", e.now("scalar")), ("dma", e.now("dma"))])
        T("vector", lambda: nc.vector.tensor_copy(out=tcol[:], in_=tcol[:]), after=[("dma", d)])
        qdq_inplace(gf[:], scol_g)
        sg = T("scalar", lambda: nc.scalar.activation(out=sigf[:], in_=gf[:], func=AF.Sigmoid),
               after=[("vector", e.now("vector"))])
        T("vector", lambda: nc.vector.tensor_scalar(out=sigf[:], in0=sigf[:], scalar1=65536.0, scalar2=MAGIC,
                                                    op0=ALU.mult, op1=ALU.add), after=[("scalar", sg)])
        T("vector", lambda: nc.vector.tensor_scalar(out=sigf[:], in0=sigf[:], scalar1=-MAGIC, scalar2=65535.0,
                                                    op0=ALU.add, op1=ALU.min))
        g2v = T("vector", lambda: nc.vector.scalar_tensor_tensor(out=gf[:], in0=sigf[:], scalar=1.0 / 65536.0,
                                                                 in1=gf[:], op0=ALU.mult, op1=ALU.mult))
        e.dma(GD[oc * 128:(oc + 1) * 128, :], gf[:], after=[("vector", g2v)])
        maxacc(gf, mx_g)
    v3 = col_to_val(mx_g, s_sc[0:1, 0:1])
    _, cc3 = e.allreduce_max(s_sc[0:1, 0:1], 1, after=[("vector", v3)])
    scale_from_cc(3, scol_g2, [("gpsimd", cc3)])

    T("vector", lambda: nc.vector.memset(mx_g[:], 0.0))
    for oc in range(24):
        d1 = e.dma(gf[:], GD[oc * 128:(oc + 1) * 128, :],
                   after=[("vector", e.now("vector")), ("scalar", e.now("scalar")), ("dma", e.now("dma"))])
        d2 = e.dma(upf[:], UD[oc * 128:(oc + 1) * 128, :])
        T("vector", lambda: nc.vector.tensor_copy(out=tcol[:], in_=tcol[:]), after=[("dma", d2)])
        qdq_inplace(gf[:], scol_g2)
        qdq_inplace(upf[:], scol_up)
        ov = T("vector", lambda: nc.vector.tensor_mul(out=gf[:], in0=gf[:], in1=upf[:]))
        e.dma(UD[oc * 128:(oc + 1) * 128, :], gf[:], after=[("vector", ov)])
        maxacc(gf, mx_g)
    v4 = col_to_val(mx_g, s_sc[0:1, 0:1])
    _, cc4 = e.allreduce_max(s_sc[0:1, 0:1], 1, after=[("vector", v4)])
    scale_from_cc(4, scol_o, [("gpsimd", cc4)])

    for oc in range(24):
        d1 = e.dma(gf[:], UD[oc * 128:(oc + 1) * 128, :],
                   after=[("vector", e.now("vector")), ("scalar", e.now("scalar")), ("dma", e.now("dma"))])
        T("vector", lambda: nc.vector.tensor_copy(out=tcol[:], in_=tcol[:]), after=[("dma", d1)])
        qdq_inplace(gf[:], scol_o)
        ob = T("vector", lambda: nc.vector.tensor_copy(out=obf[:], in_=gf[:]), after=[("dma", e.now("dma"))])
        e.dma(OD[oc * 128:(oc + 1) * 128, :], obf[:], after=[("vector", ob)])

    for c in range(NC):
        for g3 in range(3):
            dg = e.dma(w32[:, 0:D], WDOWN[c * 128:(c + 1) * 128, g3 * 1024:(g3 + 1) * 1024],
                       after=[("vector", e.now("vector")), ("tensor", e.now("tensor"))])
            nblk = 32
            T("vector", lambda: nc.vector.tensor_reduce(
                out=bm[:, 0:nblk], in_=w32[:, 0:D].rearrange("p (b k) -> p b k", k=32),
                axis=AX.X, op=ALU.max, apply_absolute_value=True), after=[("dma", dg)])
            T("vector", lambda: nc.vector.tensor_scalar(out=bm[:, 0:nblk], in0=bm[:, 0:nblk], scalar1=1.0 / 7.0,
                                                        scalar2=1e-12, op0=ALU.mult, op1=ALU.max))
            w3 = w32[:, 0:D].rearrange("p (b k) -> p b k", k=32)
            T("vector", lambda: nc.vector.reciprocal(out=rbm[:, 0:nblk], in_=bm[:, 0:nblk]))
            bmap = bm[:, 0:nblk]
            b3 = bass.AP(bmap.tensor, bmap.offset, [list(bmap.ap[0]), [bmap.ap[1][0], nblk], [0, 32]])
            rmap = rbm[:, 0:nblk]
            r3 = bass.AP(rmap.tensor, rmap.offset, [list(rmap.ap[0]), [rmap.ap[1][0], nblk], [0, 32]])
            T("vector", lambda w3=w3, r3=r3: nc.vector.tensor_tensor(out=w3, in0=w3, in1=r3, op=ALU.mult))
            T("vector", lambda: nc.vector.tensor_scalar(out=w32[:, 0:D], in0=w32[:, 0:D], scalar1=MAGIC,
                                                        scalar2=-MAGIC, op0=ALU.add, op1=ALU.add))
            wv = T("vector", lambda w3=w3, b3=b3: nc.vector.tensor_tensor(out=w3, in0=w3, in1=b3, op=ALU.mult))
            cb = T("scalar", lambda: nc.scalar.copy(out=wbf[:, 0:D], in_=w32[:, 0:D]), after=[("vector", wv)])
            for ic in range(8):
                T("tensor", lambda ic=ic: nc.tensor.transpose(
                    out=ps_t[:, ic * 128:(ic + 1) * 128], in_=wbf[:, ic * 128:(ic + 1) * 128], identity=identb[:]),
                  after=[("scalar", cb), ("scalar", e.now("scalar"))] if ic == 0 else [])
            tc = e.now("tensor")
            for ic in range(8):
                icg = g3 * 8 + ic
                T("scalar", lambda icg=icg, ic=ic, c=c: nc.scalar.copy(
                    out=wdT[:, icg * 1024 + c * 128: icg * 1024 + (c + 1) * 128],
                    in_=ps_t[:, ic * 128:(ic + 1) * 128]),
                  after=[("tensor", tc)] if ic == 0 else [])
    wd_done = e.now("scalar")

    for t in range(NT):
        dl = []
        for icg in range(24):
            dl.append(e.dma(oT_t[:, icg * 128:(icg + 1) * 128],
                            OD[icg * 128:(icg + 1) * 128, t * 128:(t + 1) * 128],
                            after=[("tensor", e.now("tensor"))] if icg == 0 else []))
        dx = e.dma(res32[:], x[t * 128:(t + 1) * 128, :], after=[("vector", e.now("vector"))])
        dat = e.dma(xa[:], ATTN0[t * 128:(t + 1) * 128, :])
        av = T("vector", lambda: nc.vector.tensor_add(out=res32[:], in0=res32[:], in1=xa[:]),
               after=[("dma", e.now("dma"))])
        for nb in range(2):
            ps = ps_k[nb]
            deps = [("scalar", wd_done), ("vector", av), ("dma", dl[-1]), ("scalar", e.now("scalar"))]
            for icg in range(24):
                T("tensor", lambda icg=icg, nb=nb, ps=ps: nc.tensor.matmul(
                    out=ps[:], lhsT=oT_t[:, icg * 128:(icg + 1) * 128],
                    rhs=wdT[:, icg * 1024 + nb * 512: icg * 1024 + (nb + 1) * 512],
                    start=(icg == 0), stop=(icg == 23)), after=deps if icg == 0 else [])
            mmc = e.now("tensor")
            fv = T("vector", lambda t=t, nb=nb, ps=ps: nc.vector.tensor_add(
                out=res32[:, nb * 512:(nb + 1) * 512], in0=res32[:, nb * 512:(nb + 1) * 512], in1=ps[:]),
                after=[("tensor", mmc)])
            e.dma(X1[t * 128:(t + 1) * 128, nb * 512:(nb + 1) * 512], res32[:, nb * 512:(nb + 1) * 512],
                  after=[("vector", fv)])
    return


def build_fused(debug=False):
    e = Emitter("fused", fused=True)
    emit_l0attn(e, fused=True, debug=False)
    e.fence()
    emit_ffn_fused(e, debug=False)
    e.fence()
    emit_l1(e, fused=True, debug=debug)
    e.op("gpsimd", lambda: None, after=e.all_now())
    return e.emit_and_close()


def _fused_in_map(inputs, c):
    m = dict(_l1_in_map(inputs, c))
    m.update({"wup": inputs["L0_ffn_w_up"], "wgate": inputs["L0_ffn_w_gate"],
              "wdown": inputs["L0_ffn_w_down"]})
    m6 = _l6_in_map(inputs, c, None)
    m6.pop("x1")
    m.update(m6)
    return {k: np.ascontiguousarray(v) for k, v in m.items()}


def kernel_fused(**inputs):
    inputs = {k: np.asarray(v, dtype=np.float32) for k, v in inputs.items()}
    ncf = build_fused()
    res = run_bass_kernel_spmd(ncf, [_fused_in_map(inputs, c) for c in range(8)], core_ids=list(range(8)))
    return np.stack([np.asarray(res.results[c]["x2"]) for c in range(8)])


# version 04
# speedup vs baseline: 1.0144x; 1.0144x over previous
import math
from contextlib import ExitStack

import numpy as np
import ml_dtypes
import concourse.bass as bass
import concourse.mybir as mybir
from concourse.bass_utils import run_bass_kernel_spmd

F32 = mybir.dt.float32
BF16 = mybir.dt.bfloat16
AF = mybir.ActivationFunctionType
ALU = mybir.AluOpType
AX = mybir.AxisListType

S = 2048
D = 1024
NT = S // 128
NC = D // 128
H, HD = 16, 64
MAGIC = 12582912.0

BF = ml_dtypes.bfloat16


def _host_consts():
    c = {}
    NB, MD = 32, 128
    rel = np.arange(-2047, 2049)
    nb = NB // 2
    buckets = (rel > 0).astype(np.int64) * nb
    rp = np.abs(rel)
    max_exact = nb // 2
    is_small = rp < max_exact
    large = max_exact + (
        np.log(np.maximum(rp, 1) / max_exact) / np.log(MD / max_exact) * (nb - max_exact)
    ).astype(np.int64)
    large = np.minimum(large, nb - 1)
    buckets = buckets + np.where(is_small, rp, large)
    ohrt = np.zeros((NB, 4096), np.float32)
    ohrt[buckets[::-1], np.arange(4096)] = 1.0
    c["ohrt"] = ohrt.astype(BF)
    grp33 = np.zeros((8, 33), np.float32)
    grp33[0:4, 0] = 1.0
    grp33[4:8, 32] = 1.0
    c["grp33"] = grp33.astype(BF)
    c["ident_bf"] = np.eye(128, dtype=np.float32).astype(BF)
    c["ident_f32"] = np.eye(128, dtype=np.float32)
    inv = 1.0 / (10000.0 ** (np.arange(32) / 32.0))
    ang = inv[:, None] * np.arange(S)[None, :]
    c["cos32"] = np.tile(np.cos(ang), (4, 1)).astype(BF)
    c["sin32"] = np.tile(np.sin(ang), (4, 1)).astype(BF)
    caus = np.zeros((128, 4, 512), np.float32)
    for v in range(4):
        off = 128 * v
        caus[:, v, :] = np.where(np.arange(512)[None, :] > off + np.arange(128)[:, None], -1e9, 0.0)
    c["caus"] = caus.reshape(128, 2048).astype(BF)
    perm = np.zeros(D, np.int64)
    for v in range(8):
        for a in range(4):
            for j in range(32):
                perm[v * 128 + a * 32 + j] = (4 * (v // 2) + a) * 64 + 32 * (v % 2) + j
    c["rope_perm"] = perm
    return c


_GNOISE = None


def _gumbel_noise():
    global _GNOISE
    if _GNOISE is None:
        import subprocess, sys, tempfile, os
        fd, path = tempfile.mkstemp(suffix=".npy")
        os.close(fd)
        code = (
            "import jax\n"
            "jax.config.update('jax_platforms', 'cpu')\n"
            "import jax.numpy as jnp, numpy as np, sys\n"
            "n = jax.random.gumbel(jax.random.key(42), (8, 2048, 8), jnp.float32) * 0.05\n"
            f"np.save({path!r}, np.asarray(n))\n"
        )
        env = dict(os.environ)
        env["JAX_PLATFORMS"] = "cpu"
        subprocess.run([sys.executable, "-c", code], check=True, env=env,
                       capture_output=True, timeout=600)
        _GNOISE = np.load(path)
        os.unlink(path)
    return _GNOISE


_CONSTS = _host_consts()


ENGINES = ("sync", "tensor", "vector", "scalar", "gpsimd")


class Emitter:

    def __init__(self, name, fused=False):
        self.nc = bass.Bass("TRN2", target_bir_lowering=False, num_devices=8)
        self.es = ExitStack()
        self.name = name
        self.fused = fused
        self.plan = {e: [] for e in ENGINES}
        self.sems = {}
        self.counts = {}
        self._drams = {}
        self.pool = None
        self._bump = 0
        self._tag = ""
        for e in ENGINES:
            self._mksem(e)
        self._mksem("dma")
        self._mksem("dmag")
        if fused:
            self.pool = self.es.enter_context(self.nc.sbuf_tensor("pool", [128, 212000], mybir.dt.uint8))
            self.ccbuf = self.nc.dram_tensor("ccbuf", [128, 512], F32, kind="Internal").ap()
            self.ccn = 0

    def phase(self, tag):
        self._bump = 0
        self._tag = tag + "_"
        if getattr(self, "_phase_es", None) is not None:
            self._phase_es.close()
        self._phase_es = ExitStack()

    def dr(self, name, shape, dtype=F32, kind="ExternalInput"):
        if name in self._drams:
            return self._drams[name]
        ap = self.nc.dram_tensor(name, shape, dtype, kind=kind).ap()
        self._drams[name] = ap
        return ap

    def allreduce_max(self, src_col_ap, n_vals, after):
        k0 = self.ccn
        self.ccn += n_vals
        assert self.ccn <= 512
        dst = bass.AP(self.ccbuf.tensor, k0, [[1, 1], [1, n_vals]])
        d = self.dma(dst, src_col_ap, after=after, queue="gpsimd")
        cc = self.op("gpsimd", lambda dst=dst: self.nc.gpsimd.collective_compute(
            "AllReduce", ALU.max, replica_groups=[[0, 1, 2, 3, 4, 5, 6, 7]],
            ins=[dst], outs=[dst]), after=[("dmag", d)], inc="gpsimd")
        return dst, cc

    def _mksem(self, n):
        self.sems[n] = self.es.enter_context(self.nc.semaphore("sem_" + n))
        self.counts[n] = 0

    def now(self, sem):
        return self.counts[sem]

    def snap(self):
        return dict(self.counts)

    def op(self, engine, fn, after=(), inc=None):
        if inc is None:
            inc = "dma" if engine == "sync" else engine
        n = 16 if inc in ("dma", "dmag") else 1
        after = tuple(after)
        if engine in ("vector", "scalar", "gpsimd") and self.counts.get(engine, 0) > 0:
            after = after + ((engine, self.counts[engine]),)
        self.counts[inc] += n
        self.plan[engine].append((fn, tuple(after), self.sems[inc], n, inc, self.counts[inc]))
        return self.counts[inc]

    def dma(self, out, in_, after=(), queue="sync"):
        sem = "dma" if queue == "sync" else "dmag"
        eng = self.nc.sync if queue == "sync" else self.nc.gpsimd
        return self.op(queue, lambda o=out, i=in_, g=eng: g.dma_start(out=o, in_=i),
                       after=after, inc=sem)

    def all_now(self):
        return [(s, c) for s, c in self.counts.items() if c > 0]

    def fence(self):
        waits = self.all_now()
        for eng in ENGINES:
            if eng == "sync":
                continue
            self.op(eng, lambda: None, after=waits, inc=eng)

    def verify_acyclic(self):
        prod = {}
        nodes = []
        for eng in ENGINES:
            for i, (fn, after, sem, n, inc, cnt) in enumerate(self.plan[eng]):
                prod[(inc, cnt)] = (eng, i)
                nodes.append((eng, i))
        bysem = {}
        for eng in ENGINES:
            for i, (fn, after, sem, n, inc, cnt) in enumerate(self.plan[eng]):
                bysem.setdefault(inc, []).append((cnt, eng, i))
        for k in bysem:
            bysem[k].sort()
        import bisect as _b
        edges = {nd: [] for nd in nodes}
        indeg = {nd: 0 for nd in nodes}
        def add(a, b):
            edges[a].append(b)
            indeg[b] += 1
        for eng in ENGINES:
            pl = self.plan[eng]
            for i in range(1, len(pl)):
                add((eng, i - 1), (eng, i))
            for i, (fn, after, sem, n, inc, cnt) in enumerate(pl):
                for sname, c in after:
                    if c <= 0:
                        continue
                    lst = bysem.get(sname, [])
                    j = _b.bisect_left(lst, (c, "", -1))
                    assert j < len(lst) and lst[j][0] >= c, (eng, i, sname, c)
                    _, peng, pi = lst[j]
                    add((peng, pi), (eng, i))
        from collections import deque
        q = deque([nd for nd in nodes if indeg[nd] == 0])
        seen = 0
        while q:
            nd = q.popleft()
            seen += 1
            for m in edges[nd]:
                indeg[m] -= 1
                if indeg[m] == 0:
                    q.append(m)
        if seen != len(nodes):
            stuck = [nd for nd in nodes if indeg[nd] > 0]
            print(f"DEADLOCK: {len(nodes)-seen} ops unreachable; first stuck: {stuck[:6]}")
            for nd in stuck[:4]:
                eng, i = nd
                fn, after, sem, n, inc, cnt = self.plan[eng][i]
                print("  stuck op", eng, i, "waits", after)
            return False
        return True

    def emit_and_close(self):
        needed = {}
        for eng in ENGINES:
            for (fn, after, sem, n, inc, cnt) in self.plan[eng]:
                for sname, c in after:
                    if c > 0:
                        needed.setdefault(sname, set()).add(c)
        block = self.es.enter_context(self.nc.Block())
        for eng_name in ENGINES:
            ops = self.plan[eng_name]
            if not ops:
                continue
            last_idx = {}
            for k, op_ in enumerate(ops):
                last_idx[op_[4]] = k

            def section(eng, ops=ops, last_idx=last_idx):
                pending = {}
                for k, (fn, after, sem, n, inc, cnt) in enumerate(ops):
                    for sname, c in after:
                        if c > 0:
                            eng.wait_ge(self.sems[sname], c)
                    inst = fn()
                    pend = pending.get(inc, 0) + n
                    sparse = inc in ("tensor", "vector", "scalar", "gpsimd") and inst is not None
                    take = (not sparse) or (cnt in needed.get(inc, ())) or (k == last_idx[inc])
                    if take:
                        if inst is not None:
                            inst.then_inc(self.sems[inc], pend)
                        else:
                            eng.sem_inc(self.sems[inc], pend)
                        pending[inc] = 0
                    else:
                        pending[inc] = pend
            getattr(block, eng_name)(section)
        self.es.close()
        return self.nc

    def sb(self, name, shape, dtype=F32):
        if self.pool is None:
            return self.es.enter_context(self.nc.sbuf_tensor(name, shape, dtype))
        nbytes = shape[1] * {F32: 4, BF16: 2, mybir.dt.uint8: 1, mybir.dt.int32: 4}[dtype]
        nbytes = (nbytes + 31) // 32 * 32
        assert self._bump + nbytes <= 212000, (self._tag, name, self._bump, nbytes)
        t = self.nc.alloc_sbuf_tensor_at(self._tag + name, shape, dtype,
                                         offset=self.nc.lookup_mloc(self.pool).addr + self._bump)
        self._bump += nbytes
        return t

    def sb_at(self, name, shape, dtype, base, byte_off=0):
        addr = self.nc.lookup_mloc(base).addr + byte_off
        return self.nc.alloc_sbuf_tensor_at(self._tag + name, shape, dtype, offset=addr)

    def psum(self, name, shape, dtype=F32):
        if self.pool is not None and getattr(self, "_phase_es", None) is not None:
            return self._phase_es.enter_context(self.nc.psum_tensor(self._tag + name, shape, dtype))
        return self.es.enter_context(self.nc.psum_tensor(name, shape, dtype))


def rev_free(ap2d, lo, hi):
    pat = [list(d) for d in ap2d.ap]
    fstep = pat[1][0]
    return bass.AP(ap2d.tensor, ap2d.offset + (hi - 1) * fstep, [pat[0], [-fstep, hi - lo]])


def emit_l0attn(e, fused=False, debug=False):
    nc = e.nc
    e.phase("A")
    x = e.dr("x", [S, D], F32)
    WQ = e.dr("wq", [D, D], F32)
    WK = e.dr("wk", [D, D], F32)
    WV = e.dr("wv", [D, D], F32)
    WO = e.dr("wo", [D, D], F32)
    BQ = e.dr("bq", [1, D], F32)
    BK = e.dr("bk", [1, D], F32)
    BV = e.dr("bv", [1, D], F32)
    BO = e.dr("bo", [1, D], F32)
    REL = e.dr("rel", [32, 16], F32)
    GRUW = e.dr("gruw", [64, 8], F32)
    GRUB = e.dr("grub", [1, 8], F32)
    GRUC = e.dr("gruc", [1, 16], F32)
    OHRT = e.dr("ohrt", [32, 4096], BF16)
    GRP33 = e.dr("grp33", [8, 33], BF16)
    IDENT = e.dr("identf", [128, 128], BF16)
    IDF32 = e.dr("identf32", [128, 128], F32)

    ATTN0 = e.dr("attn0", [S, D], F32, kind="Internal" if fused else "ExternalOutput")
    if debug:
        XN_D = e.dr("xn_dbg", [S, D], F32, kind="ExternalOutput")
        QT_D = e.dr("qT_dbg", [D, S], BF16, kind="ExternalOutput")
        KT_D = e.dr("kT_dbg", [D, S], BF16, kind="ExternalOutput")
        VTOK_D = e.dr("vtok_dbg", [S, D], BF16, kind="ExternalOutput")
        GATE_D = e.dr("gate_dbg", [H, S], BF16, kind="ExternalOutput")
        S_D = e.dr("s_dbg", [128, S], F32, kind="ExternalOutput")
        SUM_D = e.dr("sum_dbg", [4, 128], F32, kind="ExternalOutput")
        WN_D = e.dr("wn_dbg", [128, S], BF16, kind="ExternalOutput")
        CTX_D = e.dr("ctx_dbg", [D, S], BF16, kind="ExternalOutput")
        TREV_D = e.dr("trevd_dbg", [16, 4096], BF16, kind="ExternalOutput")
    XNMAX = None if fused else e.dr("xnmax", [1, 1], F32, kind="ExternalOutput")
    TREV = e.dr("trev_d", [16, 4096], BF16, kind="Internal")

    xa = e.sb("xa", [128, D])
    xnT = e.sb("xnT", [128, NC * S], BF16)
    qT = e.sb("qT", [128, NC * S], BF16)
    kT = e.sb("kT", [128, NC * S], BF16)
    vtok_all = e.sb("vtok_all", [128, NT * D], BF16)
    ctxT = e.sb("ctxT", [128, NC * S], BF16)
    w_f32 = e.sb("w_f32", [128, 512])
    w_bf = e.sb("w_bf", [128, NC * 512], BF16)
    scr_bf = e.sb("scr_bf", [128, D], BF16)
    scr32 = e.sb("scr32", [128, D])
    ssb = e.sb("ssb", [128, S])
    wn = e.sb("wn", [128, S], BF16)
    shared = e.sb("shared", [128, S], BF16)
    grow = e.sb("grow", [1, S], BF16)
    gateT = e.sb("gateT", [128, 16])
    mcol = e.sb("mcol", [128, 1])
    vcol = e.sb("vcol", [128, 1])
    negm = e.sb("negm", [128, 1])
    zcol = e.sb("zcol", [128, 1])
    rz = e.sb("rz", [128, 1])
    mx16 = e.sb("mx16", [128, NT])
    mxacc = e.sb("mxacc", [128, 1])
    mx1 = e.sb("mx1", [1, 1])
    ident = e.sb("ident", [128, 128], BF16)
    idf32 = e.sb("idf32", [128, 128])
    rel_sb = e.sb("rel_sb", [32, 16], BF16)
    ohrt_sb = e.sb("ohrt_sb", [32, 128], BF16)
    ttmp = e.sb("ttmp", [128, 16])
    trev_sb = e.sb("trev_sb", [16, 128], BF16)
    gruw_sb = e.sb("gruw_sb", [64, 8], BF16)
    gruwT = e.sb("gruwT", [8, 64], BF16)
    w2d = e.sb("w2d", [128, 33], BF16)
    bias33 = e.sb("bias33", [33, 1])
    grp_sb = e.sb("grp_sb", [8, 33], BF16)
    gcl = e.sb("gcl", [1, 16])
    grub_sb = e.sb("grub_sb", [1, 8])
    grub2 = e.sb("grub2", [1, 2])
    bqc = e.sb("bqc", [128, NC])
    bkc = e.sb("bkc", [128, NC])
    bqs = e.sb("bqs", [128, NC])
    rowbuf_bf = e.sb("rowbuf_bf", [1, D], BF16)
    ones1 = e.sb("ones1", [1, 128], BF16)
    onef = e.sb("onef", [1, 1], BF16)

    ps_k = [e.psum(f"ps_{i}", [128, 512], F32) for i in range(4)]
    ps_t = e.psum("ps_t", [128, 1024], BF16)
    ps_t2 = e.psum("ps_t2", [128, 1024], BF16)
    ps_g = e.psum("ps_g", [128, 512], F32)
    ps_h = e.psum("ps_h", [128, 512], F32)

    T = e.op

    e.dma(ident[:], IDENT[:])
    e.dma(idf32[:], IDF32[:])
    e.dma(rel_sb[:], REL[:], queue="gpsimd")
    e.dma(gruw_sb[:], GRUW[:], queue="gpsimd")
    e.dma(grp_sb[:], GRP33[:])
    e.dma(gcl[:], GRUC[:])
    e.dma(grub_sb[:], GRUB[:])
    for c in range(NC):
        e.dma(bqc[:, c:c + 1], BQ[0:1, c * 128:(c + 1) * 128])
        e.dma(bkc[:, c:c + 1], BK[0:1, c * 128:(c + 1) * 128])
    ld_misc = e.now("dma")
    T("vector", lambda: nc.vector.memset(ones1[:], 1.0))
    T("vector", lambda: nc.vector.memset(onef[:], 1.0))
    T("vector", lambda: nc.vector.memset(bias33[:], 0.0))

    for t in range(NT):
        dx = e.dma(xa[:], x[t * 128:(t + 1) * 128, :], after=[("scalar", e.now("scalar")), ("vector", e.now("vector"))])
        rs = T("vector", lambda: nc.vector.reduce_sum(out=mcol[:], in_=xa[:], axis=AX.X), after=[("dma", dx)])
        if debug and t == 0:
            dtap0 = e.dma(bass.AP(SUM_D.tensor, 0, [[1, 128], [1, 1]]), mcol[:], after=[("vector", rs)])
        sq = T("scalar", lambda: nc.scalar.activation(out=scr32[:], in_=xa[:], func=AF.Square, accum_out=vcol[:]),
               after=[("dma", dx)])
        T("vector", lambda: nc.vector.tensor_scalar(out=mcol[:], in0=mcol[:], scalar1=1.0 / D, scalar2=None, op0=ALU.mult),
          after=[("dma", e.now("dma"))] if (debug and t == 0) else [])
        T("vector", lambda: nc.vector.tensor_scalar(out=vcol[:], in0=vcol[:], scalar1=1.0 / D, scalar2=None, op0=ALU.mult),
          after=[("scalar", sq)])
        T("vector", lambda: nc.vector.tensor_scalar(out=negm[:], in0=mcol[:], scalar1=mcol[:], scalar2=None, op0=ALU.mult))
        T("vector", lambda: nc.vector.tensor_sub(out=vcol[:], in0=vcol[:], in1=negm[:]))
        vv = T("vector", lambda: nc.vector.tensor_scalar(out=vcol[:], in0=vcol[:], scalar1=1e-5, scalar2=None, op0=ALU.add))
        if debug and t == 0:
            e.dma(bass.AP(SUM_D.tensor, 1*128, [[1, 128], [1, 1]]), mcol[:], after=[("vector", vv)])
            e.dma(bass.AP(SUM_D.tensor, 2*128, [[1, 128], [1, 1]]), vcol[:], after=[("vector", vv)])
        sv = T("scalar", lambda: nc.scalar.activation(out=vcol[:], in_=vcol[:], func=AF.Sqrt), after=[("vector", vv)])
        rc = T("vector", lambda: nc.vector.reciprocal(out=vcol[:], in_=vcol[:]), after=[("scalar", sv)])
        if debug and t == 0:
            e.dma(bass.AP(SUM_D.tensor, 3*128, [[1, 128], [1, 1]]), vcol[:], after=[("vector", rc)])
        T("vector", lambda: nc.vector.tensor_scalar(out=xa[:], in0=xa[:], scalar1=mcol[:], scalar2=vcol[:],
                                                    op0=ALU.subtract, op1=ALU.mult),
          after=[("dma", e.now("dma"))] if (debug and t == 0) else [])
        vmx = T("vector", lambda t=t: nc.vector.tensor_reduce(out=mx16[:, t:t + 1], in_=xa[:], axis=AX.X,
                                                              op=ALU.max, apply_absolute_value=True))
        if debug:
            e.dma(XN_D[t * 128:(t + 1) * 128, :], xa[:], after=[("vector", vmx)])
        cs = T("scalar", lambda: nc.scalar.copy(out=scr_bf[:], in_=xa[:]),
               after=[("vector", vmx), ("tensor", e.now("tensor"))])
        for ic in range(NC):
            T("tensor", lambda ic=ic: nc.tensor.transpose(
                out=ps_t[:, ic * 128:(ic + 1) * 128], in_=scr_bf[:, ic * 128:(ic + 1) * 128], identity=ident[:]),
              after=[("scalar", cs), ("dma", e.now("dma"))] if ic == 0 else [])
        tc = e.now("tensor")
        for ic in range(NC):
            T("scalar", lambda t=t, ic=ic: nc.scalar.copy(
                out=xnT[:, ic * S + t * 128: ic * S + (t + 1) * 128], in_=ps_t[:, ic * 128:(ic + 1) * 128]),
              after=[("tensor", tc)] if ic == 0 else [])
    ln_done_v = e.now("vector")

    T("vector", lambda: nc.vector.tensor_reduce(out=mxacc[:], in_=mx16[:], axis=AX.X, op=ALU.max))
    tr1 = T("tensor", lambda: nc.tensor.matmul(out=ps_h[0:1, 0:128], lhsT=mxacc[:], rhs=idf32[:],
                                               start=True, stop=True),
            after=[("vector", e.now("vector")), ("dma", ld_misc)])
    sc1 = T("scalar", lambda: nc.scalar.copy(out=scr32[0:1, 0:128], in_=ps_h[0:1, 0:128]), after=[("tensor", tr1)])
    vm1 = T("vector", lambda: nc.vector.tensor_reduce(out=mx1[:], in_=scr32[0:1, 0:128], axis=AX.X, op=ALU.max),
            after=[("scalar", sc1)])
    if fused:
        e.allreduce_max(mx1[:], 1, after=[("vector", vm1)])
    else:
        e.dma(XNMAX[:], mx1[:], after=[("vector", vm1)])

    for j in range(32):
        dj = e.dma(ohrt_sb[:], OHRT[0:32, j * 128:(j + 1) * 128], after=[("tensor", e.now("tensor"))])
        mm = T("tensor", lambda: nc.tensor.matmul(
            out=ps_h[0:128, 0:16], lhsT=ohrt_sb[:], rhs=rel_sb[:], start=True, stop=True),
            after=[("dma", dj), ("scalar", e.now("scalar"))])
        ev = T("scalar", lambda: nc.scalar.copy(out=ttmp[:], in_=ps_h[0:128, 0:16]), after=[("tensor", mm)])
        tr = T("tensor", lambda: nc.tensor.matmul(out=ps_g[0:16, 0:128], lhsT=ttmp[:], rhs=idf32[:],
                                                  start=True, stop=True), after=[("scalar", ev)])
        sv2 = T("scalar", lambda: nc.scalar.copy(out=trev_sb[:], in_=ps_g[0:16, 0:128]), after=[("tensor", tr)])
        e.dma(TREV[0:16, j * 128:(j + 1) * 128], trev_sb[:], after=[("scalar", sv2)])
    trev_ready = e.now("dma")

    trw = T("tensor", lambda: nc.tensor.matmul(out=ps_g[0:8, 0:64], lhsT=gruw_sb[:], rhs=ident[0:64, 0:64],
                                               start=True, stop=True),
            after=[("dma", ld_misc), ("scalar", e.now("scalar"))])
    ws = T("scalar", lambda: nc.scalar.copy(out=gruwT[:], in_=ps_g[0:8, 0:64]), after=[("tensor", trw)])
    mmw = T("tensor", lambda: nc.tensor.matmul(out=ps_h[0:64, 0:33], lhsT=gruwT[:], rhs=grp_sb[:],
                                               start=True, stop=True), after=[("scalar", ws)])
    w2a = T("scalar", lambda: nc.scalar.copy(out=w2d[0:64, :], in_=ps_h[0:64, 0:33]), after=[("tensor", mmw)])
    T("scalar", lambda: nc.scalar.copy(out=w2d[64:128, :], in_=ps_h[0:64, 0:33]))
    gb = T("vector", lambda: nc.vector.tensor_reduce(
        out=grub2[:], in_=grub_sb[:].rearrange("a (g k) -> a g k", k=4), axis=AX.X, op=ALU.add),
        after=[("dma", ld_misc)])
    e.dma(bias33[0:1, 0:1], grub2[0:1, 0:1], after=[("vector", gb)], queue="gpsimd")
    e.dma(bias33[32:33, 0:1], grub2[0:1, 1:2], after=[("vector", gb)], queue="gpsimd")

    T("vector", lambda: nc.vector.tensor_scalar(out=bqs[:], in0=bqc[:], scalar1=0.125, scalar2=None, op0=ALU.mult),
      after=[("dma", ld_misc)])

    def load_weight_half(W, half):
        for ic in range(NC):
            e.dma(w_bf[:, ic * 512:(ic + 1) * 512], W[ic * 128:(ic + 1) * 128, half * 512:(half + 1) * 512],
                  after=[("tensor", e.now("tensor"))] if ic == 0 else [], queue="gpsimd")
        return e.now("dmag")

    def proj_A(W, outT, bias_cols, scale):
        for half in range(2):
            wld = load_weight_half(W, half)
            for oc4 in range(4):
                oc = half * 4 + oc4
                for nb in range(S // 512):
                    ps = ps_k[nb % 4]
                    deps = [("scalar", e.now("scalar")), ("vector", e.now("vector")), ("dmag", wld)]
                    for ic in range(NC):
                        T("tensor", lambda ic=ic, oc4=oc4, nb=nb, ps=ps: nc.tensor.matmul(
                            out=ps[:], lhsT=w_bf[:, ic * 512 + oc4 * 128: ic * 512 + (oc4 + 1) * 128],
                            rhs=xnT[:, ic * S + nb * 512: ic * S + (nb + 1) * 512],
                            start=(ic == 0), stop=(ic == NC - 1)), after=deps if ic == 0 else [])
                    mmc = e.now("tensor")
                    T("scalar", lambda oc=oc, nb=nb, ps=ps, outT=outT, bias_cols=bias_cols, scale=scale:
                      nc.scalar.activation(out=outT[:, oc * S + nb * 512: oc * S + (nb + 1) * 512], in_=ps[:],
                                           func=AF.Identity, bias=bias_cols[:, oc:oc + 1], scale=scale),
                      after=[("tensor", mmc)])

    proj_A(WQ, qT, bqs, 0.125)
    proj_A(WK, kT, bkc, 1.0)
    dbv = e.dma(rowbuf_bf[:], BV[:], queue="gpsimd")
    for half in range(2):
        wld = load_weight_half(WV, half)
        for t in range(NT):
            ps = ps_k[t % 4]
            deps = [("scalar", e.now("scalar")), ("vector", e.now("vector")), ("dmag", wld)]
            for ic in range(NC):
                T("tensor", lambda t=t, ic=ic, half=half, ps=ps: nc.tensor.matmul(
                    out=ps[:], lhsT=xnT[:, ic * S + t * 128: ic * S + (t + 1) * 128],
                    rhs=w_bf[:, ic * 512:(ic + 1) * 512],
                    start=(ic == 0), stop=False), after=deps if ic == 0 else [])
            T("tensor", lambda half=half, ps=ps: nc.tensor.matmul(
                out=ps[:], lhsT=ones1[:], rhs=rowbuf_bf[0:1, half * 512:(half + 1) * 512],
                start=False, stop=True))
            mmc = e.now("tensor")
            T("scalar", lambda t=t, half=half, ps=ps: nc.scalar.copy(
                out=vtok_all[:, t * D + half * 512: t * D + (half + 1) * 512], in_=ps[:]),
              after=[("tensor", mmc)])

    if debug:
        for c in range(NC):
            e.dma(QT_D[c * 128:(c + 1) * 128, :], qT[:, c * S:(c + 1) * S], after=[("scalar", e.now("scalar"))])
            e.dma(KT_D[c * 128:(c + 1) * 128, :], kT[:, c * S:(c + 1) * S], after=[("scalar", e.now("scalar"))])
        for t in range(NT):
            e.dma(VTOK_D[t * 128:(t + 1) * 128, :], vtok_all[:, t * D:(t + 1) * D], after=[("scalar", e.now("scalar"))])
        e.dma(TREV_D[:], TREV[:], after=[("dma", trev_ready)])

    wn2A = e.sb_at("wn2A", [128, S], BF16, w_bf, 0)
    wnBA = [wn, wn2A]
    twin = e.sb_at("twinA", [128, S], BF16, w_bf, 4096)

    def l0_gate_prep(h):
        c2 = h // 2
        pb = 64 * (h % 2)
        for nb in range(4):
            mm = T("tensor", lambda c2=c2, pb=pb, nb=nb: nc.tensor.matmul(
                out=ps_g[0:33, 0:512], lhsT=w2d[pb:pb + 64, :],
                rhs=xnT[pb:pb + 64, c2 * S + nb * 512: c2 * S + (nb + 1) * 512], start=True, stop=True),
                after=[("scalar", e.now("scalar")), ("vector", e.now("vector")), ("dmag", e.now("dmag"))])
            T("scalar", lambda nb=nb: nc.scalar.activation(
                out=shared[0:33, nb * 512:(nb + 1) * 512], in_=ps_g[0:33, 0:512], func=AF.Sigmoid, bias=bias33[:]),
              after=[("tensor", mm)])
        sgc = e.now("scalar")
        mvd = e.dma(grow[:], shared[32:33, :], after=[("scalar", sgc)], queue="gpsimd")
        T("vector", lambda h=h: nc.vector.tensor_scalar(out=grow[:], in0=grow[:], scalar1=gcl[0:1, h:h + 1],
                                                        scalar2=-1.0, op0=ALU.mult, op1=ALU.add),
          after=[("dmag", mvd)])
        T("vector", lambda: nc.vector.tensor_tensor(out=grow[:], in0=shared[0:1, :], in1=grow[:], op=ALU.mult))
        vg = T("vector", lambda: nc.vector.tensor_scalar(out=grow[:], in0=grow[:], scalar1=2.0, scalar2=None, op0=ALU.add))
        for qt in range(NT):
            T("tensor", lambda qt=qt: nc.tensor.matmul(
                out=ps_h[0:128, qt:qt + 1], lhsT=grow[0:1, qt * 128:(qt + 1) * 128], rhs=onef[:],
                start=True, stop=True),
              after=[("vector", vg), ("scalar", e.now("scalar"))] if qt == 0 else [])
        return T("scalar", lambda: nc.scalar.copy(out=gateT[:], in_=ps_h[0:128, 0:16]),
                 after=[("tensor", e.now("tensor"))])

    def l0_qk(h, qt):
        c2 = h // 2
        pb = 64 * (h % 2)
        win = bass.AP(TREV.tensor, h * 4096 + 128 * qt, [[1, 128], [1, S]])
        twd = e.dma(twin[:], win, after=[("dma", trev_ready), ("vector", e.now("vector"))])
        for kb in range(4):
            T("tensor", lambda c2=c2, pb=pb, qt=qt, kb=kb: nc.tensor.matmul(
                out=ps_k[kb][:], lhsT=qT[pb:pb + 64, c2 * S + qt * 128: c2 * S + (qt + 1) * 128],
                rhs=kT[pb:pb + 64, c2 * S + kb * 512: c2 * S + (kb + 1) * 512], start=True, stop=True),
              after=[("scalar", e.now("scalar")), ("vector", e.now("vector"))] if kb == 0 else [])
        return e.now("tensor"), twd

    def l0_soft(h, qt, mmc, twd, par):
        wnl = wnBA[par]
        for kb in range(4):
            T("vector", lambda qt=qt, kb=kb: nc.vector.scalar_tensor_tensor(
                out=ssb[:, kb * 512:(kb + 1) * 512], in0=rev_free(twin[:], S - (kb + 1) * 512, S - kb * 512),
                scalar=gateT[:, qt:qt + 1], in1=ps_k[kb][:], op0=ALU.mult, op1=ALU.add),
              after=[("tensor", mmc), ("dma", twd), ("scalar", e.now("scalar"))] if kb == 0 else [])
        T("vector", lambda: nc.vector.tensor_reduce(out=mcol[:], in_=ssb[:], axis=AX.X, op=ALU.max))
        T("vector", lambda: nc.vector.tensor_scalar(out=negm[:], in0=mcol[:], scalar1=-1.0, scalar2=None, op0=ALU.mult))
        ex = T("scalar", lambda wnl=wnl: nc.scalar.activation(out=wnl[:], in_=ssb[:], func=AF.Exp, bias=negm[:],
                                                              accum_out=zcol[:]),
               after=[("vector", e.now("vector"))])
        T("vector", lambda: nc.vector.reciprocal(out=rz[:], in_=zcol[:]), after=[("scalar", ex)])
        nw = T("vector", lambda wnl=wnl: nc.vector.tensor_scalar(out=wnl[:], in0=wnl[:], scalar1=rz[:],
                                                                 scalar2=None, op0=ALU.mult))
        return nw

    def l0_pv(h, qt, nw, par):
        wnl = wnBA[par]
        c2 = h // 2
        pb = 64 * (h % 2)
        for c in range(16):
            pst = ps_t if c < 8 else ps_t2
            T("tensor", lambda c=c, pst=pst, wnl=wnl: nc.tensor.transpose(
                out=pst[:, (c % 8) * 128:(c % 8 + 1) * 128], in_=wnl[:, c * 128:(c + 1) * 128], identity=ident[:]),
              after=[("vector", nw), ("scalar", e.now("scalar"))] if c == 0 else [])
        tcn = e.now("tensor")
        T("scalar", lambda: nc.scalar.copy(out=shared[:, 0:1024], in_=ps_t[:]), after=[("tensor", tcn)])
        ev2 = T("scalar", lambda: nc.scalar.copy(out=shared[:, 1024:2048], in_=ps_t2[:]))
        for c in range(16):
            T("tensor", lambda c=c, h=h: nc.tensor.matmul(
                out=ps_h[0:64, 0:128], lhsT=vtok_all[:, c * D + h * 64: c * D + h * 64 + 64],
                rhs=shared[:, c * 128:(c + 1) * 128], start=(c == 0), stop=(c == 15)),
              after=[("scalar", ev2)] if c == 0 else [])
        pvc = e.now("tensor")
        T("scalar", lambda c2=c2, pb=pb, qt=qt: nc.scalar.copy(
            out=ctxT[pb:pb + 64, c2 * S + qt * 128: c2 * S + (qt + 1) * 128], in_=ps_h[0:64, 0:128]),
          after=[("tensor", pvc)])

    unitsA = [(h, qt) for h in range(H) for qt in range(NT)]
    pend = None
    mmc = twd = None
    for i, (h, qt) in enumerate(unitsA):
        if qt == 0:
            if pend is not None:
                l0_pv(*pend)
                pend = None
            l0_gate_prep(h)
            mmc, twd = l0_qk(h, qt)
        nw = l0_soft(h, qt, mmc, twd, i % 2)
        if i + 1 < len(unitsA) and unitsA[i + 1][1] != 0:
            mmc, twd = l0_qk(*unitsA[i + 1])
        if pend is not None:
            l0_pv(*pend)
        pend = (h, qt, nw, i % 2)
    l0_pv(*pend)

    dbo = e.dma(rowbuf_bf[:], BO[:], after=[("tensor", e.now("tensor"))], queue="gpsimd")
    for half in range(2):
        wld = load_weight_half(WO, half)
        for t in range(NT):
            nb = half
            ps = ps_k[t % 4]
            deps = [("scalar", e.now("scalar")), ("vector", e.now("vector")), ("dmag", wld)]
            for ic in range(NC):
                T("tensor", lambda t=t, ic=ic, ps=ps: nc.tensor.matmul(
                    out=ps[:], lhsT=ctxT[:, ic * S + t * 128: ic * S + (t + 1) * 128],
                    rhs=w_bf[:, ic * 512:(ic + 1) * 512],
                    start=(ic == 0), stop=False), after=deps if ic == 0 else [])
            T("tensor", lambda half=half, ps=ps: nc.tensor.matmul(
                out=ps[:], lhsT=ones1[:], rhs=rowbuf_bf[0:1, half * 512:(half + 1) * 512],
                start=False, stop=True))
            mmc = e.now("tensor")
            sv3 = T("scalar", lambda ps=ps: nc.scalar.copy(out=scr32[:, 0:512], in_=ps[:]), after=[("tensor", mmc)])
            e.dma(ATTN0[t * 128:(t + 1) * 128, nb * 512:(nb + 1) * 512], scr32[:, 0:512], after=[("scalar", sv3)])

    return


def build_launch1(debug=False):
    e = Emitter("l1")
    emit_l0attn(e, fused=False, debug=debug)
    e.op("gpsimd", lambda: None, after=e.all_now())
    return e.emit_and_close()


def _l1_in_map(inputs, c):
    consts = _CONSTS
    m = {
        "x": inputs["L0_attn_x"][c],
        "wq": inputs["L0_attn_Wq"], "wk": inputs["L0_attn_Wk"],
        "wv": inputs["L0_attn_Wv"], "wo": inputs["L0_attn_Wo"],
        "bq": inputs["L0_attn_bq"].reshape(1, D), "bk": inputs["L0_attn_bk"].reshape(1, D),
        "bv": inputs["L0_attn_bv"].reshape(1, D), "bo": inputs["L0_attn_bo"].reshape(1, D),
        "rel": inputs["L0_attn_rel_embed"], "gruw": inputs["L0_attn_gru_w"],
        "grub": inputs["L0_attn_gru_b"].reshape(1, 8), "gruc": inputs["L0_attn_gru_const"].reshape(1, 16),
        "ohrt": consts["ohrt"], "grp33": consts["grp33"],
        "identf": consts["ident_bf"], "identf32": consts["ident_f32"],
    }
    return {k: np.ascontiguousarray(v) for k, v in m.items()}


def _ffn_in_map(inputs, c, stage, st):
    consts = _CONSTS
    m = {
        "x": inputs["L0_attn_x"][c],
        "xnmax8": st["xnmax8"],
        "wup": inputs["L0_ffn_w_up"], "wgate": inputs["L0_ffn_w_gate"],
        "identf32": consts["ident_f32"], "identf": consts["ident_bf"],
    }
    if stage >= 2:
        m["ugmax8"] = st["ugmax8"]
    if stage >= 3:
        m["g2max8"] = st["g2max8"]
    if stage >= 4:
        m["omax8"] = st["omax8"]
        m["wdown"] = inputs["L0_ffn_w_down"]
        m["attn0"] = st["attn0"][c]
    return {k: np.ascontiguousarray(v) for k, v in m.items()}


def run_l0(inputs):
    st = {}
    nc1 = build_launch1()
    res = run_bass_kernel_spmd(nc1, [_l1_in_map(inputs, c) for c in range(8)], core_ids=list(range(8)))
    st["attn0"] = np.stack([np.asarray(res.results[c]["attn0"]) for c in range(8)])
    st["xnmax8"] = np.concatenate([np.asarray(res.results[c]["xnmax"]) for c in range(8)], axis=1)

    nc = build_l0ffn(1)
    res = run_bass_kernel_spmd(nc, [_ffn_in_map(inputs, c, 1, st) for c in range(8)], core_ids=list(range(8)))
    st["ugmax8"] = np.concatenate([np.asarray(res.results[c]["ugmax"]) for c in range(8)], axis=1)

    nc = build_l0ffn(2)
    res = run_bass_kernel_spmd(nc, [_ffn_in_map(inputs, c, 2, st) for c in range(8)], core_ids=list(range(8)))
    st["g2max8"] = np.concatenate([np.asarray(res.results[c]["g2max"]) for c in range(8)], axis=1)

    nc = build_l0ffn(3)
    res = run_bass_kernel_spmd(nc, [_ffn_in_map(inputs, c, 3, st) for c in range(8)], core_ids=list(range(8)))
    st["omax8"] = np.concatenate([np.asarray(res.results[c]["omax"]) for c in range(8)], axis=1)

    nc = build_l0ffn(4)
    res = run_bass_kernel_spmd(nc, [_ffn_in_map(inputs, c, 4, st) for c in range(8)], core_ids=list(range(8)))
    x1 = np.stack([np.asarray(res.results[c]["x1"]) for c in range(8)])
    return x1


def _l6_in_map(inputs, c, x1):
    consts = _CONSTS
    gn = _gumbel_noise()
    m = {
        "x1": x1[c] if x1 is not None else np.zeros((S, D), np.float32),
        "awq": inputs["L1_attn_Wq"], "awk": inputs["L1_attn_Wk"],
        "awv": inputs["L1_attn_Wv"], "awo": inputs["L1_attn_Wo"],
        "abq": inputs["L1_attn_bq"].reshape(1, D), "abk": inputs["L1_attn_bk"].reshape(1, D),
        "abv": inputs["L1_attn_bv"].reshape(1, D), "abo": inputs["L1_attn_bo"].reshape(1, D),
        "scale1": inputs["L1_ffn_scale1"].reshape(1, D), "scale2": inputs["L1_ffn_scale2"].reshape(1, D),
        "fwq": inputs["L1_ffn_wq"][:, consts["rope_perm"]], "fwk": inputs["L1_ffn_wk"][:, consts["rope_perm"]],
        "fwv": inputs["L1_ffn_wv"], "fwo": inputs["L1_ffn_wo"],
        "router": inputs["L1_ffn_router_w"],
        "w1": inputs["L1_ffn_w1"].reshape(8 * D, D), "w2": inputs["L1_ffn_w2"].reshape(8 * D, D),
        "wout": inputs["L1_ffn_wout"].reshape(8 * D, D),
        "b1": inputs["L1_ffn_b1"], "b2": inputs["L1_ffn_b2"], "bout": inputs["L1_ffn_bout"],
        "gnoise": gn[c], "cos32": consts["cos32"], "sin32": consts["sin32"],
        "identf": consts["ident_bf"], "identf32": consts["ident_f32"], "caus": consts["caus"],
    }
    return {k: np.ascontiguousarray(v) for k, v in m.items()}


def kernel_multi(**inputs):
    inputs = {k: np.asarray(v, dtype=np.float32) for k, v in inputs.items()}
    x1 = run_l0(inputs)
    nc6 = build_l1()
    res = run_bass_kernel_spmd(nc6, [_l6_in_map(inputs, c, x1) for c in range(8)], core_ids=list(range(8)))
    x2 = np.stack([np.asarray(res.results[c]["x2"]) for c in range(8)])
    return x2


def kernel(**inputs):
    return kernel_fused(**inputs)



def build_l0ffn(stage, debug=False):
    e = Emitter(f"ffn{stage}")
    nc = e.nc
    x = nc.dram_tensor("x", [S, D], F32, kind="ExternalInput").ap()
    XNMAX8 = nc.dram_tensor("xnmax8", [1, 8], F32, kind="ExternalInput").ap()
    WUP = nc.dram_tensor("wup", [3072, D], F32, kind="ExternalInput").ap()
    WGATE = nc.dram_tensor("wgate", [3072, D], F32, kind="ExternalInput").ap()
    IDF32 = nc.dram_tensor("identf32", [128, 128], F32, kind="ExternalInput").ap()
    IDENT = nc.dram_tensor("identf", [128, 128], BF16, kind="ExternalInput").ap()
    if stage >= 2:
        UGMAX8 = nc.dram_tensor("ugmax8", [2, 8], F32, kind="ExternalInput").ap()
    if stage >= 3:
        G2MAX8 = nc.dram_tensor("g2max8", [1, 8], F32, kind="ExternalInput").ap()
    if stage >= 4:
        OMAX8 = nc.dram_tensor("omax8", [1, 8], F32, kind="ExternalInput").ap()
        WDOWN = nc.dram_tensor("wdown", [D, 3072], F32, kind="ExternalInput").ap()
        ATTN0 = nc.dram_tensor("attn0", [S, D], F32, kind="ExternalInput").ap()
        X1 = nc.dram_tensor("x1", [S, D], F32, kind="ExternalOutput").ap()
        ODRAM = nc.dram_tensor("od", [3072, S], BF16, kind="Internal").ap()
    if stage == 1:
        UGMAX = nc.dram_tensor("ugmax", [2, 1], F32, kind="ExternalOutput").ap()
    if stage == 2:
        G2MAX = nc.dram_tensor("g2max", [1, 1], F32, kind="ExternalOutput").ap()
    if stage == 3:
        OMAX = nc.dram_tensor("omax", [1, 1], F32, kind="ExternalOutput").ap()
    if debug:
        UPQ_D = nc.dram_tensor("upq_dbg", [3072, S], F32, kind="ExternalOutput").ap()
        G2_D = nc.dram_tensor("g2_dbg", [3072, S], F32, kind="ExternalOutput").ap()

    xa = e.sb("xa", [128, D])
    xqT = e.sb("xqT", [128, NC * S], BF16)
    scr32 = e.sb("scr32", [128, D])
    scr_bf = e.sb("scr_bf", [128, D], BF16)
    w32 = e.sb("w32", [128, D])
    wbf = e.sb("wbf", [128, D], BF16)
    wT = e.sb("wT", [128, NC * 128], BF16)
    bm = e.sb("bm", [128, 32])
    rbm = e.sb("rbm", [128, 32])
    upf = e.sb("upf", [128, S])
    gf = e.sb("gf", [128, S])
    sigf = e.sb("sigf", [128, S])
    obf = e.sb("obf", [128, S], BF16)
    mcol = e.sb("mcol", [128, 1])
    vcol = e.sb("vcol", [128, 1])
    tcol = e.sb("tcol", [128, 1])
    mx_up = e.sb("mx_up", [128, 1])
    mx_g = e.sb("mx_g", [128, 1])
    row8 = e.sb("row8", [2, 8])
    s_sc = e.sb("s_sc", [2, 1])
    scol_xn = e.sb("scol_xn", [128, 1])
    scol_up = e.sb("scol_up", [128, 1])
    scol_g = e.sb("scol_g", [128, 1])
    scol_g2 = e.sb("scol_g2", [128, 1])
    scol_o = e.sb("scol_o", [128, 1])
    onef32 = e.sb("onef32", [1, 128])
    idf32 = e.sb("idf32", [128, 128])
    identb = e.sb("identb", [128, 128], BF16)
    mrow = e.sb("mrow", [1, 128])

    ps_k = [e.psum(f"ps_{i}", [128, 512], F32) for i in range(4)]
    ps_g2 = [e.psum(f"psg_{i}", [128, 512], F32) for i in range(2)]
    ps_t = e.psum("ps_t", [128, 1024], BF16)
    ps_h = e.psum("ps_h", [128, 512], F32)

    T = e.op
    e.dma(idf32[:], IDF32[:])
    e.dma(identb[:], IDENT[:])
    T("vector", lambda: nc.vector.memset(onef32[:], 1.0))

    def bcast_col(src11, dst_col, dep):
        mm = T("tensor", lambda: nc.tensor.matmul(out=ps_h[0:128, 0:1], lhsT=onef32[:], rhs=src11,
                                                  start=True, stop=True),
               after=dep + [("scalar", e.now("scalar")), ("dma", e.now("dma"))])
        return T("scalar", lambda: nc.scalar.copy(out=dst_col[:], in_=ps_h[0:128, 0:1]), after=[("tensor", mm)])

    def scale_from_row(src_row, dst_col):
        d = e.dma(row8[0:1, :], src_row, after=[("vector", e.now("vector")), ("tensor", e.now("tensor"))],
                  queue="gpsimd")
        v1 = T("vector", lambda: nc.vector.tensor_reduce(out=s_sc[0:1, :], in_=row8[0:1, :], axis=AX.X, op=ALU.max),
               after=[("dmag", d), ("tensor", e.now("tensor"))])
        T("vector", lambda: nc.vector.tensor_scalar(out=s_sc[0:1, :], in0=s_sc[0:1, :], scalar1=1.0 / 32767.0,
                                                    scalar2=1e-12, op0=ALU.mult, op1=ALU.max))
        return bcast_col(s_sc[0:1, :], dst_col, [("vector", e.now("vector"))])

    scale_from_row(XNMAX8[0:1, :], scol_xn)
    if stage >= 2:
        scale_from_row(UGMAX8[0:1, :], scol_up)
        scale_from_row(UGMAX8[1:2, :], scol_g)
    if stage >= 3:
        scale_from_row(G2MAX8[0:1, :], scol_g2)
    if stage >= 4:
        scale_from_row(OMAX8[0:1, :], scol_o)

    rcol = e.sb("rcol", [128, 1])

    def qdq_inplace(ap, scol):
        T("vector", lambda: nc.vector.reciprocal(out=rcol[:], in_=scol[:]),
          after=[("scalar", e.now("scalar"))])
        T("vector", lambda: nc.vector.tensor_scalar(out=ap, in0=ap, scalar1=rcol[:], scalar2=None, op0=ALU.mult))
        T("vector", lambda: nc.vector.tensor_scalar(out=ap, in0=ap, scalar1=MAGIC, scalar2=-MAGIC, op0=ALU.add, op1=ALU.add))
        T("vector", lambda: nc.vector.tensor_scalar(out=ap, in0=ap, scalar1=scol[:], scalar2=None, op0=ALU.mult))

    for t in range(NT):
        dx = e.dma(xa[:], x[t * 128:(t + 1) * 128, :], after=[("scalar", e.now("scalar")), ("vector", e.now("vector"))])
        T("vector", lambda: nc.vector.reduce_sum(out=mcol[:], in_=xa[:], axis=AX.X), after=[("dma", dx)])
        sq = T("scalar", lambda: nc.scalar.activation(out=scr32[:], in_=xa[:], func=AF.Square, accum_out=vcol[:]),
               after=[("dma", dx)])
        T("vector", lambda: nc.vector.tensor_scalar(out=mcol[:], in0=mcol[:], scalar1=1.0 / D, scalar2=None, op0=ALU.mult))
        T("vector", lambda: nc.vector.tensor_scalar(out=vcol[:], in0=vcol[:], scalar1=1.0 / D, scalar2=None, op0=ALU.mult),
          after=[("scalar", sq)])
        T("vector", lambda: nc.vector.tensor_scalar(out=tcol[:], in0=mcol[:], scalar1=mcol[:], scalar2=None, op0=ALU.mult))
        T("vector", lambda: nc.vector.tensor_sub(out=vcol[:], in0=vcol[:], in1=tcol[:]))
        vv = T("vector", lambda: nc.vector.tensor_scalar(out=vcol[:], in0=vcol[:], scalar1=1e-5, scalar2=None, op0=ALU.add))
        sv = T("scalar", lambda: nc.scalar.activation(out=vcol[:], in_=vcol[:], func=AF.Sqrt), after=[("vector", vv)])
        T("vector", lambda: nc.vector.reciprocal(out=vcol[:], in_=vcol[:]), after=[("scalar", sv)])
        T("vector", lambda: nc.vector.tensor_scalar(out=xa[:], in0=xa[:], scalar1=mcol[:], scalar2=vcol[:],
                                                    op0=ALU.subtract, op1=ALU.mult))
        qdq_inplace(xa[:], scol_xn)
        cs = T("scalar", lambda: nc.scalar.copy(out=scr_bf[:], in_=xa[:]),
               after=[("vector", e.now("vector")), ("tensor", e.now("tensor"))])
        for ic in range(NC):
            T("tensor", lambda ic=ic: nc.tensor.transpose(
                out=ps_t[:, ic * 128:(ic + 1) * 128], in_=scr_bf[:, ic * 128:(ic + 1) * 128], identity=identb[:]),
              after=[("scalar", cs)] if ic == 0 else [])
        tc = e.now("tensor")
        for ic in range(NC):
            T("scalar", lambda t=t, ic=ic: nc.scalar.copy(
                out=xqT[:, ic * S + t * 128: ic * S + (t + 1) * 128], in_=ps_t[:, ic * 128:(ic + 1) * 128]),
              after=[("tensor", tc)] if ic == 0 else [])
    T("vector", lambda: nc.vector.memset(mx_up[:], 0.0))
    T("vector", lambda: nc.vector.memset(mx_g[:], 0.0))

    def dequant_to_wT(Wsrc, oc, icols=D):
        nblk = icols // 32
        d = e.dma(w32[:, 0:icols], Wsrc[oc * 128:(oc + 1) * 128, :],
                  after=[("vector", e.now("vector")), ("tensor", e.now("tensor"))])
        T("vector", lambda: nc.vector.tensor_reduce(
            out=bm[:, 0:nblk], in_=w32[:, 0:icols].rearrange("p (b k) -> p b k", k=32),
            axis=AX.X, op=ALU.max, apply_absolute_value=True), after=[("dma", d)])
        T("vector", lambda: nc.vector.tensor_scalar(out=bm[:, 0:nblk], in0=bm[:, 0:nblk], scalar1=1.0 / 7.0,
                                                    scalar2=1e-12, op0=ALU.mult, op1=ALU.max))
        w3 = w32[:, 0:icols].rearrange("p (b k) -> p b k", k=32)
        T("vector", lambda: nc.vector.reciprocal(out=rbm[:, 0:nblk], in_=bm[:, 0:nblk]))
        bmap = bm[:, 0:nblk]
        b3 = bass.AP(bmap.tensor, bmap.offset, [list(bmap.ap[0]), [bmap.ap[1][0], nblk], [0, 32]])
        rmap = rbm[:, 0:nblk]
        r3 = bass.AP(rmap.tensor, rmap.offset, [list(rmap.ap[0]), [rmap.ap[1][0], nblk], [0, 32]])
        T("vector", lambda: nc.vector.tensor_tensor(out=w3, in0=w3, in1=r3, op=ALU.mult))
        T("vector", lambda: nc.vector.tensor_scalar(out=w32[:, 0:icols], in0=w32[:, 0:icols], scalar1=MAGIC,
                                                    scalar2=-MAGIC, op0=ALU.add, op1=ALU.add))
        wv = T("vector", lambda: nc.vector.tensor_tensor(out=w3, in0=w3, in1=b3, op=ALU.mult))
        cb = T("scalar", lambda: nc.scalar.copy(out=wbf[:, 0:icols], in_=w32[:, 0:icols]), after=[("vector", wv)])
        nic = icols // 128
        for ic in range(nic):
            T("tensor", lambda ic=ic: nc.tensor.transpose(
                out=ps_t[:, (ic % 8) * 128:(ic % 8 + 1) * 128], in_=wbf[:, ic * 128:(ic + 1) * 128], identity=identb[:]),
              after=[("scalar", cb), ("scalar", e.now("scalar"))] if ic == 0 else [])
        tc = e.now("tensor")
        for ic in range(nic):
            T("scalar", lambda ic=ic: nc.scalar.copy(
                out=wT[:, ic * 128:(ic + 1) * 128], in_=ps_t[:, (ic % 8) * 128:(ic % 8 + 1) * 128]),
              after=[("tensor", tc)] if ic == 0 else [])
        return e.now("scalar")

    for oc in range(24):
        wld = dequant_to_wT(WGATE, oc)
        for nb in range(4):
            ps = ps_g2[nb % 2]
            deps = [("scalar", e.now("scalar")), ("vector", e.now("vector"))]
            for ic in range(NC):
                T("tensor", lambda ic=ic, nb=nb, ps=ps: nc.tensor.matmul(
                    out=ps[:], lhsT=wT[:, ic * 128:(ic + 1) * 128],
                    rhs=xqT[:, ic * S + nb * 512: ic * S + (nb + 1) * 512],
                    start=(ic == 0), stop=(ic == NC - 1)), after=deps if ic == 0 else [])
            mmc = e.now("tensor")
            if stage == 1:
                T("vector", lambda ps=ps: nc.vector.tensor_reduce(out=tcol[:], in_=ps[:], axis=AX.X,
                                                                  op=ALU.max, apply_absolute_value=True),
                  after=[("tensor", mmc)])
                T("vector", lambda: nc.vector.tensor_max(out=mx_g[:], in0=mx_g[:], in1=tcol[:]))
            else:
                T("scalar", lambda nb=nb, ps=ps: nc.scalar.copy(out=gf[:, nb * 512:(nb + 1) * 512], in_=ps[:]),
                  after=[("tensor", mmc)])
        if stage >= 2:
            qdq_inplace(gf[:], scol_g)
            sg = T("scalar", lambda: nc.scalar.activation(out=sigf[:], in_=gf[:], func=AF.Sigmoid),
                   after=[("vector", e.now("vector"))])
            T("vector", lambda: nc.vector.tensor_scalar(out=sigf[:], in0=sigf[:], scalar1=65536.0, scalar2=MAGIC,
                                                        op0=ALU.mult, op1=ALU.add), after=[("scalar", sg)])
            T("vector", lambda: nc.vector.tensor_scalar(out=sigf[:], in0=sigf[:], scalar1=-MAGIC, scalar2=65535.0,
                                                        op0=ALU.add, op1=ALU.min))
            g2v = T("vector", lambda: nc.vector.scalar_tensor_tensor(out=gf[:], in0=sigf[:], scalar=1.0 / 65536.0,
                                                                     in1=gf[:], op0=ALU.mult, op1=ALU.mult))
            if stage == 2:
                T("vector", lambda: nc.vector.tensor_reduce(out=tcol[:], in_=gf[:], axis=AX.X,
                                                            op=ALU.max, apply_absolute_value=True))
                T("vector", lambda: nc.vector.tensor_max(out=mx_g[:], in0=mx_g[:], in1=tcol[:]))
        if stage >= 3:
            qdq_inplace(gf[:], scol_g2)
        if stage == 1 or stage >= 3:
            wld = dequant_to_wT(WUP, oc)
            for nb in range(4):
                ps = ps_k[nb % 2 + 2]
                deps = [("scalar", e.now("scalar")), ("vector", e.now("vector"))]
                for ic in range(NC):
                    T("tensor", lambda ic=ic, nb=nb, ps=ps: nc.tensor.matmul(
                        out=ps[:], lhsT=wT[:, ic * 128:(ic + 1) * 128],
                        rhs=xqT[:, ic * S + nb * 512: ic * S + (nb + 1) * 512],
                        start=(ic == 0), stop=(ic == NC - 1)), after=deps if ic == 0 else [])
                mmc = e.now("tensor")
                if stage == 1:
                    T("vector", lambda ps=ps: nc.vector.tensor_reduce(out=tcol[:], in_=ps[:], axis=AX.X,
                                                                      op=ALU.max, apply_absolute_value=True),
                      after=[("tensor", mmc)])
                    T("vector", lambda: nc.vector.tensor_max(out=mx_up[:], in0=mx_up[:], in1=tcol[:]))
                else:
                    T("scalar", lambda nb=nb, ps=ps: nc.scalar.copy(out=upf[:, nb * 512:(nb + 1) * 512], in_=ps[:]),
                      after=[("tensor", mmc)])
        if stage >= 3:
            qdq_inplace(upf[:], scol_up)
            if debug:
                dmp = e.dma(UPQ_D[oc * 128:(oc + 1) * 128, :], upf[:], after=[("vector", e.now("vector"))])
                T("vector", lambda: nc.vector.tensor_copy(out=tcol[:], in_=tcol[:]), after=[("dma", dmp)])
            ov = T("vector", lambda: nc.vector.tensor_mul(out=gf[:], in0=gf[:], in1=upf[:]))
            if stage == 3:
                T("vector", lambda: nc.vector.tensor_reduce(out=tcol[:], in_=gf[:], axis=AX.X,
                                                            op=ALU.max, apply_absolute_value=True))
                T("vector", lambda: nc.vector.tensor_max(out=mx_g[:], in0=mx_g[:], in1=tcol[:]))
            if debug:
                dmp = e.dma(G2_D[oc * 128:(oc + 1) * 128, :], gf[:], after=[("vector", e.now("vector"))])
                T("vector", lambda: nc.vector.tensor_copy(out=tcol[:], in_=tcol[:]), after=[("dma", dmp)])
        if stage >= 4:
            qdq_inplace(gf[:], scol_o)
            ob = T("vector", lambda: nc.vector.tensor_copy(out=obf[:], in_=gf[:]), after=[("dma", e.now("dma"))])
            e.dma(ODRAM[oc * 128:(oc + 1) * 128, :], obf[:], after=[("vector", ob)])

    def col_to_scalar_max(col, dst11):
        mm = T("tensor", lambda: nc.tensor.matmul(out=ps_h[0:1, 0:128], lhsT=col[:], rhs=idf32[:],
                                                  start=True, stop=True),
               after=[("vector", e.now("vector")), ("scalar", e.now("scalar"))])
        sc = T("scalar", lambda: nc.scalar.copy(out=mrow[:], in_=ps_h[0:1, 0:128]), after=[("tensor", mm)])
        return T("vector", lambda: nc.vector.tensor_reduce(out=dst11, in_=mrow[:], axis=AX.X, op=ALU.max),
                 after=[("scalar", sc)])

    if stage == 1:
        v1 = col_to_scalar_max(mx_up, s_sc[0:1, 0:1])
        e.dma(UGMAX[0:1, :], s_sc[0:1, 0:1], after=[("vector", v1)])
        v2 = col_to_scalar_max(mx_g, mrow[0:1, 0:1])
        e.dma(UGMAX[1:2, :], mrow[0:1, 0:1], after=[("vector", v2)])
    if stage == 2:
        v2 = col_to_scalar_max(mx_g, s_sc[0:1, 0:1])
        e.dma(G2MAX[:], s_sc[0:1, 0:1], after=[("vector", v2)])
    if stage == 3:
        v2 = col_to_scalar_max(mx_g, s_sc[0:1, 0:1])
        e.dma(OMAX[:], s_sc[0:1, 0:1], after=[("vector", v2)])

    if stage >= 4:
        wdT = e.sb("wdT", [128, 24 * 1024], BF16)
        for c in range(NC):
            d = e.dma(w32[:, 0:D] if False else w32[:], WDOWN[c * 128:(c + 1) * 128, 0:1024],
                      after=[("vector", e.now("vector")), ("tensor", e.now("tensor"))])
            for g3 in range(3):
                dg = e.dma(w32[:], WDOWN[c * 128:(c + 1) * 128, g3 * 1024:(g3 + 1) * 1024],
                           after=[("vector", e.now("vector")), ("tensor", e.now("tensor"))]) if g3 > 0 else d
                nblk = 32
                T("vector", lambda: nc.vector.tensor_reduce(
                    out=bm[:, 0:nblk], in_=w32[:].rearrange("p (b k) -> p b k", k=32),
                    axis=AX.X, op=ALU.max, apply_absolute_value=True), after=[("dma", dg)])
                T("vector", lambda: nc.vector.tensor_scalar(out=bm[:, 0:nblk], in0=bm[:, 0:nblk], scalar1=1.0 / 7.0,
                                                            scalar2=1e-12, op0=ALU.mult, op1=ALU.max))
                w3 = w32[:].rearrange("p (b k) -> p b k", k=32)
                T("vector", lambda: nc.vector.reciprocal(out=rbm[:, 0:nblk], in_=bm[:, 0:nblk]))
                bmap = bm[:, 0:nblk]
                b3 = bass.AP(bmap.tensor, bmap.offset, [list(bmap.ap[0]), [bmap.ap[1][0], nblk], [0, 32]])
                rmap = rbm[:, 0:nblk]
                r3 = bass.AP(rmap.tensor, rmap.offset, [list(rmap.ap[0]), [rmap.ap[1][0], nblk], [0, 32]])
                T("vector", lambda w3=w3, r3=r3: nc.vector.tensor_tensor(out=w3, in0=w3, in1=r3, op=ALU.mult))
                T("vector", lambda: nc.vector.tensor_scalar(out=w32[:], in0=w32[:], scalar1=MAGIC,
                                                            scalar2=-MAGIC, op0=ALU.add, op1=ALU.add))
                wv = T("vector", lambda w3=w3, b3=b3: nc.vector.tensor_tensor(out=w3, in0=w3, in1=b3, op=ALU.mult))
                cb = T("scalar", lambda: nc.scalar.copy(out=wbf[:], in_=w32[:]), after=[("vector", wv)])
                for ic in range(8):
                    T("tensor", lambda ic=ic: nc.tensor.transpose(
                        out=ps_t[:, ic * 128:(ic + 1) * 128], in_=wbf[:, ic * 128:(ic + 1) * 128], identity=identb[:]),
                      after=[("scalar", cb), ("scalar", e.now("scalar"))] if ic == 0 else [])
                tc = e.now("tensor")
                for ic in range(8):
                    icg = g3 * 8 + ic
                    T("scalar", lambda icg=icg, ic=ic, c=c: nc.scalar.copy(
                        out=wdT[:, icg * 1024 + c * 128: icg * 1024 + (c + 1) * 128],
                        in_=ps_t[:, ic * 128:(ic + 1) * 128]),
                      after=[("tensor", tc)] if ic == 0 else [])
        wd_done = e.now("scalar")
        oT_t = e.sb("oT_t", [128, 24 * 128], BF16)
        res32 = e.sb("res32", [128, D])
        for t in range(NT):
            dl = []
            for icg in range(24):
                dl.append(e.dma(oT_t[:, icg * 128:(icg + 1) * 128],
                                ODRAM[icg * 128:(icg + 1) * 128, t * 128:(t + 1) * 128],
                                after=[("tensor", e.now("tensor"))] if icg == 0 else []))
            dx = e.dma(res32[:], x[t * 128:(t + 1) * 128, :], after=[("vector", e.now("vector"))])
            dat = e.dma(xa[:], ATTN0[t * 128:(t + 1) * 128, :])
            av = T("vector", lambda: nc.vector.tensor_add(out=res32[:], in0=res32[:], in1=xa[:]),
                   after=[("dma", e.now("dma"))])
            for nb in range(2):
                ps = ps_k[nb]
                deps = [("scalar", wd_done), ("vector", av), ("dma", dl[-1])]
                for icg in range(24):
                    T("tensor", lambda icg=icg, nb=nb, ps=ps: nc.tensor.matmul(
                        out=ps[:], lhsT=oT_t[:, icg * 128:(icg + 1) * 128],
                        rhs=wdT[:, icg * 1024 + nb * 512: icg * 1024 + (nb + 1) * 512],
                        start=(icg == 0), stop=(icg == 23)), after=deps if icg == 0 else [])
                mmc = e.now("tensor")
                fv = T("vector", lambda t=t, nb=nb, ps=ps: nc.vector.tensor_add(
                    out=res32[:, nb * 512:(nb + 1) * 512], in0=res32[:, nb * 512:(nb + 1) * 512], in1=ps[:]),
                    after=[("tensor", mmc)])
                e.dma(X1[t * 128:(t + 1) * 128, nb * 512:(nb + 1) * 512], res32[:, nb * 512:(nb + 1) * 512],
                      after=[("vector", fv)])

    T("gpsimd", lambda: None, after=e.all_now())
    return e.emit_and_close()


def emit_l1(e, fused=False, debug=False):
    nc = e.nc
    e.phase("C")
    X1I = e.dr("x1", [S, D], F32, kind="Internal" if fused else "ExternalInput")
    AWQ = e.dr("awq", [D, D], F32)
    AWK = e.dr("awk", [D, D], F32)
    AWV = e.dr("awv", [D, D], F32)
    AWO = e.dr("awo", [D, D], F32)
    ABQ = e.dr("abq", [1, D], F32)
    ABK = e.dr("abk", [1, D], F32)
    ABV = e.dr("abv", [1, D], F32)
    ABO = e.dr("abo", [1, D], F32)
    SC1 = e.dr("scale1", [1, D], F32)
    SC2 = e.dr("scale2", [1, D], F32)
    FWQ = e.dr("fwq", [D, D], F32)
    FWK = e.dr("fwk", [D, D], F32)
    FWV = e.dr("fwv", [D, D], F32)
    FWO = e.dr("fwo", [D, D], F32)
    ROUT = e.dr("router", [D, 8], F32)
    W1 = e.dr("w1", [8 * D, D], F32)
    W2 = e.dr("w2", [8 * D, D], F32)
    WOUT = e.dr("wout", [8 * D, D], F32)
    B1 = e.dr("b1", [8, D], F32)
    B2 = e.dr("b2", [8, D], F32)
    BOUT = e.dr("bout", [8, D], F32)
    GN = e.dr("gnoise", [S, 8], F32)
    COS32 = e.dr("cos32", [128, S], BF16)
    SIN32 = e.dr("sin32", [128, S], BF16)
    IDENT = e.dr("identf", [128, 128], BF16)
    IDF32 = e.dr("identf32", [128, 128], F32)
    CAUS = e.dr("caus", [128, 4 * 512], BF16)

    X2O = e.dr("x2", [S, D], F32, kind="ExternalOutput")
    XN1 = e.dr("xn1_d", [S, D], F32, kind="Internal")
    A1 = e.dr("a1_d", [S, D], F32, kind="Internal")
    H1 = e.dr("h1_d", [S, D], F32, kind="Internal")
    MOE = e.dr("moe_d", [S, D], F32, kind="Internal")
    if debug:
        A1_D = e.dr("a1_dbg", [S, D], F32, kind="ExternalOutput")
        H1_D = e.dr("h1_dbg", [S, D], F32, kind="ExternalOutput")
        MOE_D = e.dr("moe_dbg", [S, D], F32, kind="ExternalOutput")
        V38_D = e.dr("v38_dbg", [2, 128], F32, kind="ExternalOutput")
        WN_D = e.dr("wn_dbg", [128, S], BF16, kind="ExternalOutput")
        CTX_D = e.dr("ctx_dbg", [D, S], BF16, kind="ExternalOutput")
        PRB_D = e.dr("prb_dbg", [128, 8], F32, kind="ExternalOutput")
        WTS_D = e.dr("wts_dbg", [128, 128], F32, kind="ExternalOutput")

    xa = e.sb("xa", [128, D])
    xb2 = e.sb("xb2", [128, D])
    scr32 = e.sb("scr32", [128, D])
    scr_bf = e.sb("scr_bf", [128, D], BF16)
    xnT = e.sb("xnT", [128, NC * S], BF16)
    qT = e.sb("qT", [128, NC * S], BF16)
    kT = e.sb("kT", [128, NC * S], BF16)
    vtok = e.sb("vtok", [128, NT * D], BF16)
    ctxT = e.sb("ctxT", [128, NC * S], BF16)
    wfull = e.sb("wfull", [128, NC * D], BF16)
    ssb = e.sb("ssb", [128, S])
    sel = e.sb_at("sel", [128, S], F32, xnT)
    wn = e.sb_at("wn", [128, S], BF16, wfull, 0)
    shared = e.sb_at("shared", [128, S], BF16, wfull, 8192)
    causb = e.sb("causb", [128, 4 * 512], BF16)
    ropeA = e.sb_at("ropeA", [128, S], BF16, wfull, 0)
    ropeB = e.sb_at("ropeB", [128, S], BF16, wfull, 4096)
    ropeC = e.sb_at("ropeC", [128, S], BF16, wfull, 8192)
    ropeD = e.sb_at("ropeD", [128, S], BF16, wfull, 12288)
    cosb = e.sb_at("cosb", [128, S], BF16, ctxT, 0)
    sinb = e.sb_at("sinb", [128, S], BF16, ctxT, 4096)
    m8 = e.sb("m8", [128, 8])
    mcol = e.sb("mcol", [128, 1])
    vcol = e.sb("vcol", [128, 1])
    tcol = e.sb("tcol", [128, 1])
    zcol = e.sb("zcol", [128, 1])
    rz = e.sb("rz", [128, 1])
    negm = e.sb("negm", [128, 1])
    bqc = e.sb("bqc", [128, NC])
    bkc = e.sb("bkc", [128, NC])
    s1b = e.sb_at("s1b", [128, D], F32, qT, 0)
    wts_all = e.sb("wts_all", [128, NT * 8])
    prb = e.sb("prb", [128, 8])
    gn_t = e.sb("gn_t", [128, 8])
    rowbuf_bf = e.sb_at("rowbuf_bf", [1, D], BF16, scr_bf, 0)
    rowb2 = e.sb_at("rowb2", [1, D], BF16, xb2, 0)
    boutall = e.sb_at("boutall", [8, D], BF16, xb2, 2048)
    wtsT = e.sb("wtsT", [8, 128], BF16)
    a1s = e.sb_at("a1s", [128, D], BF16, ssb, 0)
    a2s = e.sb_at("a2s", [128, D], BF16, ssb, 2048)
    heT = e.sb_at("heT", [128, D], BF16, ssb, 4096)
    ident = e.sb("ident", [128, 128], BF16)
    idf32 = e.sb("idf32", [128, 128])
    ones1 = e.sb("ones1", [1, 128], BF16)
    onef32 = e.sb("onef32", [1, 128])

    ps_k = [e.psum(f"ps_{i}", [128, 512], F32) for i in range(4)]
    ps_t = e.psum("ps_t", [128, 1024], BF16)
    ps_t2 = e.psum("ps_t2", [128, 1024], BF16)
    ps_g2 = [e.psum(f"psg_{i}", [128, 512], F32) for i in range(2)]

    T = e.op
    e.dma(ident[:], IDENT[:])
    e.dma(idf32[:], IDF32[:])
    ld0 = e.now("dma")
    T("vector", lambda: nc.vector.memset(ones1[:], 1.0))
    T("vector", lambda: nc.vector.memset(onef32[:], 1.0))
    for c in range(NC):
        e.dma(bqc[:, c:c + 1], ABQ[0:1, c * 128:(c + 1) * 128])
        e.dma(bkc[:, c:c + 1], ABK[0:1, c * 128:(c + 1) * 128])
    ld_b = e.now("dma")

    def ln_stream(SRC, DSTDRAM, eps, rms=False, scale_b=None):
        for t in range(NT):
            dx = e.dma(xa[:], SRC[t * 128:(t + 1) * 128, :],
                       after=[("scalar", e.now("scalar")), ("vector", e.now("vector")), ("dma", e.now("dma"))])
            if not rms:
                T("vector", lambda: nc.vector.reduce_sum(out=mcol[:], in_=xa[:], axis=AX.X), after=[("dma", dx)])
            sq = T("scalar", lambda: nc.scalar.activation(out=scr32[:], in_=xa[:], func=AF.Square, accum_out=vcol[:]),
                   after=[("dma", dx)])
            if not rms:
                T("vector", lambda: nc.vector.tensor_scalar(out=mcol[:], in0=mcol[:], scalar1=1.0 / D, scalar2=None, op0=ALU.mult))
            T("vector", lambda: nc.vector.tensor_scalar(out=vcol[:], in0=vcol[:], scalar1=1.0 / D, scalar2=None, op0=ALU.mult),
              after=[("scalar", sq)])
            if not rms:
                T("vector", lambda: nc.vector.tensor_scalar(out=tcol[:], in0=mcol[:], scalar1=mcol[:], scalar2=None, op0=ALU.mult))
                T("vector", lambda: nc.vector.tensor_sub(out=vcol[:], in0=vcol[:], in1=tcol[:]))
            vv = T("vector", lambda: nc.vector.tensor_scalar(out=vcol[:], in0=vcol[:], scalar1=eps, scalar2=None, op0=ALU.add))
            sv = T("scalar", lambda: nc.scalar.activation(out=vcol[:], in_=vcol[:], func=AF.Sqrt), after=[("vector", vv)])
            T("vector", lambda: nc.vector.reciprocal(out=vcol[:], in_=vcol[:]), after=[("scalar", sv)])
            if rms:
                T("vector", lambda: nc.vector.tensor_scalar(out=xa[:], in0=xa[:], scalar1=vcol[:], scalar2=None, op0=ALU.mult))
                T("vector", lambda: nc.vector.tensor_mul(out=xa[:], in0=xa[:], in1=scale_b[:]))
            else:
                T("vector", lambda: nc.vector.tensor_scalar(out=xa[:], in0=xa[:], scalar1=mcol[:], scalar2=vcol[:],
                                                            op0=ALU.subtract, op1=ALU.mult))
            lv = e.now("vector")
            if DSTDRAM is not None:
                e.dma(DSTDRAM[t * 128:(t + 1) * 128, :], xa[:], after=[("vector", lv)])
            cs = T("scalar", lambda: nc.scalar.copy(out=scr_bf[:], in_=xa[:]),
                   after=[("vector", lv), ("tensor", e.now("tensor"))])
            for ic in range(NC):
                T("tensor", lambda ic=ic: nc.tensor.transpose(
                    out=ps_t[:, ic * 128:(ic + 1) * 128], in_=scr_bf[:, ic * 128:(ic + 1) * 128], identity=ident[:]),
                  after=[("scalar", cs)] if ic == 0 else [])
            tc = e.now("tensor")
            for ic in range(NC):
                T("scalar", lambda t=t, ic=ic: nc.scalar.copy(
                    out=xnT[:, ic * S + t * 128: ic * S + (t + 1) * 128], in_=ps_t[:, ic * 128:(ic + 1) * 128]),
                  after=[("tensor", tc)] if ic == 0 else [])

    ln_stream(X1I, XN1, 1e-5, rms=False)

    def load_w_full(W):
        for r in range(NC):
            e.dma(wfull[:, r * D:(r + 1) * D], W[r * 128:(r + 1) * 128, :],
                  after=[("tensor", e.now("tensor"))] if r == 0 else [], queue="gpsimd")
        return e.now("dmag")

    def transpose_w_to(WDST):
        dep = [("dmag", e.now("dmag"))]
        for oc in range(NC):
            for ic in range(NC):
                T("tensor", lambda oc=oc, ic=ic: nc.tensor.transpose(
                    out=ps_t[:, (ic % 8) * 128:(ic % 8 + 1) * 128],
                    in_=wfull[:, oc * D + ic * 128: oc * D + (ic + 1) * 128], identity=ident[:]),
                  after=dep + [("scalar", e.now("scalar"))] if ic == 0 else [])
            tc = e.now("tensor")
            for ic in range(NC):
                T("scalar", lambda oc=oc, ic=ic: nc.scalar.copy(
                    out=WDST[:, ic * D + oc * 128: ic * D + (oc + 1) * 128],
                    in_=ps_t[:, (ic % 8) * 128:(ic % 8 + 1) * 128]),
                  after=[("tensor", tc)] if ic == 0 else [])
        return e.now("scalar")

    def proj_from_pieces(PIECES, outT, bias_cols, ldep):
        for oc in range(NC):
            for nb in range(S // 512):
                ps = ps_k[nb % 4]
                deps = [("scalar", e.now("scalar")), ("vector", e.now("vector"))] + ldep
                for ic in range(NC):
                    T("tensor", lambda ic=ic, oc=oc, nb=nb, ps=ps: nc.tensor.matmul(
                        out=ps[:], lhsT=PIECES[:, ic * D + oc * 128: ic * D + (oc + 1) * 128],
                        rhs=xnT[:, ic * S + nb * 512: ic * S + (nb + 1) * 512],
                        start=(ic == 0), stop=(ic == NC - 1)), after=deps if ic == 0 else [])
                mmc = e.now("tensor")
                if bias_cols is not None:
                    T("scalar", lambda oc=oc, nb=nb, ps=ps, outT=outT: nc.scalar.activation(
                        out=outT[:, oc * S + nb * 512: oc * S + (nb + 1) * 512], in_=ps[:],
                        func=AF.Identity, bias=bias_cols[:, oc:oc + 1], scale=1.0), after=[("tensor", mmc)])
                else:
                    T("scalar", lambda oc=oc, nb=nb, ps=ps, outT=outT: nc.scalar.copy(
                        out=outT[:, oc * S + nb * 512: oc * S + (nb + 1) * 512], in_=ps[:]), after=[("tensor", mmc)])

    def proj_tokmajor(RHS_PIECES, dst_tok, brow_bf, ldep):
        for t in range(NT):
            for nb in range(2):
                ps = ps_k[(2 * t + nb) % 4]
                deps = [("scalar", e.now("scalar")), ("vector", e.now("vector"))] + ldep
                last = NC - 1 if brow_bf is None else -1
                for ic in range(NC):
                    T("tensor", lambda t=t, ic=ic, nb=nb, ps=ps, last=last: nc.tensor.matmul(
                        out=ps[:], lhsT=xnT[:, ic * S + t * 128: ic * S + (t + 1) * 128],
                        rhs=RHS_PIECES[:, ic * D + nb * 512: ic * D + (nb + 1) * 512],
                        start=(ic == 0), stop=(ic == last)), after=deps if ic == 0 else [])
                if brow_bf is not None:
                    T("tensor", lambda nb=nb, ps=ps, brow_bf=brow_bf: nc.tensor.matmul(
                        out=ps[:], lhsT=ones1[:], rhs=brow_bf[0:1, nb * 512:(nb + 1) * 512], start=False, stop=True))
                mmc = e.now("tensor")
                T("scalar", lambda t=t, nb=nb, ps=ps, dst_tok=dst_tok: nc.scalar.copy(
                    out=dst_tok[:, t * D + nb * 512: t * D + (nb + 1) * 512], in_=ps[:]), after=[("tensor", mmc)])

    bqs = e.sb("bqs", [128, NC])
    T("vector", lambda: nc.vector.tensor_scalar(out=bqs[:], in0=bqc[:], scalar1=0.125, scalar2=None, op0=ALU.mult),
      after=[("dma", ld_b)])
    load_w_full(AWQ)
    wdep = transpose_w_to(ctxT)
    for oc in range(NC):
        pass
    for oc in range(NC):
        for nb in range(S // 512):
            ps = ps_k[nb % 4]
            deps = [("scalar", e.now("scalar")), ("vector", e.now("vector"))]
            for ic in range(NC):
                T("tensor", lambda ic=ic, oc=oc, nb=nb, ps=ps: nc.tensor.matmul(
                    out=ps[:], lhsT=ctxT[:, ic * D + oc * 128: ic * D + (oc + 1) * 128],
                    rhs=xnT[:, ic * S + nb * 512: ic * S + (nb + 1) * 512],
                    start=(ic == 0), stop=(ic == NC - 1)), after=deps if ic == 0 else [])
            mmc = e.now("tensor")
            T("scalar", lambda oc=oc, nb=nb, ps=ps: nc.scalar.activation(
                out=qT[:, oc * S + nb * 512: oc * S + (nb + 1) * 512], in_=ps[:],
                func=AF.Identity, bias=bqs[:, oc:oc + 1], scale=0.125), after=[("tensor", mmc)])
    load_w_full(AWK)
    transpose_w_to(ctxT)
    proj_from_pieces(ctxT, kT, bkc, [])
    dbv = e.dma(rowbuf_bf[:], ABV[:], queue="gpsimd")
    load_w_full(AWV)
    transpose_w_to(ctxT)
    proj_tokmajor(ctxT, vtok, rowbuf_bf, [])

    U = 38
    units = [(h, qt) for h in range(H) for qt in range(NT)]
    wnB = [wn, e.sb_at("wn2", [128, S], BF16, wfull, 4096)]
    shB = [shared, e.sb_at("sh2", [128, S], BF16, wfull, 12288)]
    wtB = [causb, e.sb_at("wt2", [128, S], BF16, xb2, 0)]

    def l1a_qk(h, qt):
        c2, pb = h // 2, 64 * (h % 2)
        for kb in range(4):
            T("tensor", lambda c2=c2, pb=pb, qt=qt, kb=kb: nc.tensor.matmul(
                out=ps_k[kb][:], lhsT=qT[pb:pb + 64, c2 * S + qt * 128: c2 * S + (qt + 1) * 128],
                rhs=kT[pb:pb + 64, c2 * S + kb * 512: c2 * S + (kb + 1) * 512], start=True, stop=True),
              after=[("scalar", e.now("scalar")), ("vector", e.now("vector"))] if kb == 0 else [])
        return e.now("tensor")

    def l1a_soft(h, qt, mmc, par):
        wn = wnB[par]
        shared = shB[par]
        for kb in range(4):
            T("scalar", lambda kb=kb: nc.scalar.copy(out=ssb[:, kb * 512:(kb + 1) * 512], in_=ps_k[kb][:]),
              after=[("tensor", mmc), ("vector", e.now("vector"))] if kb == 0 else [])
        sevac = e.now("scalar")
        T("vector", lambda: nc.vector.tensor_copy(out=sel[:], in_=ssb[:]), after=[("scalar", sevac)])
        for rnd in range(5):
            T("vector", lambda: nc.vector.max(out=m8[:], in_=sel[:]))
            if rnd == 0:
                T("vector", lambda: nc.vector.tensor_copy(out=mcol[:], in_=m8[:, 0:1]))
            if rnd < 4:
                T("vector", lambda: nc.vector.match_replace(out=sel[:], in_to_replace=m8[:], in_values=sel[:],
                                                            imm_value=-1e30))
        T("vector", lambda: nc.vector.tensor_copy(out=vcol[:], in_=m8[:, 5:6]))
        T("vector", lambda: nc.vector.tensor_single_scalar(out=wn[:], in_=ssb[:], scalar=vcol[:], op=ALU.is_ge))
        T("vector", lambda: nc.vector.tensor_scalar(out=negm[:], in0=mcol[:], scalar1=-1.0, scalar2=None, op0=ALU.mult))
        ex = T("scalar", lambda: nc.scalar.activation(out=shared[:], in_=ssb[:], func=AF.Exp, bias=negm[:]),
               after=[("vector", e.now("vector"))])
        T("vector", lambda: nc.vector.tensor_mul(out=wn[:], in0=wn[:], in1=shared[:]), after=[("scalar", ex)])
        T("vector", lambda: nc.vector.reduce_sum(out=zcol[:], in_=wn[:], axis=AX.X))
        T("vector", lambda: nc.vector.reciprocal(out=rz[:], in_=zcol[:]))
        nw = T("vector", lambda: nc.vector.tensor_scalar(out=wn[:], in0=wn[:], scalar1=rz[:], scalar2=None, op0=ALU.mult))
        return nw

    def l1a_pv(h, qt, nw, par):
        wn = wnB[par]
        shared = wtB[par]
        c2, pb = h // 2, 64 * (h % 2)
        for c in range(16):
            pst = ps_t if c < 8 else ps_t2
            T("tensor", lambda c=c, pst=pst: nc.tensor.transpose(
                out=pst[:, (c % 8) * 128:(c % 8 + 1) * 128], in_=wn[:, c * 128:(c + 1) * 128], identity=ident[:]),
              after=[("vector", nw), ("scalar", e.now("scalar"))] if c == 0 else [])
        tcn = e.now("tensor")
        T("scalar", lambda: nc.scalar.copy(out=shared[:, 0:1024], in_=ps_t[:]), after=[("tensor", tcn)])
        ev2 = T("scalar", lambda: nc.scalar.copy(out=shared[:, 1024:2048], in_=ps_t2[:]))
        for c in range(16):
            T("tensor", lambda c=c, h=h: nc.tensor.matmul(
                out=ps_g2[0][0:64, 0:128], lhsT=vtok[:, c * D + h * 64: c * D + h * 64 + 64],
                rhs=shared[:, c * 128:(c + 1) * 128], start=(c == 0), stop=(c == 15)),
              after=[("scalar", ev2)] if c == 0 else [])
        pvc = e.now("tensor")
        T("scalar", lambda c2=c2, pb=pb, qt=qt: nc.scalar.copy(
            out=ctxT[pb:pb + 64, c2 * S + qt * 128: c2 * S + (qt + 1) * 128], in_=ps_g2[0][0:64, 0:128]),
          after=[("tensor", pvc)])

    mmc = l1a_qk(*units[0])
    pend = None
    for i, (h, qt) in enumerate(units):
        nw = l1a_soft(h, qt, mmc, i % 2)
        if i + 1 < len(units):
            mmc = l1a_qk(*units[i + 1])
        if pend is not None:
            l1a_pv(*pend)
        pend = (h, qt, nw, i % 2)
    l1a_pv(*pend)

    dbo = e.dma(rowbuf_bf[:], ABO[:], after=[("tensor", e.now("tensor"))], queue="gpsimd")
    load_w_full(AWO)
    wot = transpose_w_to(qT)
    for t in range(NT):
        for nb in range(2):
            ps = ps_k[(2 * t + nb) % 4]
            deps = [("scalar", e.now("scalar")), ("vector", e.now("vector")), ("dmag", e.now("dmag"))]
            for ic in range(NC):
                T("tensor", lambda t=t, ic=ic, nb=nb, ps=ps: nc.tensor.matmul(
                    out=ps[:], lhsT=ctxT[:, ic * S + t * 128: ic * S + (t + 1) * 128],
                    rhs=qT[:, ic * D + nb * 512: ic * D + (nb + 1) * 512],
                    start=(ic == 0), stop=False), after=deps if ic == 0 else [])
            T("tensor", lambda nb=nb, ps=ps: nc.tensor.matmul(
                out=ps[:], lhsT=ones1[:], rhs=rowbuf_bf[0:1, nb * 512:(nb + 1) * 512], start=False, stop=True))
            mmc = e.now("tensor")
            sv = T("scalar", lambda nb=nb, ps=ps: nc.scalar.copy(out=scr32[:, nb * 512:(nb + 1) * 512], in_=ps[:]),
                   after=[("tensor", mmc), ("dma", e.now("dma"))])
        e.dma(A1[t * 128:(t + 1) * 128, :], scr32[:], after=[("scalar", e.now("scalar"))])
        if debug:
            e.dma(A1_D[t * 128:(t + 1) * 128, :], scr32[:], after=[("scalar", e.now("scalar"))])

    ds1 = e.dma(scr32[0:1, :], SC1[:], after=[("scalar", e.now("scalar")), ("vector", e.now("vector")),
                                              ("dma", e.now("dma"))])
    cs1 = T("vector", lambda: nc.vector.tensor_copy(out=rowb2[:], in_=scr32[0:1, :]), after=[("dma", ds1)])
    for nb in range(2):
        mm = T("tensor", lambda nb=nb: nc.tensor.matmul(
            out=ps_g2[0][:, 0:512], lhsT=ones1[:], rhs=rowb2[0:1, nb * 512:(nb + 1) * 512], start=True, stop=True),
            after=[("vector", cs1), ("scalar", e.now("scalar"))])
        T("scalar", lambda nb=nb: nc.scalar.copy(out=s1b[:, nb * 512:(nb + 1) * 512], in_=ps_g2[0][:, 0:512]),
          after=[("tensor", mm)])
    sb_done = e.now("scalar")
    T("vector", lambda: nc.vector.tensor_copy(out=tcol[:], in_=tcol[:]), after=[("scalar", sb_done)])
    ln_stream(XN1, None, 1e-6, rms=True, scale_b=s1b)

    def rope_chunks(tT):
        for c in range(4):
            x1 = tT[:, (2 * c) * S:(2 * c + 1) * S]
            x2 = tT[:, (2 * c + 1) * S:(2 * c + 2) * S]
            T("vector", lambda x1=x1: nc.vector.tensor_mul(out=ropeA[:], in0=x1, in1=cosb[:]),
              after=[("scalar", e.now("scalar"))])
            T("vector", lambda x1=x1: nc.vector.tensor_mul(out=ropeB[:], in0=x1, in1=sinb[:]))
            T("vector", lambda x2=x2: nc.vector.tensor_mul(out=ropeC[:], in0=x2, in1=sinb[:]))
            T("vector", lambda x2=x2: nc.vector.tensor_mul(out=ropeD[:], in0=x2, in1=cosb[:]))
            T("vector", lambda x1=x1: nc.vector.tensor_sub(out=x1, in0=ropeA[:], in1=ropeC[:]))
            T("vector", lambda x2=x2: nc.vector.tensor_add(out=x2, in0=ropeB[:], in1=ropeD[:]))

    ldq = load_w_full(FWQ)
    for oc in range(NC):
        for nb in range(S // 512):
            ps = ps_k[nb % 4]
            deps = [("scalar", e.now("scalar")), ("vector", e.now("vector")), ("dmag", ldq)]
            for ic in range(NC):
                T("tensor", lambda ic=ic, oc=oc, nb=nb, ps=ps: nc.tensor.matmul(
                    out=ps[:], lhsT=wfull[:, ic * D + oc * 128: ic * D + (oc + 1) * 128],
                    rhs=xnT[:, ic * S + nb * 512: ic * S + (nb + 1) * 512],
                    start=(ic == 0), stop=(ic == NC - 1)), after=deps if ic == 0 else [])
            mmc = e.now("tensor")
            T("scalar", lambda oc=oc, nb=nb, ps=ps: nc.scalar.activation(
                out=qT[:, oc * S + nb * 512: oc * S + (nb + 1) * 512], in_=ps[:],
                func=AF.Identity, scale=0.125), after=[("tensor", mmc)])
    ldk = load_w_full(FWK)
    for oc in range(NC):
        for nb in range(S // 512):
            ps = ps_k[nb % 4]
            deps = [("scalar", e.now("scalar")), ("vector", e.now("vector")), ("dmag", ldk)]
            for ic in range(NC):
                T("tensor", lambda ic=ic, oc=oc, nb=nb, ps=ps: nc.tensor.matmul(
                    out=ps[:], lhsT=wfull[:, ic * D + oc * 128: ic * D + (oc + 1) * 128],
                    rhs=xnT[:, ic * S + nb * 512: ic * S + (nb + 1) * 512],
                    start=(ic == 0), stop=(ic == NC - 1)), after=deps if ic == 0 else [])
            mmc = e.now("tensor")
            T("scalar", lambda oc=oc, nb=nb, ps=ps: nc.scalar.copy(
                out=kT[:, oc * S + nb * 512: oc * S + (nb + 1) * 512], in_=ps[:]), after=[("tensor", mmc)])
    ldv = load_w_full(FWV)
    proj_tokmajor(wfull, vtok, None, [("dmag", ldv)])
    dcs = e.dma(cosb[:], COS32[:], after=[("tensor", e.now("tensor")), ("scalar", e.now("scalar"))])
    dsn = e.dma(sinb[:], SIN32[:])
    T("vector", lambda: nc.vector.tensor_copy(out=tcol[:], in_=tcol[:]), after=[("dma", dsn)])
    rope_chunks(qT)
    rope_chunks(kT)
    dca = e.dma(causb[:], CAUS[:], after=[("tensor", e.now("tensor")), ("scalar", e.now("scalar")),
                                          ("vector", e.now("vector"))])
    T("vector", lambda: nc.vector.tensor_copy(out=tcol[:], in_=tcol[:]), after=[("dma", dca)])

    for h in range(H):
        c2 = h // 2
        pb = 64 * (h % 2)
        for qt in range(NT):
            nkc = qt + 1
            nkb = (nkc * 128 + 511) // 512
            cc = h // 4
            aa = h % 4
            rb = 32 * aa
            for kb in range(nkb):
                for xh in range(2):
                    cblk = (2 * cc + xh) * S
                    T("tensor", lambda qt=qt, kb=kb, rb=rb, cblk=cblk, xh=xh: nc.tensor.matmul(
                        out=ps_k[kb][:], lhsT=qT[rb:rb + 32, cblk + qt * 128: cblk + (qt + 1) * 128],
                        rhs=kT[rb:rb + 32, cblk + kb * 512: cblk + (kb + 1) * 512],
                        start=(xh == 0), stop=(xh == 1), tile_position=(rb, 0)),
                      after=[("scalar", e.now("scalar")), ("vector", e.now("vector"))] if (kb == 0 and xh == 0) else [])
            mmc = e.now("tensor")
            for kb in range(nkb):
                T("scalar", lambda kb=kb: nc.scalar.copy(out=ssb[:, kb * 512:(kb + 1) * 512], in_=ps_k[kb][:]),
                  after=[("tensor", mmc)] if kb == 0 else [])
            sevac = e.now("scalar")
            dkb = (qt * 128) // 512
            dofs = qt * 128 - dkb * 512
            T("vector", lambda dkb=dkb, dofs=dofs: nc.vector.tensor_add(
                out=ssb[:, dkb * 512:(dkb + 1) * 512], in0=ssb[:, dkb * 512:(dkb + 1) * 512],
                in1=causb[:, (dofs // 128) * 512:(dofs // 128 + 1) * 512]),
              after=[("scalar", sevac)])
            nv = nkc * 128
            T("vector", lambda nv=nv: nc.vector.tensor_reduce(out=mcol[:], in_=ssb[:, 0:nv], axis=AX.X, op=ALU.max))
            T("vector", lambda: nc.vector.tensor_scalar(out=negm[:], in0=mcol[:], scalar1=-1.0, scalar2=None, op0=ALU.mult))
            ex = T("scalar", lambda nv=nv: nc.scalar.activation(out=wn[:, 0:nv], in_=ssb[:, 0:nv], func=AF.Exp,
                                                                bias=negm[:], accum_out=zcol[:]),
                   after=[("vector", e.now("vector"))])
            T("vector", lambda: nc.vector.reciprocal(out=rz[:], in_=zcol[:]), after=[("scalar", ex)])
            nw = T("vector", lambda nv=nv: nc.vector.tensor_scalar(out=wn[:, 0:nv], in0=wn[:, 0:nv], scalar1=rz[:],
                                                                   scalar2=None, op0=ALU.mult))
            for c in range(nkc):
                pst = ps_t if (c % 16) < 8 else ps_t2
                T("tensor", lambda c=c, pst=pst: nc.tensor.transpose(
                    out=pst[:, (c % 8) * 128:(c % 8 + 1) * 128], in_=wn[:, c * 128:(c + 1) * 128], identity=ident[:]),
                  after=[("vector", nw), ("scalar", e.now("scalar"))] if c == 0 else [])
            tcn = e.now("tensor")
            T("scalar", lambda w=min(nkc, 8): nc.scalar.copy(out=shared[:, 0:w * 128], in_=ps_t[:, 0:w * 128]),
              after=[("tensor", tcn)])
            if nkc > 8:
                T("scalar", lambda w=nkc - 8: nc.scalar.copy(out=shared[:, 1024:1024 + w * 128], in_=ps_t2[:, 0:w * 128]))
            ev2 = e.now("scalar")
            for c in range(nkc):
                T("tensor", lambda c=c, h=h: nc.tensor.matmul(
                    out=ps_g2[0][0:64, 0:128], lhsT=vtok[:, c * D + h * 64: c * D + h * 64 + 64],
                    rhs=shared[:, c * 128:(c + 1) * 128], start=(c == 0), stop=(c == nkc - 1)),
                  after=[("scalar", ev2)] if c == 0 else [])
            pvc = e.now("tensor")
            T("scalar", lambda c2=c2, pb=pb, qt=qt: nc.scalar.copy(
                out=ctxT[pb:pb + 64, c2 * S + qt * 128: c2 * S + (qt + 1) * 128], in_=ps_g2[0][0:64, 0:128]),
              after=[("tensor", pvc)])

    ldo = load_w_full(FWO)
    for t in range(NT):
        dxn = e.dma(xb2[:], XN1[t * 128:(t + 1) * 128, :], after=[("vector", e.now("vector"))])
        for nb in range(2):
            ps = ps_k[(2 * t + nb) % 4]
            deps = [("scalar", e.now("scalar")), ("vector", e.now("vector")), ("dmag", ldo), ("dma", dxn)]
            for ic in range(NC):
                T("tensor", lambda t=t, ic=ic, nb=nb, ps=ps: nc.tensor.matmul(
                    out=ps[:], lhsT=ctxT[:, ic * S + t * 128: ic * S + (t + 1) * 128],
                    rhs=wfull[:, ic * D + nb * 512: ic * D + (nb + 1) * 512],
                    start=(ic == 0), stop=(ic == NC - 1)), after=deps if ic == 0 else [])
            mmc = e.now("tensor")
            T("vector", lambda t=t, nb=nb, ps=ps: nc.vector.tensor_add(
                out=scr32[:, nb * 512:(nb + 1) * 512], in0=xb2[:, nb * 512:(nb + 1) * 512], in1=ps[:]),
              after=[("tensor", mmc), ("dma", e.now("dma"))])
        hv = e.now("vector")
        e.dma(H1[t * 128:(t + 1) * 128, :], scr32[:], after=[("vector", hv)])
        if debug:
            e.dma(H1_D[t * 128:(t + 1) * 128, :], scr32[:], after=[("vector", hv)])

    ds2 = e.dma(scr32[0:1, :], SC2[:], after=[("scalar", e.now("scalar")), ("vector", e.now("vector")),
                                              ("dma", e.now("dma"))])
    cs2 = T("vector", lambda: nc.vector.tensor_copy(out=rowb2[:], in_=scr32[0:1, :]), after=[("dma", ds2)])
    for nb in range(2):
        mm = T("tensor", lambda nb=nb: nc.tensor.matmul(
            out=ps_g2[0][:, 0:512], lhsT=ones1[:], rhs=rowb2[0:1, nb * 512:(nb + 1) * 512], start=True, stop=True),
            after=[("vector", cs2), ("scalar", e.now("scalar"))])
        T("scalar", lambda nb=nb: nc.scalar.copy(out=s1b[:, nb * 512:(nb + 1) * 512], in_=ps_g2[0][:, 0:512]),
          after=[("tensor", mm)])
    T("vector", lambda: nc.vector.tensor_copy(out=tcol[:], in_=tcol[:]), after=[("scalar", e.now("scalar"))])
    ln_stream(H1, None, 1e-6, rms=True, scale_b=s1b)

    for ic in range(NC):
        e.dma(wfull[:, ic * 8:(ic + 1) * 8], ROUT[ic * 128:(ic + 1) * 128, :],
              after=[("tensor", e.now("tensor"))] if ic == 0 else [], queue="gpsimd")
    ldr = e.now("dmag")
    for t in range(NT):
        dgn = e.dma(gn_t[:], GN[t * 128:(t + 1) * 128, :], after=[("vector", e.now("vector"))])
        deps = [("scalar", e.now("scalar")), ("vector", e.now("vector")), ("dmag", ldr)]
        for ic in range(NC):
            T("tensor", lambda t=t, ic=ic: nc.tensor.matmul(
                out=ps_g2[1][0:128, 0:8], lhsT=xnT[:, ic * S + t * 128: ic * S + (t + 1) * 128],
                rhs=wfull[:, ic * 8:(ic + 1) * 8], start=(ic == 0), stop=(ic == NC - 1)),
              after=deps if ic == 0 else [])
        mmc = e.now("tensor")
        T("vector", lambda: nc.vector.tensor_add(out=prb[:], in0=ps_g2[1][0:128, 0:8], in1=gn_t[:]),
          after=[("tensor", mmc), ("dma", dgn)])
        T("vector", lambda: nc.vector.tensor_reduce(out=mcol[:], in_=prb[:], axis=AX.X, op=ALU.max))
        T("vector", lambda: nc.vector.tensor_scalar(out=negm[:], in0=mcol[:], scalar1=-1.0, scalar2=None, op0=ALU.mult))
        exr = T("scalar", lambda: nc.scalar.activation(out=prb[:], in_=prb[:], func=AF.Exp, bias=negm[:],
                                                       accum_out=zcol[:]),
                after=[("vector", e.now("vector"))])
        T("vector", lambda: nc.vector.reciprocal(out=rz[:], in_=zcol[:]), after=[("scalar", exr)])
        T("vector", lambda: nc.vector.tensor_scalar(out=prb[:], in0=prb[:], scalar1=rz[:], scalar2=None, op0=ALU.mult))
        T("vector", lambda: nc.vector.max(out=m8[:], in_=prb[:]))
        T("vector", lambda: nc.vector.tensor_copy(out=vcol[:], in_=m8[:, 1:2]))
        T("vector", lambda: nc.vector.tensor_single_scalar(out=m8[:], in_=prb[:], scalar=vcol[:], op=ALU.is_ge))
        T("vector", lambda t=t: nc.vector.tensor_tensor(out=wts_all[:, t * 8:(t + 1) * 8], in0=prb[:],
                                                        in1=m8[:], op=ALU.mult))
        if debug and t == 0:
            e.dma(PRB_D[:], prb[:], after=[("vector", e.now("vector"))])
    if debug:
        e.dma(WTS_D[:], wts_all[:], after=[("vector", e.now("vector"))])

    e.dma(boutall[:], BOUT[:], after=[("tensor", e.now("tensor"))], queue="gpsimd")
    bout_ld = e.now("dmag")

    for ex in range(8):
        for r in range(NC):
            e.dma(qT[:, r * D:(r + 1) * D], W1[ex * D + r * 128: ex * D + (r + 1) * 128, :],
                  after=[("tensor", e.now("tensor"))] if r == 0 else [], queue="gpsimd")
            e.dma(kT[:, r * D:(r + 1) * D], W2[ex * D + r * 128: ex * D + (r + 1) * 128, :], queue="gpsimd")
            e.dma(vtok[:, r * D:(r + 1) * D], WOUT[ex * D + r * 128: ex * D + (r + 1) * 128, :], queue="gpsimd")
        e.dma(rowbuf_bf[:], B1[ex:ex + 1, :], queue="gpsimd")
        e.dma(rowb2[:], B2[ex:ex + 1, :], queue="gpsimd")
        wld = e.now("dmag")
        for t in range(NT):
            for nb in range(2):
                deps = [("scalar", e.now("scalar")), ("vector", e.now("vector")), ("dmag", wld)]
                for ic in range(NC):
                    T("tensor", lambda t=t, ic=ic, nb=nb: nc.tensor.matmul(
                        out=ps_k[nb][:], lhsT=xnT[:, ic * S + t * 128: ic * S + (t + 1) * 128],
                        rhs=qT[:, ic * D + nb * 512: ic * D + (nb + 1) * 512],
                        start=(ic == 0), stop=False), after=deps if ic == 0 else [])
                T("tensor", lambda nb=nb: nc.tensor.matmul(
                    out=ps_k[nb][:], lhsT=ones1[:], rhs=rowbuf_bf[0:1, nb * 512:(nb + 1) * 512],
                    start=False, stop=True))
                for ic in range(NC):
                    T("tensor", lambda t=t, ic=ic, nb=nb: nc.tensor.matmul(
                        out=ps_k[2 + nb][:], lhsT=xnT[:, ic * S + t * 128: ic * S + (t + 1) * 128],
                        rhs=kT[:, ic * D + nb * 512: ic * D + (nb + 1) * 512],
                        start=(ic == 0), stop=False))
                T("tensor", lambda nb=nb: nc.tensor.matmul(
                    out=ps_k[2 + nb][:], lhsT=ones1[:], rhs=rowb2[0:1, nb * 512:(nb + 1) * 512],
                    start=False, stop=True))
            mmc = e.now("tensor")
            for nb in range(2):
                T("scalar", lambda nb=nb: nc.scalar.activation(out=a1s[:, nb * 512:(nb + 1) * 512],
                                                               in_=ps_k[nb][:], func=AF.Silu),
                  after=[("tensor", mmc)] if nb == 0 else [])
                T("scalar", lambda nb=nb: nc.scalar.copy(out=a2s[:, nb * 512:(nb + 1) * 512], in_=ps_k[2 + nb][:]))
            aev = e.now("scalar")
            hv = T("vector", lambda t=t, ex=ex: nc.vector.scalar_tensor_tensor(
                out=a1s[:], in0=a1s[:], scalar=wts_all[:, t * 8 + ex: t * 8 + ex + 1], in1=a2s[:],
                op0=ALU.mult, op1=ALU.mult), after=[("scalar", aev)])
            for ic in range(NC):
                T("tensor", lambda ic=ic: nc.tensor.transpose(
                    out=ps_t[:, ic * 128:(ic + 1) * 128], in_=a1s[:, ic * 128:(ic + 1) * 128], identity=ident[:]),
                  after=[("vector", hv), ("scalar", e.now("scalar"))] if ic == 0 else [])
            tcn = e.now("tensor")
            T("scalar", lambda: nc.scalar.copy(out=heT[:], in_=ps_t[:, 0:1024]), after=[("tensor", tcn)])
            hev = e.now("scalar")
            for nb in range(2):
                deps2 = [("scalar", hev)]
                last = NC - 1 if ex != 7 else -1
                for ic in range(NC):
                    T("tensor", lambda ic=ic, nb=nb, last=last: nc.tensor.matmul(
                        out=ps_g2[nb][:], lhsT=heT[:, ic * 128:(ic + 1) * 128],
                        rhs=vtok[:, ic * D + nb * 512: ic * D + (nb + 1) * 512],
                        start=(ic == 0), stop=(ic == last)), after=deps2 if ic == 0 else [])
                if ex == 7:
                    T("tensor", lambda t=t, nb=nb: nc.tensor.matmul(
                        out=ps_g2[nb][:], lhsT=wtsT[:], rhs=boutall[:, nb * 512:(nb + 1) * 512],
                        start=False, stop=True))
            mmc = e.now("tensor")
            sv = T("scalar", lambda: nc.scalar.copy(out=scr32[:, 0:512], in_=ps_g2[0][:]),
                   after=[("tensor", mmc), ("dmag", e.now("dmag"))])
            sv2 = T("scalar", lambda: nc.scalar.copy(out=scr32[:, 512:1024], in_=ps_g2[1][:]))
            op = ALU.bypass if ex == 0 else ALU.add
            md = e.op("gpsimd", lambda t=t, op=op: nc.gpsimd.dma_start(
                out=MOE[t * 128:(t + 1) * 128, :], in_=scr32[:], accum_op=op), after=[("scalar", sv2)], inc="dmag")
            if ex == 6:
                wm = T("tensor", lambda t=t: nc.tensor.matmul(
                    out=ps_g2[0][0:8, 0:128], lhsT=wts_all[:, t * 8:(t + 1) * 8], rhs=idf32[:],
                    start=True, stop=True),
                    after=[("scalar", e.now("scalar")), ("vector", e.now("vector"))])
                T("scalar", lambda: nc.scalar.copy(out=wtsT[:], in_=ps_g2[0][0:8, 0:128]), after=[("tensor", wm)])

    for t in range(NT):
        d1 = e.dma(xa[:], X1I[t * 128:(t + 1) * 128, :], after=[("vector", e.now("vector")), ("scalar", e.now("scalar"))])
        d2 = e.dma(xb2[:], A1[t * 128:(t + 1) * 128, :], after=[("vector", e.now("vector"))])
        v1 = T("vector", lambda: nc.vector.tensor_add(out=xa[:], in0=xa[:], in1=xb2[:]), after=[("dma", d2)])
        d3 = e.dma(xb2[:], H1[t * 128:(t + 1) * 128, :], after=[("vector", v1)])
        v2 = T("vector", lambda: nc.vector.tensor_add(out=xa[:], in0=xa[:], in1=xb2[:]), after=[("dma", d3)])
        d4 = e.dma(xb2[:], MOE[t * 128:(t + 1) * 128, :], after=[("vector", v2), ("dmag", e.now("dmag"))])
        v3 = T("vector", lambda: nc.vector.tensor_add(out=xa[:], in0=xa[:], in1=xb2[:]), after=[("dma", d4)])
        e.dma(X2O[t * 128:(t + 1) * 128, :], xa[:], after=[("vector", v3)])
        if debug:
            dm = e.dma(xb2[:], MOE[t * 128:(t + 1) * 128, :], after=[("vector", e.now("vector"))])
            vv = T("vector", lambda: nc.vector.tensor_copy(out=scr32[:], in_=xb2[:]), after=[("dma", dm)])
            e.dma(MOE_D[t * 128:(t + 1) * 128, :], scr32[:], after=[("vector", vv)])

    return


def build_l1(debug=False):
    e = Emitter("l6")
    emit_l1(e, fused=False, debug=debug)
    e.op("gpsimd", lambda: None, after=e.all_now())
    return e.emit_and_close()


def emit_ffn_fused(e, debug=False):
    nc = e.nc
    e.phase("B")
    T = e.op
    x = e.dr("x", [S, D])
    WUP = e.dr("wup", [3072, D])
    WGATE = e.dr("wgate", [3072, D])
    WDOWN = e.dr("wdown", [D, 3072])
    IDF32 = e.dr("identf32", [128, 128], F32)
    IDENT = e.dr("identf", [128, 128], BF16)
    ATTN0 = e.dr("attn0", [S, D], F32, kind="Internal")
    X1 = e.dr("x1", [S, D], F32, kind="Internal")
    GD = e.dr("gd", [3072, S], F32, kind="Internal")
    UD = e.dr("ud", [3072, S], F32, kind="Internal")
    OD = e.dr("od", [3072, S], BF16, kind="Internal")

    xa = e.sb("xa", [128, D])
    xqT = e.sb("xqT", [128, NC * S], BF16)
    scr32 = e.sb("scr32", [128, D])
    scr_bf = e.sb("scr_bf", [128, D], BF16)
    w32 = e.sb("w32", [128, 3072])
    wbf = e.sb("wbf", [128, 3072], BF16)
    wT = e.sb("wT", [128, NC * 128], BF16)
    bm = e.sb("bm", [128, 96])
    rbm = e.sb("rbm", [128, 96])
    upf = e.sb("upf", [128, S])
    gf = e.sb("gf", [128, S])
    sigf = e.sb("sigf", [128, S])
    obf = e.sb("obf", [128, S], BF16)
    mcol = e.sb("mcol", [128, 1])
    vcol = e.sb("vcol", [128, 1])
    tcol = e.sb("tcol", [128, 1])
    rcol = e.sb("rcol", [128, 1])
    mx_up = e.sb("mx_up", [128, 1])
    mx_g = e.sb("mx_g", [128, 1])
    s_sc = e.sb("s_sc", [1, 8])
    scol_xn = e.sb("scol_xn", [128, 1])
    scol_up = e.sb("scol_up", [128, 1])
    scol_g = e.sb("scol_g", [128, 1])
    scol_g2 = e.sb("scol_g2", [128, 1])
    scol_o = e.sb("scol_o", [128, 1])
    onef32 = e.sb("onef32", [1, 128])
    idf32 = e.sb("idf32", [128, 128])
    identb = e.sb("identb", [128, 128], BF16)
    mrow = e.sb("mrow", [1, 128])
    wdT = e.sb("wdT", [128, 24 * 1024], BF16)
    oT_t = e.sb("oT_t", [128, 24 * 128], BF16)
    res32 = e.sb("res32", [128, D])

    ps_k = [e.psum(f"psB_{i}", [128, 512], F32) for i in range(4)]
    ps_g2 = [e.psum(f"psgB_{i}", [128, 512], F32) for i in range(2)]
    ps_t = e.psum("ps_tB", [128, 1024], BF16)
    ps_h = e.psum("ps_hB", [128, 512], F32)

    e.dma(idf32[:], IDF32[:])
    e.dma(identb[:], IDENT[:])
    T("vector", lambda: nc.vector.memset(onef32[:], 1.0))

    def bcast_col(src11, dst_col, dep):
        mm = T("tensor", lambda src11=src11: nc.tensor.matmul(
            out=ps_h[0:128, 0:1], lhsT=onef32[:], rhs=src11, start=True, stop=True),
            after=dep + [("scalar", e.now("scalar")), ("dma", e.now("dma"))])
        return T("scalar", lambda dst_col=dst_col: nc.scalar.copy(out=dst_col[:], in_=ps_h[0:128, 0:1]),
                 after=[("tensor", mm)])

    def scale_from_cc(slot, dst_col, cc_dep):
        flat = bass.AP(e.ccbuf.tensor, slot, [[1, 1], [1, 1]])
        d = e.dma(s_sc[0:1, 0:1], flat, after=cc_dep + [("vector", e.now("vector")), ("tensor", e.now("tensor"))],
                  queue="gpsimd")
        T("vector", lambda: nc.vector.tensor_scalar(out=s_sc[0:1, 0:1], in0=s_sc[0:1, 0:1], scalar1=1.0 / 32767.0,
                                                    scalar2=1e-12, op0=ALU.mult, op1=ALU.max),
          after=[("dmag", d)])
        return bcast_col(s_sc[0:1, 0:1], dst_col, [("vector", e.now("vector"))])

    def qdq_inplace(ap, scol):
        T("vector", lambda: nc.vector.reciprocal(out=rcol[:], in_=scol[:]), after=[("scalar", e.now("scalar"))])
        T("vector", lambda ap=ap: nc.vector.tensor_scalar(out=ap, in0=ap, scalar1=rcol[:], scalar2=None, op0=ALU.mult))
        T("vector", lambda ap=ap: nc.vector.tensor_scalar(out=ap, in0=ap, scalar1=MAGIC, scalar2=-MAGIC, op0=ALU.add, op1=ALU.add))
        T("vector", lambda ap=ap, scol=scol: nc.vector.tensor_scalar(out=ap, in0=ap, scalar1=scol[:], scalar2=None, op0=ALU.mult))

    scale_from_cc(0, scol_xn, [("gpsimd", e.now("gpsimd"))])

    for t in range(NT):
        dx = e.dma(xa[:], x[t * 128:(t + 1) * 128, :], after=[("scalar", e.now("scalar")), ("vector", e.now("vector"))])
        T("vector", lambda: nc.vector.reduce_sum(out=mcol[:], in_=xa[:], axis=AX.X), after=[("dma", dx)])
        sq = T("scalar", lambda: nc.scalar.activation(out=scr32[:], in_=xa[:], func=AF.Square, accum_out=vcol[:]),
               after=[("dma", dx)])
        T("vector", lambda: nc.vector.tensor_scalar(out=mcol[:], in0=mcol[:], scalar1=1.0 / D, scalar2=None, op0=ALU.mult))
        T("vector", lambda: nc.vector.tensor_scalar(out=vcol[:], in0=vcol[:], scalar1=1.0 / D, scalar2=None, op0=ALU.mult),
          after=[("scalar", sq)])
        T("vector", lambda: nc.vector.tensor_scalar(out=tcol[:], in0=mcol[:], scalar1=mcol[:], scalar2=None, op0=ALU.mult))
        T("vector", lambda: nc.vector.tensor_sub(out=vcol[:], in0=vcol[:], in1=tcol[:]))
        vv = T("vector", lambda: nc.vector.tensor_scalar(out=vcol[:], in0=vcol[:], scalar1=1e-5, scalar2=None, op0=ALU.add))
        sv = T("scalar", lambda: nc.scalar.activation(out=vcol[:], in_=vcol[:], func=AF.Sqrt), after=[("vector", vv)])
        T("vector", lambda: nc.vector.reciprocal(out=vcol[:], in_=vcol[:]), after=[("scalar", sv)])
        T("vector", lambda: nc.vector.tensor_scalar(out=xa[:], in0=xa[:], scalar1=mcol[:], scalar2=vcol[:],
                                                    op0=ALU.subtract, op1=ALU.mult))
        qdq_inplace(xa[:], scol_xn)
        cs = T("scalar", lambda: nc.scalar.copy(out=scr_bf[:], in_=xa[:]),
               after=[("vector", e.now("vector")), ("tensor", e.now("tensor"))])
        for ic in range(NC):
            T("tensor", lambda ic=ic: nc.tensor.transpose(
                out=ps_t[:, ic * 128:(ic + 1) * 128], in_=scr_bf[:, ic * 128:(ic + 1) * 128], identity=identb[:]),
              after=[("scalar", cs)] if ic == 0 else [])
        tc = e.now("tensor")
        for ic in range(NC):
            T("scalar", lambda t=t, ic=ic: nc.scalar.copy(
                out=xqT[:, ic * S + t * 128: ic * S + (t + 1) * 128], in_=ps_t[:, ic * 128:(ic + 1) * 128]),
              after=[("tensor", tc)] if ic == 0 else [])
    T("vector", lambda: nc.vector.memset(mx_up[:], 0.0))
    T("vector", lambda: nc.vector.memset(mx_g[:], 0.0))

    def dequant_to_wT(Wsrc, oc):
        nblk = 32
        d = e.dma(w32[:, 0:D], Wsrc[oc * 128:(oc + 1) * 128, :],
                  after=[("vector", e.now("vector")), ("tensor", e.now("tensor"))])
        T("vector", lambda: nc.vector.tensor_reduce(
            out=bm[:, 0:nblk], in_=w32[:, 0:D].rearrange("p (b k) -> p b k", k=32),
            axis=AX.X, op=ALU.max, apply_absolute_value=True), after=[("dma", d)])
        T("vector", lambda: nc.vector.tensor_scalar(out=bm[:, 0:nblk], in0=bm[:, 0:nblk], scalar1=1.0 / 7.0,
                                                    scalar2=1e-12, op0=ALU.mult, op1=ALU.max))
        w3 = w32[:, 0:D].rearrange("p (b k) -> p b k", k=32)
        T("vector", lambda: nc.vector.reciprocal(out=rbm[:, 0:nblk], in_=bm[:, 0:nblk]))
        bmap = bm[:, 0:nblk]
        b3 = bass.AP(bmap.tensor, bmap.offset, [list(bmap.ap[0]), [bmap.ap[1][0], nblk], [0, 32]])
        rmap = rbm[:, 0:nblk]
        r3 = bass.AP(rmap.tensor, rmap.offset, [list(rmap.ap[0]), [rmap.ap[1][0], nblk], [0, 32]])
        T("vector", lambda: nc.vector.tensor_tensor(out=w3, in0=w3, in1=r3, op=ALU.mult))
        T("vector", lambda: nc.vector.tensor_scalar(out=w32[:, 0:D], in0=w32[:, 0:D], scalar1=MAGIC,
                                                    scalar2=-MAGIC, op0=ALU.add, op1=ALU.add))
        wv = T("vector", lambda: nc.vector.tensor_tensor(out=w3, in0=w3, in1=b3, op=ALU.mult))
        cb = T("scalar", lambda: nc.scalar.copy(out=wbf[:, 0:D], in_=w32[:, 0:D]), after=[("vector", wv)])
        for ic in range(NC):
            T("tensor", lambda ic=ic: nc.tensor.transpose(
                out=ps_t[:, ic * 128:(ic + 1) * 128], in_=wbf[:, ic * 128:(ic + 1) * 128], identity=identb[:]),
              after=[("scalar", cb), ("scalar", e.now("scalar"))] if ic == 0 else [])
        tc = e.now("tensor")
        for ic in range(NC):
            T("scalar", lambda ic=ic: nc.scalar.copy(
                out=wT[:, ic * 128:(ic + 1) * 128], in_=ps_t[:, ic * 128:(ic + 1) * 128]),
              after=[("tensor", tc)] if ic == 0 else [])
        return e.now("scalar")

    def hidden_matmul(dst_sb, oc, psl):
        for nb in range(4):
            ps = psl[nb % 2]
            deps = [("scalar", e.now("scalar")), ("vector", e.now("vector"))]
            for ic in range(NC):
                T("tensor", lambda ic=ic, nb=nb, ps=ps: nc.tensor.matmul(
                    out=ps[:], lhsT=wT[:, ic * 128:(ic + 1) * 128],
                    rhs=xqT[:, ic * S + nb * 512: ic * S + (nb + 1) * 512],
                    start=(ic == 0), stop=(ic == NC - 1)), after=deps if ic == 0 else [])
            mmc = e.now("tensor")
            T("scalar", lambda nb=nb, ps=ps, dst_sb=dst_sb: nc.scalar.copy(
                out=dst_sb[:, nb * 512:(nb + 1) * 512], in_=ps[:]), after=[("tensor", mmc)])
        return e.now("scalar")

    def maxacc(src_sb, acc_col):
        T("vector", lambda src_sb=src_sb: nc.vector.tensor_reduce(
            out=tcol[:], in_=src_sb[:], axis=AX.X, op=ALU.max, apply_absolute_value=True),
          after=[("scalar", e.now("scalar"))])
        return T("vector", lambda acc_col=acc_col: nc.vector.tensor_max(out=acc_col[:], in0=acc_col[:], in1=tcol[:]))

    def col_to_val(col, dst11):
        mm = T("tensor", lambda col=col: nc.tensor.matmul(out=ps_h[0:1, 0:128], lhsT=col[:], rhs=idf32[:],
                                                          start=True, stop=True),
               after=[("vector", e.now("vector")), ("scalar", e.now("scalar"))])
        sc = T("scalar", lambda: nc.scalar.copy(out=mrow[:], in_=ps_h[0:1, 0:128]), after=[("tensor", mm)])
        return T("vector", lambda dst11=dst11: nc.vector.tensor_reduce(out=dst11, in_=mrow[:], axis=AX.X, op=ALU.max),
                 after=[("scalar", sc)])

    for oc in range(24):
        dequant_to_wT(WGATE, oc)
        gev = hidden_matmul(gf, oc, ps_g2)
        e.dma(GD[oc * 128:(oc + 1) * 128, :], gf[:], after=[("scalar", gev)])
        maxacc(gf, mx_g)
        dequant_to_wT(WUP, oc)
        uev = hidden_matmul(upf, oc, [ps_k[2], ps_k[3]])
        e.dma(UD[oc * 128:(oc + 1) * 128, :], upf[:], after=[("scalar", uev)])
        maxacc(upf, mx_up)
    col_to_val(mx_up, s_sc[0:1, 0:1])
    v2 = col_to_val(mx_g, s_sc[0:1, 1:2])
    _, cc12 = e.allreduce_max(s_sc[0:1, 0:2], 2, after=[("vector", v2)])
    scale_from_cc(1, scol_up, [("gpsimd", cc12)])
    scale_from_cc(2, scol_g, [("gpsimd", cc12)])

    T("vector", lambda: nc.vector.memset(mx_g[:], 0.0))
    for oc in range(24):
        d = e.dma(gf[:], GD[oc * 128:(oc + 1) * 128, :],
                  after=[("vector", e.now("vector")), ("scalar", e.now("scalar")), ("dma", e.now("dma"))])
        T("vector", lambda: nc.vector.tensor_copy(out=tcol[:], in_=tcol[:]), after=[("dma", d)])
        qdq_inplace(gf[:], scol_g)
        sg = T("scalar", lambda: nc.scalar.activation(out=sigf[:], in_=gf[:], func=AF.Sigmoid),
               after=[("vector", e.now("vector"))])
        T("vector", lambda: nc.vector.tensor_scalar(out=sigf[:], in0=sigf[:], scalar1=65536.0, scalar2=MAGIC,
                                                    op0=ALU.mult, op1=ALU.add), after=[("scalar", sg)])
        T("vector", lambda: nc.vector.tensor_scalar(out=sigf[:], in0=sigf[:], scalar1=-MAGIC, scalar2=65535.0,
                                                    op0=ALU.add, op1=ALU.min))
        g2v = T("vector", lambda: nc.vector.scalar_tensor_tensor(out=gf[:], in0=sigf[:], scalar=1.0 / 65536.0,
                                                                 in1=gf[:], op0=ALU.mult, op1=ALU.mult))
        e.dma(GD[oc * 128:(oc + 1) * 128, :], gf[:], after=[("vector", g2v)])
        maxacc(gf, mx_g)
    v3 = col_to_val(mx_g, s_sc[0:1, 0:1])
    _, cc3 = e.allreduce_max(s_sc[0:1, 0:1], 1, after=[("vector", v3)])
    scale_from_cc(3, scol_g2, [("gpsimd", cc3)])

    T("vector", lambda: nc.vector.memset(mx_g[:], 0.0))
    for oc in range(24):
        d1 = e.dma(gf[:], GD[oc * 128:(oc + 1) * 128, :],
                   after=[("vector", e.now("vector")), ("scalar", e.now("scalar")), ("dma", e.now("dma"))])
        d2 = e.dma(upf[:], UD[oc * 128:(oc + 1) * 128, :])
        T("vector", lambda: nc.vector.tensor_copy(out=tcol[:], in_=tcol[:]), after=[("dma", d2)])
        qdq_inplace(gf[:], scol_g2)
        qdq_inplace(upf[:], scol_up)
        ov = T("vector", lambda: nc.vector.tensor_mul(out=gf[:], in0=gf[:], in1=upf[:]))
        e.dma(UD[oc * 128:(oc + 1) * 128, :], gf[:], after=[("vector", ov)])
        maxacc(gf, mx_g)
    v4 = col_to_val(mx_g, s_sc[0:1, 0:1])
    _, cc4 = e.allreduce_max(s_sc[0:1, 0:1], 1, after=[("vector", v4)])
    scale_from_cc(4, scol_o, [("gpsimd", cc4)])

    for oc in range(24):
        d1 = e.dma(gf[:], UD[oc * 128:(oc + 1) * 128, :],
                   after=[("vector", e.now("vector")), ("scalar", e.now("scalar")), ("dma", e.now("dma"))])
        T("vector", lambda: nc.vector.tensor_copy(out=tcol[:], in_=tcol[:]), after=[("dma", d1)])
        qdq_inplace(gf[:], scol_o)
        ob = T("vector", lambda: nc.vector.tensor_copy(out=obf[:], in_=gf[:]), after=[("dma", e.now("dma"))])
        e.dma(OD[oc * 128:(oc + 1) * 128, :], obf[:], after=[("vector", ob)])

    for c in range(NC):
        for g3 in range(3):
            dg = e.dma(w32[:, 0:D], WDOWN[c * 128:(c + 1) * 128, g3 * 1024:(g3 + 1) * 1024],
                       after=[("vector", e.now("vector")), ("tensor", e.now("tensor"))])
            nblk = 32
            T("vector", lambda: nc.vector.tensor_reduce(
                out=bm[:, 0:nblk], in_=w32[:, 0:D].rearrange("p (b k) -> p b k", k=32),
                axis=AX.X, op=ALU.max, apply_absolute_value=True), after=[("dma", dg)])
            T("vector", lambda: nc.vector.tensor_scalar(out=bm[:, 0:nblk], in0=bm[:, 0:nblk], scalar1=1.0 / 7.0,
                                                        scalar2=1e-12, op0=ALU.mult, op1=ALU.max))
            w3 = w32[:, 0:D].rearrange("p (b k) -> p b k", k=32)
            T("vector", lambda: nc.vector.reciprocal(out=rbm[:, 0:nblk], in_=bm[:, 0:nblk]))
            bmap = bm[:, 0:nblk]
            b3 = bass.AP(bmap.tensor, bmap.offset, [list(bmap.ap[0]), [bmap.ap[1][0], nblk], [0, 32]])
            rmap = rbm[:, 0:nblk]
            r3 = bass.AP(rmap.tensor, rmap.offset, [list(rmap.ap[0]), [rmap.ap[1][0], nblk], [0, 32]])
            T("vector", lambda w3=w3, r3=r3: nc.vector.tensor_tensor(out=w3, in0=w3, in1=r3, op=ALU.mult))
            T("vector", lambda: nc.vector.tensor_scalar(out=w32[:, 0:D], in0=w32[:, 0:D], scalar1=MAGIC,
                                                        scalar2=-MAGIC, op0=ALU.add, op1=ALU.add))
            wv = T("vector", lambda w3=w3, b3=b3: nc.vector.tensor_tensor(out=w3, in0=w3, in1=b3, op=ALU.mult))
            cb = T("scalar", lambda: nc.scalar.copy(out=wbf[:, 0:D], in_=w32[:, 0:D]), after=[("vector", wv)])
            for ic in range(8):
                T("tensor", lambda ic=ic: nc.tensor.transpose(
                    out=ps_t[:, ic * 128:(ic + 1) * 128], in_=wbf[:, ic * 128:(ic + 1) * 128], identity=identb[:]),
                  after=[("scalar", cb), ("scalar", e.now("scalar"))] if ic == 0 else [])
            tc = e.now("tensor")
            for ic in range(8):
                icg = g3 * 8 + ic
                T("scalar", lambda icg=icg, ic=ic, c=c: nc.scalar.copy(
                    out=wdT[:, icg * 1024 + c * 128: icg * 1024 + (c + 1) * 128],
                    in_=ps_t[:, ic * 128:(ic + 1) * 128]),
                  after=[("tensor", tc)] if ic == 0 else [])
    wd_done = e.now("scalar")

    for t in range(NT):
        dl = []
        for icg in range(24):
            dl.append(e.dma(oT_t[:, icg * 128:(icg + 1) * 128],
                            OD[icg * 128:(icg + 1) * 128, t * 128:(t + 1) * 128],
                            after=[("tensor", e.now("tensor"))] if icg == 0 else []))
        dx = e.dma(res32[:], x[t * 128:(t + 1) * 128, :], after=[("vector", e.now("vector"))])
        dat = e.dma(xa[:], ATTN0[t * 128:(t + 1) * 128, :])
        av = T("vector", lambda: nc.vector.tensor_add(out=res32[:], in0=res32[:], in1=xa[:]),
               after=[("dma", e.now("dma"))])
        for nb in range(2):
            ps = ps_k[nb]
            deps = [("scalar", wd_done), ("vector", av), ("dma", dl[-1]), ("scalar", e.now("scalar"))]
            for icg in range(24):
                T("tensor", lambda icg=icg, nb=nb, ps=ps: nc.tensor.matmul(
                    out=ps[:], lhsT=oT_t[:, icg * 128:(icg + 1) * 128],
                    rhs=wdT[:, icg * 1024 + nb * 512: icg * 1024 + (nb + 1) * 512],
                    start=(icg == 0), stop=(icg == 23)), after=deps if icg == 0 else [])
            mmc = e.now("tensor")
            fv = T("vector", lambda t=t, nb=nb, ps=ps: nc.vector.tensor_add(
                out=res32[:, nb * 512:(nb + 1) * 512], in0=res32[:, nb * 512:(nb + 1) * 512], in1=ps[:]),
                after=[("tensor", mmc)])
            e.dma(X1[t * 128:(t + 1) * 128, nb * 512:(nb + 1) * 512], res32[:, nb * 512:(nb + 1) * 512],
                  after=[("vector", fv)])
    return


def build_fused(debug=False):
    e = Emitter("fused", fused=True)
    emit_l0attn(e, fused=True, debug=False)
    e.fence()
    emit_ffn_fused(e, debug=False)
    e.fence()
    emit_l1(e, fused=True, debug=debug)
    e.op("gpsimd", lambda: None, after=e.all_now())
    return e.emit_and_close()


def _fused_in_map(inputs, c):
    m = dict(_l1_in_map(inputs, c))
    m.update({"wup": inputs["L0_ffn_w_up"], "wgate": inputs["L0_ffn_w_gate"],
              "wdown": inputs["L0_ffn_w_down"]})
    m6 = _l6_in_map(inputs, c, None)
    m6.pop("x1")
    m.update(m6)
    return {k: np.ascontiguousarray(v) for k, v in m.items()}


def kernel_fused(**inputs):
    inputs = {k: np.asarray(v, dtype=np.float32) for k, v in inputs.items()}
    ncf = build_fused()
    res = run_bass_kernel_spmd(ncf, [_fused_in_map(inputs, c) for c in range(8)], core_ids=list(range(8)))
    return np.stack([np.asarray(res.results[c]["x2"]) for c in range(8)])


# version 06 (final)
# speedup vs baseline: 1.0254x; 1.0108x over previous
import math
from contextlib import ExitStack

import numpy as np
import ml_dtypes
import concourse.bass as bass
import concourse.mybir as mybir
from concourse.bass_utils import run_bass_kernel_spmd

F32 = mybir.dt.float32
BF16 = mybir.dt.bfloat16
AF = mybir.ActivationFunctionType
ALU = mybir.AluOpType
AX = mybir.AxisListType

S = 2048
D = 1024
NT = S // 128
NC = D // 128
H, HD = 16, 64
MAGIC = 12582912.0

BF = ml_dtypes.bfloat16


def _host_consts():
    c = {}
    NB, MD = 32, 128
    rel = np.arange(-2047, 2049)
    nb = NB // 2
    buckets = (rel > 0).astype(np.int64) * nb
    rp = np.abs(rel)
    max_exact = nb // 2
    is_small = rp < max_exact
    large = max_exact + (
        np.log(np.maximum(rp, 1) / max_exact) / np.log(MD / max_exact) * (nb - max_exact)
    ).astype(np.int64)
    large = np.minimum(large, nb - 1)
    buckets = buckets + np.where(is_small, rp, large)
    ohrt = np.zeros((NB, 4096), np.float32)
    ohrt[buckets[::-1], np.arange(4096)] = 1.0
    c["ohrt"] = ohrt.astype(BF)
    grp33 = np.zeros((8, 33), np.float32)
    grp33[0:4, 0] = 1.0
    grp33[4:8, 32] = 1.0
    c["grp33"] = grp33.astype(BF)
    c["ident_bf"] = np.eye(128, dtype=np.float32).astype(BF)
    c["ident_f32"] = np.eye(128, dtype=np.float32)
    inv = 1.0 / (10000.0 ** (np.arange(32) / 32.0))
    ang = inv[:, None] * np.arange(S)[None, :]
    c["cos32"] = np.tile(np.cos(ang), (4, 1)).astype(BF)
    c["sin32"] = np.tile(np.sin(ang), (4, 1)).astype(BF)
    caus = np.zeros((128, 4, 512), np.float32)
    for v in range(4):
        off = 128 * v
        caus[:, v, :] = np.where(np.arange(512)[None, :] > off + np.arange(128)[:, None], -1e9, 0.0)
    c["caus"] = caus.reshape(128, 2048).astype(BF)
    perm = np.zeros(D, np.int64)
    for v in range(8):
        for a in range(4):
            for j in range(32):
                perm[v * 128 + a * 32 + j] = (4 * (v // 2) + a) * 64 + 32 * (v % 2) + j
    c["rope_perm"] = perm
    return c


_GNOISE = None


def _gumbel_noise():
    global _GNOISE
    if _GNOISE is None:
        import subprocess, sys, tempfile, os
        fd, path = tempfile.mkstemp(suffix=".npy")
        os.close(fd)
        code = (
            "import jax\n"
            "jax.config.update('jax_platforms', 'cpu')\n"
            "import jax.numpy as jnp, numpy as np, sys\n"
            "n = jax.random.gumbel(jax.random.key(42), (8, 2048, 8), jnp.float32) * 0.05\n"
            f"np.save({path!r}, np.asarray(n))\n"
        )
        env = dict(os.environ)
        env["JAX_PLATFORMS"] = "cpu"
        subprocess.run([sys.executable, "-c", code], check=True, env=env,
                       capture_output=True, timeout=600)
        _GNOISE = np.load(path)
        os.unlink(path)
    return _GNOISE


_CONSTS = _host_consts()


ENGINES = ("sync", "tensor", "vector", "scalar", "gpsimd")


class Emitter:

    def __init__(self, name, fused=False):
        self.nc = bass.Bass("TRN2", target_bir_lowering=False, num_devices=8)
        self.es = ExitStack()
        self.name = name
        self.fused = fused
        self.plan = {e: [] for e in ENGINES}
        self.sems = {}
        self.counts = {}
        self._drams = {}
        self.pool = None
        self._bump = 0
        self._tag = ""
        for e in ENGINES:
            self._mksem(e)
        self._mksem("dma")
        self._mksem("dmag")
        if fused:
            self.pool = self.es.enter_context(self.nc.sbuf_tensor("pool", [128, 212000], mybir.dt.uint8))
            self.ccbuf = self.nc.dram_tensor("ccbuf", [128, 512], F32, kind="Internal").ap()
            self.ccn = 0

    def phase(self, tag):
        self._bump = 0
        self._tag = tag + "_"
        if getattr(self, "_phase_es", None) is not None:
            self._phase_es.close()
        self._phase_es = ExitStack()

    def dr(self, name, shape, dtype=F32, kind="ExternalInput"):
        if name in self._drams:
            return self._drams[name]
        ap = self.nc.dram_tensor(name, shape, dtype, kind=kind).ap()
        self._drams[name] = ap
        return ap

    def allreduce_max(self, src_col_ap, n_vals, after):
        k0 = self.ccn
        self.ccn += n_vals
        assert self.ccn <= 512
        dst = bass.AP(self.ccbuf.tensor, k0, [[1, 1], [1, n_vals]])
        d = self.dma(dst, src_col_ap, after=after, queue="gpsimd")
        cc = self.op("gpsimd", lambda dst=dst: self.nc.gpsimd.collective_compute(
            "AllReduce", ALU.max, replica_groups=[[0, 1, 2, 3, 4, 5, 6, 7]],
            ins=[dst], outs=[dst]), after=[("dmag", d)], inc="gpsimd")
        return dst, cc

    def _mksem(self, n):
        self.sems[n] = self.es.enter_context(self.nc.semaphore("sem_" + n))
        self.counts[n] = 0

    def now(self, sem):
        return self.counts[sem]

    def snap(self):
        return dict(self.counts)

    def op(self, engine, fn, after=(), inc=None):
        if inc is None:
            inc = "dma" if engine == "sync" else engine
        n = 16 if inc in ("dma", "dmag") else 1
        after = tuple(after)
        if engine in ("vector", "scalar", "gpsimd") and self.counts.get(engine, 0) > 0:
            after = after + ((engine, self.counts[engine]),)
        self.counts[inc] += n
        self.plan[engine].append((fn, tuple(after), self.sems[inc], n, inc, self.counts[inc]))
        return self.counts[inc]

    def dma(self, out, in_, after=(), queue="sync"):
        sem = "dma" if queue == "sync" else "dmag"
        eng = self.nc.sync if queue == "sync" else self.nc.gpsimd
        return self.op(queue, lambda o=out, i=in_, g=eng: g.dma_start(out=o, in_=i),
                       after=after, inc=sem)

    def all_now(self):
        return [(s, c) for s, c in self.counts.items() if c > 0]

    def fence(self):
        waits = self.all_now()
        for eng in ENGINES:
            if eng == "sync":
                continue
            self.op(eng, lambda: None, after=waits, inc=eng)

    def verify_acyclic(self):
        prod = {}
        nodes = []
        for eng in ENGINES:
            for i, (fn, after, sem, n, inc, cnt) in enumerate(self.plan[eng]):
                prod[(inc, cnt)] = (eng, i)
                nodes.append((eng, i))
        bysem = {}
        for eng in ENGINES:
            for i, (fn, after, sem, n, inc, cnt) in enumerate(self.plan[eng]):
                bysem.setdefault(inc, []).append((cnt, eng, i))
        for k in bysem:
            bysem[k].sort()
        import bisect as _b
        edges = {nd: [] for nd in nodes}
        indeg = {nd: 0 for nd in nodes}
        def add(a, b):
            edges[a].append(b)
            indeg[b] += 1
        for eng in ENGINES:
            pl = self.plan[eng]
            for i in range(1, len(pl)):
                add((eng, i - 1), (eng, i))
            for i, (fn, after, sem, n, inc, cnt) in enumerate(pl):
                for sname, c in after:
                    if c <= 0:
                        continue
                    lst = bysem.get(sname, [])
                    j = _b.bisect_left(lst, (c, "", -1))
                    assert j < len(lst) and lst[j][0] >= c, (eng, i, sname, c)
                    _, peng, pi = lst[j]
                    add((peng, pi), (eng, i))
        from collections import deque
        q = deque([nd for nd in nodes if indeg[nd] == 0])
        seen = 0
        while q:
            nd = q.popleft()
            seen += 1
            for m in edges[nd]:
                indeg[m] -= 1
                if indeg[m] == 0:
                    q.append(m)
        if seen != len(nodes):
            stuck = [nd for nd in nodes if indeg[nd] > 0]
            print(f"DEADLOCK: {len(nodes)-seen} ops unreachable; first stuck: {stuck[:6]}")
            for nd in stuck[:4]:
                eng, i = nd
                fn, after, sem, n, inc, cnt = self.plan[eng][i]
                print("  stuck op", eng, i, "waits", after)
            return False
        return True

    def emit_and_close(self):
        needed = {}
        for eng in ENGINES:
            for (fn, after, sem, n, inc, cnt) in self.plan[eng]:
                for sname, c in after:
                    if c > 0:
                        needed.setdefault(sname, set()).add(c)
        block = self.es.enter_context(self.nc.Block())
        for eng_name in ENGINES:
            ops = self.plan[eng_name]
            if not ops:
                continue
            last_idx = {}
            for k, op_ in enumerate(ops):
                last_idx[op_[4]] = k

            def section(eng, ops=ops, last_idx=last_idx):
                pending = {}
                for k, (fn, after, sem, n, inc, cnt) in enumerate(ops):
                    for sname, c in after:
                        if c > 0:
                            eng.wait_ge(self.sems[sname], c)
                    inst = fn()
                    pend = pending.get(inc, 0) + n
                    sparse = inc in ("tensor", "vector", "scalar", "gpsimd") and inst is not None
                    take = (not sparse) or (cnt in needed.get(inc, ())) or (k == last_idx[inc])
                    if take:
                        if inst is not None:
                            inst.then_inc(self.sems[inc], pend)
                        else:
                            eng.sem_inc(self.sems[inc], pend)
                        pending[inc] = 0
                    else:
                        pending[inc] = pend
            getattr(block, eng_name)(section)
        self.es.close()
        return self.nc

    def sb(self, name, shape, dtype=F32):
        if self.pool is None:
            return self.es.enter_context(self.nc.sbuf_tensor(name, shape, dtype))
        nbytes = shape[1] * {F32: 4, BF16: 2, mybir.dt.uint8: 1, mybir.dt.int32: 4}[dtype]
        nbytes = (nbytes + 31) // 32 * 32
        assert self._bump + nbytes <= 212000, (self._tag, name, self._bump, nbytes)
        t = self.nc.alloc_sbuf_tensor_at(self._tag + name, shape, dtype,
                                         offset=self.nc.lookup_mloc(self.pool).addr + self._bump)
        self._bump += nbytes
        return t

    def sb_at(self, name, shape, dtype, base, byte_off=0):
        addr = self.nc.lookup_mloc(base).addr + byte_off
        return self.nc.alloc_sbuf_tensor_at(self._tag + name, shape, dtype, offset=addr)

    def psum(self, name, shape, dtype=F32):
        if self.pool is not None and getattr(self, "_phase_es", None) is not None:
            return self._phase_es.enter_context(self.nc.psum_tensor(self._tag + name, shape, dtype))
        return self.es.enter_context(self.nc.psum_tensor(name, shape, dtype))


def rev_free(ap2d, lo, hi):
    pat = [list(d) for d in ap2d.ap]
    fstep = pat[1][0]
    return bass.AP(ap2d.tensor, ap2d.offset + (hi - 1) * fstep, [pat[0], [-fstep, hi - lo]])


def emit_l0attn(e, fused=False, debug=False):
    nc = e.nc
    e.phase("A")
    x = e.dr("x", [S, D], F32)
    WQ = e.dr("wq", [D, D], F32)
    WK = e.dr("wk", [D, D], F32)
    WV = e.dr("wv", [D, D], F32)
    WO = e.dr("wo", [D, D], F32)
    BQ = e.dr("bq", [1, D], F32)
    BK = e.dr("bk", [1, D], F32)
    BV = e.dr("bv", [1, D], F32)
    BO = e.dr("bo", [1, D], F32)
    REL = e.dr("rel", [32, 16], F32)
    GRUW = e.dr("gruw", [64, 8], F32)
    GRUB = e.dr("grub", [1, 8], F32)
    GRUC = e.dr("gruc", [1, 16], F32)
    OHRT = e.dr("ohrt", [32, 4096], BF16)
    GRP33 = e.dr("grp33", [8, 33], BF16)
    IDENT = e.dr("identf", [128, 128], BF16)
    IDF32 = e.dr("identf32", [128, 128], F32)

    ATTN0 = e.dr("attn0", [S, D], F32, kind="Internal" if fused else "ExternalOutput")
    if debug:
        XN_D = e.dr("xn_dbg", [S, D], F32, kind="ExternalOutput")
        QT_D = e.dr("qT_dbg", [D, S], BF16, kind="ExternalOutput")
        KT_D = e.dr("kT_dbg", [D, S], BF16, kind="ExternalOutput")
        VTOK_D = e.dr("vtok_dbg", [S, D], BF16, kind="ExternalOutput")
        GATE_D = e.dr("gate_dbg", [H, S], BF16, kind="ExternalOutput")
        S_D = e.dr("s_dbg", [128, S], F32, kind="ExternalOutput")
        SUM_D = e.dr("sum_dbg", [4, 128], F32, kind="ExternalOutput")
        WN_D = e.dr("wn_dbg", [128, S], BF16, kind="ExternalOutput")
        CTX_D = e.dr("ctx_dbg", [D, S], BF16, kind="ExternalOutput")
        TREV_D = e.dr("trevd_dbg", [16, 4096], BF16, kind="ExternalOutput")
    XNMAX = None if fused else e.dr("xnmax", [1, 1], F32, kind="ExternalOutput")
    TREV = e.dr("trev_d", [16, 4096], BF16, kind="Internal")

    xa = e.sb("xa", [128, D])
    xnT = e.sb("xnT", [128, NC * S], BF16)
    qT = e.sb("qT", [128, NC * S], BF16)
    kT = e.sb("kT", [128, NC * S], BF16)
    vtok_all = e.sb("vtok_all", [128, NT * D], BF16)
    ctxT = e.sb("ctxT", [128, NC * S], BF16)
    w_f32 = e.sb("w_f32", [128, 512])
    w_bf = e.sb("w_bf", [128, NC * 512], BF16)
    scr_bf = e.sb("scr_bf", [128, D], BF16)
    scr32 = e.sb("scr32", [128, D])
    ssb = e.sb("ssb", [128, S])
    wn = e.sb("wn", [128, S], BF16)
    shared = e.sb("shared", [128, S], BF16)
    grow = e.sb("grow", [1, S], BF16)
    gateT = e.sb("gateT", [128, 16])
    mcol = e.sb("mcol", [128, 1])
    vcol = e.sb("vcol", [128, 1])
    negm = e.sb("negm", [128, 1])
    zcol = e.sb("zcol", [128, 1])
    rz = e.sb("rz", [128, 1])
    mx16 = e.sb("mx16", [128, NT])
    mxacc = e.sb("mxacc", [128, 1])
    mx1 = e.sb("mx1", [1, 1])
    ident = e.sb("ident", [128, 128], BF16)
    idf32 = e.sb("idf32", [128, 128])
    rel_sb = e.sb("rel_sb", [32, 16], BF16)
    ohrt_sb = e.sb("ohrt_sb", [32, 128], BF16)
    ttmp = e.sb("ttmp", [128, 16])
    trev_sb = e.sb("trev_sb", [16, 128], BF16)
    gruw_sb = e.sb("gruw_sb", [64, 8], BF16)
    gruwT = e.sb("gruwT", [8, 64], BF16)
    w2d = e.sb("w2d", [128, 33], BF16)
    bias33 = e.sb("bias33", [33, 1])
    grp_sb = e.sb("grp_sb", [8, 33], BF16)
    gcl = e.sb("gcl", [1, 16])
    grub_sb = e.sb("grub_sb", [1, 8])
    grub2 = e.sb("grub2", [1, 2])
    bqc = e.sb("bqc", [128, NC])
    bkc = e.sb("bkc", [128, NC])
    bqs = e.sb("bqs", [128, NC])
    rowbuf_bf = e.sb("rowbuf_bf", [1, D], BF16)
    ones1 = e.sb("ones1", [1, 128], BF16)
    onef = e.sb("onef", [1, 1], BF16)

    ps_k = [e.psum(f"ps_{i}", [128, 512], F32) for i in range(4)]
    ps_t = e.psum("ps_t", [128, 1024], BF16)
    ps_t2 = e.psum("ps_t2", [128, 1024], BF16)
    ps_g = e.psum("ps_g", [128, 512], F32)
    ps_h = e.psum("ps_h", [128, 512], F32)

    T = e.op

    e.dma(ident[:], IDENT[:])
    e.dma(idf32[:], IDF32[:])
    e.dma(rel_sb[:], REL[:], queue="gpsimd")
    e.dma(gruw_sb[:], GRUW[:], queue="gpsimd")
    e.dma(grp_sb[:], GRP33[:])
    e.dma(gcl[:], GRUC[:])
    e.dma(grub_sb[:], GRUB[:])
    for c in range(NC):
        e.dma(bqc[:, c:c + 1], BQ[0:1, c * 128:(c + 1) * 128])
        e.dma(bkc[:, c:c + 1], BK[0:1, c * 128:(c + 1) * 128])
    ld_misc = e.now("dma")
    T("vector", lambda: nc.vector.memset(ones1[:], 1.0))
    T("vector", lambda: nc.vector.memset(onef[:], 1.0))
    T("vector", lambda: nc.vector.memset(bias33[:], 0.0))

    for t in range(NT):
        dx = e.dma(xa[:], x[t * 128:(t + 1) * 128, :], after=[("scalar", e.now("scalar")), ("vector", e.now("vector"))])
        rs = T("vector", lambda: nc.vector.reduce_sum(out=mcol[:], in_=xa[:], axis=AX.X), after=[("dma", dx)])
        if debug and t == 0:
            dtap0 = e.dma(bass.AP(SUM_D.tensor, 0, [[1, 128], [1, 1]]), mcol[:], after=[("vector", rs)])
        sq = T("scalar", lambda: nc.scalar.activation(out=scr32[:], in_=xa[:], func=AF.Square, accum_out=vcol[:]),
               after=[("dma", dx)])
        T("vector", lambda: nc.vector.tensor_scalar(out=mcol[:], in0=mcol[:], scalar1=1.0 / D, scalar2=None, op0=ALU.mult),
          after=[("dma", e.now("dma"))] if (debug and t == 0) else [])
        T("vector", lambda: nc.vector.tensor_scalar(out=vcol[:], in0=vcol[:], scalar1=1.0 / D, scalar2=None, op0=ALU.mult),
          after=[("scalar", sq)])
        T("vector", lambda: nc.vector.tensor_scalar(out=negm[:], in0=mcol[:], scalar1=mcol[:], scalar2=None, op0=ALU.mult))
        T("vector", lambda: nc.vector.tensor_sub(out=vcol[:], in0=vcol[:], in1=negm[:]))
        vv = T("vector", lambda: nc.vector.tensor_scalar(out=vcol[:], in0=vcol[:], scalar1=1e-5, scalar2=None, op0=ALU.add))
        if debug and t == 0:
            e.dma(bass.AP(SUM_D.tensor, 1*128, [[1, 128], [1, 1]]), mcol[:], after=[("vector", vv)])
            e.dma(bass.AP(SUM_D.tensor, 2*128, [[1, 128], [1, 1]]), vcol[:], after=[("vector", vv)])
        sv = T("scalar", lambda: nc.scalar.activation(out=vcol[:], in_=vcol[:], func=AF.Sqrt), after=[("vector", vv)])
        rc = T("vector", lambda: nc.vector.reciprocal(out=vcol[:], in_=vcol[:]), after=[("scalar", sv)])
        if debug and t == 0:
            e.dma(bass.AP(SUM_D.tensor, 3*128, [[1, 128], [1, 1]]), vcol[:], after=[("vector", rc)])
        T("vector", lambda: nc.vector.tensor_scalar(out=xa[:], in0=xa[:], scalar1=mcol[:], scalar2=vcol[:],
                                                    op0=ALU.subtract, op1=ALU.mult),
          after=[("dma", e.now("dma"))] if (debug and t == 0) else [])
        vmx = T("vector", lambda t=t: nc.vector.tensor_reduce(out=mx16[:, t:t + 1], in_=xa[:], axis=AX.X,
                                                              op=ALU.max, apply_absolute_value=True))
        if debug:
            e.dma(XN_D[t * 128:(t + 1) * 128, :], xa[:], after=[("vector", vmx)])
        cs = T("scalar", lambda: nc.scalar.copy(out=scr_bf[:], in_=xa[:]),
               after=[("vector", vmx), ("tensor", e.now("tensor"))])
        for ic in range(NC):
            T("tensor", lambda ic=ic: nc.tensor.transpose(
                out=ps_t[:, ic * 128:(ic + 1) * 128], in_=scr_bf[:, ic * 128:(ic + 1) * 128], identity=ident[:]),
              after=[("scalar", cs), ("dma", e.now("dma"))] if ic == 0 else [])
        tc = e.now("tensor")
        for ic in range(NC):
            T("scalar", lambda t=t, ic=ic: nc.scalar.copy(
                out=xnT[:, ic * S + t * 128: ic * S + (t + 1) * 128], in_=ps_t[:, ic * 128:(ic + 1) * 128]),
              after=[("tensor", tc)] if ic == 0 else [])
    ln_done_v = e.now("vector")

    T("vector", lambda: nc.vector.tensor_reduce(out=mxacc[:], in_=mx16[:], axis=AX.X, op=ALU.max))
    tr1 = T("tensor", lambda: nc.tensor.matmul(out=ps_h[0:1, 0:128], lhsT=mxacc[:], rhs=idf32[:],
                                               start=True, stop=True),
            after=[("vector", e.now("vector")), ("dma", ld_misc)])
    sc1 = T("scalar", lambda: nc.scalar.copy(out=scr32[0:1, 0:128], in_=ps_h[0:1, 0:128]), after=[("tensor", tr1)])
    vm1 = T("vector", lambda: nc.vector.tensor_reduce(out=mx1[:], in_=scr32[0:1, 0:128], axis=AX.X, op=ALU.max),
            after=[("scalar", sc1)])
    if fused:
        e.allreduce_max(mx1[:], 1, after=[("vector", vm1)])
    else:
        e.dma(XNMAX[:], mx1[:], after=[("vector", vm1)])

    for j in range(32):
        dj = e.dma(ohrt_sb[:], OHRT[0:32, j * 128:(j + 1) * 128], after=[("tensor", e.now("tensor"))])
        mm = T("tensor", lambda: nc.tensor.matmul(
            out=ps_h[0:128, 0:16], lhsT=ohrt_sb[:], rhs=rel_sb[:], start=True, stop=True),
            after=[("dma", dj), ("scalar", e.now("scalar"))])
        ev = T("scalar", lambda: nc.scalar.copy(out=ttmp[:], in_=ps_h[0:128, 0:16]), after=[("tensor", mm)])
        tr = T("tensor", lambda: nc.tensor.matmul(out=ps_g[0:16, 0:128], lhsT=ttmp[:], rhs=idf32[:],
                                                  start=True, stop=True), after=[("scalar", ev)])
        sv2 = T("scalar", lambda: nc.scalar.copy(out=trev_sb[:], in_=ps_g[0:16, 0:128]), after=[("tensor", tr)])
        e.dma(TREV[0:16, j * 128:(j + 1) * 128], trev_sb[:], after=[("scalar", sv2)])
    trev_ready = e.now("dma")

    trw = T("tensor", lambda: nc.tensor.matmul(out=ps_g[0:8, 0:64], lhsT=gruw_sb[:], rhs=ident[0:64, 0:64],
                                               start=True, stop=True),
            after=[("dma", ld_misc), ("scalar", e.now("scalar"))])
    ws = T("scalar", lambda: nc.scalar.copy(out=gruwT[:], in_=ps_g[0:8, 0:64]), after=[("tensor", trw)])
    mmw = T("tensor", lambda: nc.tensor.matmul(out=ps_h[0:64, 0:33], lhsT=gruwT[:], rhs=grp_sb[:],
                                               start=True, stop=True), after=[("scalar", ws)])
    w2a = T("scalar", lambda: nc.scalar.copy(out=w2d[0:64, :], in_=ps_h[0:64, 0:33]), after=[("tensor", mmw)])
    T("scalar", lambda: nc.scalar.copy(out=w2d[64:128, :], in_=ps_h[0:64, 0:33]))
    gb = T("vector", lambda: nc.vector.tensor_reduce(
        out=grub2[:], in_=grub_sb[:].rearrange("a (g k) -> a g k", k=4), axis=AX.X, op=ALU.add),
        after=[("dma", ld_misc)])
    e.dma(bias33[0:1, 0:1], grub2[0:1, 0:1], after=[("vector", gb)], queue="gpsimd")
    e.dma(bias33[32:33, 0:1], grub2[0:1, 1:2], after=[("vector", gb)], queue="gpsimd")

    T("vector", lambda: nc.vector.tensor_scalar(out=bqs[:], in0=bqc[:], scalar1=0.125, scalar2=None, op0=ALU.mult),
      after=[("dma", ld_misc)])

    def load_weight_half(W, half):
        for ic in range(NC):
            e.dma(w_bf[:, ic * 512:(ic + 1) * 512], W[ic * 128:(ic + 1) * 128, half * 512:(half + 1) * 512],
                  after=[("tensor", e.now("tensor"))] if ic == 0 else [], queue="gpsimd")
        return e.now("dmag")

    def proj_A(W, outT, bias_cols, scale):
        for half in range(2):
            wld = load_weight_half(W, half)
            for oc4 in range(4):
                oc = half * 4 + oc4
                for nb in range(S // 512):
                    ps = ps_k[nb % 4]
                    deps = [("scalar", e.now("scalar")), ("vector", e.now("vector")), ("dmag", wld)]
                    for ic in range(NC):
                        T("tensor", lambda ic=ic, oc4=oc4, nb=nb, ps=ps: nc.tensor.matmul(
                            out=ps[:], lhsT=w_bf[:, ic * 512 + oc4 * 128: ic * 512 + (oc4 + 1) * 128],
                            rhs=xnT[:, ic * S + nb * 512: ic * S + (nb + 1) * 512],
                            start=(ic == 0), stop=(ic == NC - 1)), after=deps if ic == 0 else [])
                    mmc = e.now("tensor")
                    T("scalar", lambda oc=oc, nb=nb, ps=ps, outT=outT, bias_cols=bias_cols, scale=scale:
                      nc.scalar.activation(out=outT[:, oc * S + nb * 512: oc * S + (nb + 1) * 512], in_=ps[:],
                                           func=AF.Identity, bias=bias_cols[:, oc:oc + 1], scale=scale),
                      after=[("tensor", mmc)])

    proj_A(WQ, qT, bqs, 0.125)
    proj_A(WK, kT, bkc, 1.0)
    dbv = e.dma(rowbuf_bf[:], BV[:], queue="gpsimd")
    for half in range(2):
        wld = load_weight_half(WV, half)
        for t in range(NT):
            ps = ps_k[t % 4]
            deps = [("scalar", e.now("scalar")), ("vector", e.now("vector")), ("dmag", wld)]
            for ic in range(NC):
                T("tensor", lambda t=t, ic=ic, half=half, ps=ps: nc.tensor.matmul(
                    out=ps[:], lhsT=xnT[:, ic * S + t * 128: ic * S + (t + 1) * 128],
                    rhs=w_bf[:, ic * 512:(ic + 1) * 512],
                    start=(ic == 0), stop=False), after=deps if ic == 0 else [])
            T("tensor", lambda half=half, ps=ps: nc.tensor.matmul(
                out=ps[:], lhsT=ones1[:], rhs=rowbuf_bf[0:1, half * 512:(half + 1) * 512],
                start=False, stop=True))
            mmc = e.now("tensor")
            T("scalar", lambda t=t, half=half, ps=ps: nc.scalar.copy(
                out=vtok_all[:, t * D + half * 512: t * D + (half + 1) * 512], in_=ps[:]),
              after=[("tensor", mmc)])

    if debug:
        for c in range(NC):
            e.dma(QT_D[c * 128:(c + 1) * 128, :], qT[:, c * S:(c + 1) * S], after=[("scalar", e.now("scalar"))])
            e.dma(KT_D[c * 128:(c + 1) * 128, :], kT[:, c * S:(c + 1) * S], after=[("scalar", e.now("scalar"))])
        for t in range(NT):
            e.dma(VTOK_D[t * 128:(t + 1) * 128, :], vtok_all[:, t * D:(t + 1) * D], after=[("scalar", e.now("scalar"))])
        e.dma(TREV_D[:], TREV[:], after=[("dma", trev_ready)])

    wn2A = e.sb_at("wn2A", [128, S], BF16, w_bf, 0)
    wnBA = [wn, wn2A]
    twin = e.sb_at("twinA", [128, S], BF16, w_bf, 4096)

    def l0_gate_prep(h):
        c2 = h // 2
        pb = 64 * (h % 2)
        for nb in range(4):
            mm = T("tensor", lambda c2=c2, pb=pb, nb=nb: nc.tensor.matmul(
                out=ps_g[0:33, 0:512], lhsT=w2d[pb:pb + 64, :],
                rhs=xnT[pb:pb + 64, c2 * S + nb * 512: c2 * S + (nb + 1) * 512], start=True, stop=True),
                after=[("scalar", e.now("scalar")), ("vector", e.now("vector")), ("dmag", e.now("dmag"))])
            T("scalar", lambda nb=nb: nc.scalar.activation(
                out=shared[0:33, nb * 512:(nb + 1) * 512], in_=ps_g[0:33, 0:512], func=AF.Sigmoid, bias=bias33[:]),
              after=[("tensor", mm)])
        sgc = e.now("scalar")
        mvd = e.dma(grow[:], shared[32:33, :], after=[("scalar", sgc)], queue="gpsimd")
        T("vector", lambda h=h: nc.vector.tensor_scalar(out=grow[:], in0=grow[:], scalar1=gcl[0:1, h:h + 1],
                                                        scalar2=-1.0, op0=ALU.mult, op1=ALU.add),
          after=[("dmag", mvd)])
        T("vector", lambda: nc.vector.tensor_tensor(out=grow[:], in0=shared[0:1, :], in1=grow[:], op=ALU.mult))
        vg = T("vector", lambda: nc.vector.tensor_scalar(out=grow[:], in0=grow[:], scalar1=2.0, scalar2=None, op0=ALU.add))
        for qt in range(NT):
            T("tensor", lambda qt=qt: nc.tensor.matmul(
                out=ps_h[0:128, qt:qt + 1], lhsT=grow[0:1, qt * 128:(qt + 1) * 128], rhs=onef[:],
                start=True, stop=True),
              after=[("vector", vg), ("scalar", e.now("scalar"))] if qt == 0 else [])
        return T("scalar", lambda: nc.scalar.copy(out=gateT[:], in_=ps_h[0:128, 0:16]),
                 after=[("tensor", e.now("tensor"))])

    def l0_qk(h, qt):
        c2 = h // 2
        pb = 64 * (h % 2)
        win = bass.AP(TREV.tensor, h * 4096 + 128 * qt, [[1, 128], [1, S]])
        twd = e.dma(twin[:], win, after=[("dma", trev_ready), ("vector", e.now("vector"))])
        for kb in range(4):
            T("tensor", lambda c2=c2, pb=pb, qt=qt, kb=kb: nc.tensor.matmul(
                out=ps_k[kb][:], lhsT=qT[pb:pb + 64, c2 * S + qt * 128: c2 * S + (qt + 1) * 128],
                rhs=kT[pb:pb + 64, c2 * S + kb * 512: c2 * S + (kb + 1) * 512], start=True, stop=True),
              after=[("scalar", e.now("scalar")), ("vector", e.now("vector"))] if kb == 0 else [])
        return e.now("tensor"), twd

    def l0_soft(h, qt, mmc, twd, par):
        wnl = wnBA[par]
        for kb in range(4):
            T("vector", lambda qt=qt, kb=kb: nc.vector.scalar_tensor_tensor(
                out=ssb[:, kb * 512:(kb + 1) * 512], in0=rev_free(twin[:], S - (kb + 1) * 512, S - kb * 512),
                scalar=gateT[:, qt:qt + 1], in1=ps_k[kb][:], op0=ALU.mult, op1=ALU.add),
              after=[("tensor", mmc), ("dma", twd), ("scalar", e.now("scalar"))] if kb == 0 else [])
        T("vector", lambda: nc.vector.tensor_reduce(out=mcol[:], in_=ssb[:], axis=AX.X, op=ALU.max))
        T("vector", lambda: nc.vector.tensor_scalar(out=negm[:], in0=mcol[:], scalar1=-1.0, scalar2=None, op0=ALU.mult))
        ex = T("scalar", lambda wnl=wnl: nc.scalar.activation(out=wnl[:], in_=ssb[:], func=AF.Exp, bias=negm[:],
                                                              accum_out=zcol[:]),
               after=[("vector", e.now("vector"))])
        T("vector", lambda: nc.vector.reciprocal(out=rz[:], in_=zcol[:]), after=[("scalar", ex)])
        nw = T("vector", lambda wnl=wnl: nc.vector.tensor_scalar(out=wnl[:], in0=wnl[:], scalar1=rz[:],
                                                                 scalar2=None, op0=ALU.mult))
        return nw

    def l0_pv(h, qt, nw, par):
        wnl = wnBA[par]
        c2 = h // 2
        pb = 64 * (h % 2)
        for c in range(16):
            pst = ps_t if c < 8 else ps_t2
            T("tensor", lambda c=c, pst=pst, wnl=wnl: nc.tensor.transpose(
                out=pst[:, (c % 8) * 128:(c % 8 + 1) * 128], in_=wnl[:, c * 128:(c + 1) * 128], identity=ident[:]),
              after=[("vector", nw), ("scalar", e.now("scalar"))] if c == 0 else [])
        tcn = e.now("tensor")
        T("scalar", lambda: nc.scalar.copy(out=shared[:, 0:1024], in_=ps_t[:]), after=[("tensor", tcn)])
        ev2 = T("scalar", lambda: nc.scalar.copy(out=shared[:, 1024:2048], in_=ps_t2[:]))
        for c in range(16):
            T("tensor", lambda c=c, h=h: nc.tensor.matmul(
                out=ps_h[0:64, 0:128], lhsT=vtok_all[:, c * D + h * 64: c * D + h * 64 + 64],
                rhs=shared[:, c * 128:(c + 1) * 128], start=(c == 0), stop=(c == 15)),
              after=[("scalar", ev2)] if c == 0 else [])
        pvc = e.now("tensor")
        T("scalar", lambda c2=c2, pb=pb, qt=qt: nc.scalar.copy(
            out=ctxT[pb:pb + 64, c2 * S + qt * 128: c2 * S + (qt + 1) * 128], in_=ps_h[0:64, 0:128]),
          after=[("tensor", pvc)])

    unitsA = [(h, qt) for h in range(H) for qt in range(NT)]
    pend = None
    mmc = twd = None
    for i, (h, qt) in enumerate(unitsA):
        if qt == 0:
            if pend is not None:
                l0_pv(*pend)
                pend = None
            l0_gate_prep(h)
            mmc, twd = l0_qk(h, qt)
        nw = l0_soft(h, qt, mmc, twd, i % 2)
        if i + 1 < len(unitsA) and unitsA[i + 1][1] != 0:
            mmc, twd = l0_qk(*unitsA[i + 1])
        if pend is not None:
            l0_pv(*pend)
        pend = (h, qt, nw, i % 2)
    l0_pv(*pend)

    dbo = e.dma(rowbuf_bf[:], BO[:], after=[("tensor", e.now("tensor"))], queue="gpsimd")
    for half in range(2):
        wld = load_weight_half(WO, half)
        for t in range(NT):
            nb = half
            ps = ps_k[t % 4]
            deps = [("scalar", e.now("scalar")), ("vector", e.now("vector")), ("dmag", wld)]
            for ic in range(NC):
                T("tensor", lambda t=t, ic=ic, ps=ps: nc.tensor.matmul(
                    out=ps[:], lhsT=ctxT[:, ic * S + t * 128: ic * S + (t + 1) * 128],
                    rhs=w_bf[:, ic * 512:(ic + 1) * 512],
                    start=(ic == 0), stop=False), after=deps if ic == 0 else [])
            T("tensor", lambda half=half, ps=ps: nc.tensor.matmul(
                out=ps[:], lhsT=ones1[:], rhs=rowbuf_bf[0:1, half * 512:(half + 1) * 512],
                start=False, stop=True))
            mmc = e.now("tensor")
            sv3 = T("scalar", lambda ps=ps: nc.scalar.copy(out=scr32[:, 0:512], in_=ps[:]), after=[("tensor", mmc)])
            e.dma(ATTN0[t * 128:(t + 1) * 128, nb * 512:(nb + 1) * 512], scr32[:, 0:512], after=[("scalar", sv3)])

    return


def build_launch1(debug=False):
    e = Emitter("l1")
    emit_l0attn(e, fused=False, debug=debug)
    e.op("gpsimd", lambda: None, after=e.all_now())
    return e.emit_and_close()


def _l1_in_map(inputs, c):
    consts = _CONSTS
    m = {
        "x": inputs["L0_attn_x"][c],
        "wq": inputs["L0_attn_Wq"], "wk": inputs["L0_attn_Wk"],
        "wv": inputs["L0_attn_Wv"], "wo": inputs["L0_attn_Wo"],
        "bq": inputs["L0_attn_bq"].reshape(1, D), "bk": inputs["L0_attn_bk"].reshape(1, D),
        "bv": inputs["L0_attn_bv"].reshape(1, D), "bo": inputs["L0_attn_bo"].reshape(1, D),
        "rel": inputs["L0_attn_rel_embed"], "gruw": inputs["L0_attn_gru_w"],
        "grub": inputs["L0_attn_gru_b"].reshape(1, 8), "gruc": inputs["L0_attn_gru_const"].reshape(1, 16),
        "ohrt": consts["ohrt"], "grp33": consts["grp33"],
        "identf": consts["ident_bf"], "identf32": consts["ident_f32"],
    }
    return {k: np.ascontiguousarray(v) for k, v in m.items()}


def _ffn_in_map(inputs, c, stage, st):
    consts = _CONSTS
    m = {
        "x": inputs["L0_attn_x"][c],
        "xnmax8": st["xnmax8"],
        "wup": inputs["L0_ffn_w_up"], "wgate": inputs["L0_ffn_w_gate"],
        "identf32": consts["ident_f32"], "identf": consts["ident_bf"],
    }
    if stage >= 2:
        m["ugmax8"] = st["ugmax8"]
    if stage >= 3:
        m["g2max8"] = st["g2max8"]
    if stage >= 4:
        m["omax8"] = st["omax8"]
        m["wdown"] = inputs["L0_ffn_w_down"]
        m["attn0"] = st["attn0"][c]
    return {k: np.ascontiguousarray(v) for k, v in m.items()}


def run_l0(inputs):
    st = {}
    nc1 = build_launch1()
    res = run_bass_kernel_spmd(nc1, [_l1_in_map(inputs, c) for c in range(8)], core_ids=list(range(8)))
    st["attn0"] = np.stack([np.asarray(res.results[c]["attn0"]) for c in range(8)])
    st["xnmax8"] = np.concatenate([np.asarray(res.results[c]["xnmax"]) for c in range(8)], axis=1)

    nc = build_l0ffn(1)
    res = run_bass_kernel_spmd(nc, [_ffn_in_map(inputs, c, 1, st) for c in range(8)], core_ids=list(range(8)))
    st["ugmax8"] = np.concatenate([np.asarray(res.results[c]["ugmax"]) for c in range(8)], axis=1)

    nc = build_l0ffn(2)
    res = run_bass_kernel_spmd(nc, [_ffn_in_map(inputs, c, 2, st) for c in range(8)], core_ids=list(range(8)))
    st["g2max8"] = np.concatenate([np.asarray(res.results[c]["g2max"]) for c in range(8)], axis=1)

    nc = build_l0ffn(3)
    res = run_bass_kernel_spmd(nc, [_ffn_in_map(inputs, c, 3, st) for c in range(8)], core_ids=list(range(8)))
    st["omax8"] = np.concatenate([np.asarray(res.results[c]["omax"]) for c in range(8)], axis=1)

    nc = build_l0ffn(4)
    res = run_bass_kernel_spmd(nc, [_ffn_in_map(inputs, c, 4, st) for c in range(8)], core_ids=list(range(8)))
    x1 = np.stack([np.asarray(res.results[c]["x1"]) for c in range(8)])
    return x1


def _l6_in_map(inputs, c, x1):
    consts = _CONSTS
    gn = _gumbel_noise()
    m = {
        "x1": x1[c] if x1 is not None else np.zeros((S, D), np.float32),
        "awq": inputs["L1_attn_Wq"], "awk": inputs["L1_attn_Wk"],
        "awv": inputs["L1_attn_Wv"], "awo": inputs["L1_attn_Wo"],
        "abq": inputs["L1_attn_bq"].reshape(1, D), "abk": inputs["L1_attn_bk"].reshape(1, D),
        "abv": inputs["L1_attn_bv"].reshape(1, D), "abo": inputs["L1_attn_bo"].reshape(1, D),
        "scale1": inputs["L1_ffn_scale1"].reshape(1, D), "scale2": inputs["L1_ffn_scale2"].reshape(1, D),
        "fwq": inputs["L1_ffn_wq"][:, consts["rope_perm"]], "fwk": inputs["L1_ffn_wk"][:, consts["rope_perm"]],
        "fwv": inputs["L1_ffn_wv"], "fwo": inputs["L1_ffn_wo"],
        "router": inputs["L1_ffn_router_w"],
        "w1": inputs["L1_ffn_w1"].reshape(8 * D, D), "w2": inputs["L1_ffn_w2"].reshape(8 * D, D),
        "wout": inputs["L1_ffn_wout"].reshape(8 * D, D),
        "b1": inputs["L1_ffn_b1"], "b2": inputs["L1_ffn_b2"], "bout": inputs["L1_ffn_bout"],
        "gnoise": gn[c], "cos32": consts["cos32"], "sin32": consts["sin32"],
        "identf": consts["ident_bf"], "identf32": consts["ident_f32"], "caus": consts["caus"],
    }
    return {k: np.ascontiguousarray(v) for k, v in m.items()}


def kernel_multi(**inputs):
    inputs = {k: np.asarray(v, dtype=np.float32) for k, v in inputs.items()}
    x1 = run_l0(inputs)
    nc6 = build_l1()
    res = run_bass_kernel_spmd(nc6, [_l6_in_map(inputs, c, x1) for c in range(8)], core_ids=list(range(8)))
    x2 = np.stack([np.asarray(res.results[c]["x2"]) for c in range(8)])
    return x2


def kernel(**inputs):
    return kernel_fused(**inputs)



def build_l0ffn(stage, debug=False):
    e = Emitter(f"ffn{stage}")
    nc = e.nc
    x = nc.dram_tensor("x", [S, D], F32, kind="ExternalInput").ap()
    XNMAX8 = nc.dram_tensor("xnmax8", [1, 8], F32, kind="ExternalInput").ap()
    WUP = nc.dram_tensor("wup", [3072, D], F32, kind="ExternalInput").ap()
    WGATE = nc.dram_tensor("wgate", [3072, D], F32, kind="ExternalInput").ap()
    IDF32 = nc.dram_tensor("identf32", [128, 128], F32, kind="ExternalInput").ap()
    IDENT = nc.dram_tensor("identf", [128, 128], BF16, kind="ExternalInput").ap()
    if stage >= 2:
        UGMAX8 = nc.dram_tensor("ugmax8", [2, 8], F32, kind="ExternalInput").ap()
    if stage >= 3:
        G2MAX8 = nc.dram_tensor("g2max8", [1, 8], F32, kind="ExternalInput").ap()
    if stage >= 4:
        OMAX8 = nc.dram_tensor("omax8", [1, 8], F32, kind="ExternalInput").ap()
        WDOWN = nc.dram_tensor("wdown", [D, 3072], F32, kind="ExternalInput").ap()
        ATTN0 = nc.dram_tensor("attn0", [S, D], F32, kind="ExternalInput").ap()
        X1 = nc.dram_tensor("x1", [S, D], F32, kind="ExternalOutput").ap()
        ODRAM = nc.dram_tensor("od", [3072, S], BF16, kind="Internal").ap()
    if stage == 1:
        UGMAX = nc.dram_tensor("ugmax", [2, 1], F32, kind="ExternalOutput").ap()
    if stage == 2:
        G2MAX = nc.dram_tensor("g2max", [1, 1], F32, kind="ExternalOutput").ap()
    if stage == 3:
        OMAX = nc.dram_tensor("omax", [1, 1], F32, kind="ExternalOutput").ap()
    if debug:
        UPQ_D = nc.dram_tensor("upq_dbg", [3072, S], F32, kind="ExternalOutput").ap()
        G2_D = nc.dram_tensor("g2_dbg", [3072, S], F32, kind="ExternalOutput").ap()

    xa = e.sb("xa", [128, D])
    xqT = e.sb("xqT", [128, NC * S], BF16)
    scr32 = e.sb("scr32", [128, D])
    scr_bf = e.sb("scr_bf", [128, D], BF16)
    w32 = e.sb("w32", [128, D])
    wbf = e.sb("wbf", [128, D], BF16)
    wT = e.sb("wT", [128, NC * 128], BF16)
    bm = e.sb("bm", [128, 32])
    rbm = e.sb("rbm", [128, 32])
    upf = e.sb("upf", [128, S])
    gf = e.sb("gf", [128, S])
    sigf = e.sb("sigf", [128, S])
    obf = e.sb("obf", [128, S], BF16)
    mcol = e.sb("mcol", [128, 1])
    vcol = e.sb("vcol", [128, 1])
    tcol = e.sb("tcol", [128, 1])
    mx_up = e.sb("mx_up", [128, 1])
    mx_g = e.sb("mx_g", [128, 1])
    row8 = e.sb("row8", [2, 8])
    s_sc = e.sb("s_sc", [2, 1])
    scol_xn = e.sb("scol_xn", [128, 1])
    scol_up = e.sb("scol_up", [128, 1])
    scol_g = e.sb("scol_g", [128, 1])
    scol_g2 = e.sb("scol_g2", [128, 1])
    scol_o = e.sb("scol_o", [128, 1])
    onef32 = e.sb("onef32", [1, 128])
    idf32 = e.sb("idf32", [128, 128])
    identb = e.sb("identb", [128, 128], BF16)
    mrow = e.sb("mrow", [1, 128])

    ps_k = [e.psum(f"ps_{i}", [128, 512], F32) for i in range(4)]
    ps_g2 = [e.psum(f"psg_{i}", [128, 512], F32) for i in range(2)]
    ps_t = e.psum("ps_t", [128, 1024], BF16)
    ps_h = e.psum("ps_h", [128, 512], F32)

    T = e.op
    e.dma(idf32[:], IDF32[:])
    e.dma(identb[:], IDENT[:])
    T("vector", lambda: nc.vector.memset(onef32[:], 1.0))

    def bcast_col(src11, dst_col, dep):
        mm = T("tensor", lambda: nc.tensor.matmul(out=ps_h[0:128, 0:1], lhsT=onef32[:], rhs=src11,
                                                  start=True, stop=True),
               after=dep + [("scalar", e.now("scalar")), ("dma", e.now("dma"))])
        return T("scalar", lambda: nc.scalar.copy(out=dst_col[:], in_=ps_h[0:128, 0:1]), after=[("tensor", mm)])

    def scale_from_row(src_row, dst_col):
        d = e.dma(row8[0:1, :], src_row, after=[("vector", e.now("vector")), ("tensor", e.now("tensor"))],
                  queue="gpsimd")
        v1 = T("vector", lambda: nc.vector.tensor_reduce(out=s_sc[0:1, :], in_=row8[0:1, :], axis=AX.X, op=ALU.max),
               after=[("dmag", d), ("tensor", e.now("tensor"))])
        T("vector", lambda: nc.vector.tensor_scalar(out=s_sc[0:1, :], in0=s_sc[0:1, :], scalar1=1.0 / 32767.0,
                                                    scalar2=1e-12, op0=ALU.mult, op1=ALU.max))
        return bcast_col(s_sc[0:1, :], dst_col, [("vector", e.now("vector"))])

    scale_from_row(XNMAX8[0:1, :], scol_xn)
    if stage >= 2:
        scale_from_row(UGMAX8[0:1, :], scol_up)
        scale_from_row(UGMAX8[1:2, :], scol_g)
    if stage >= 3:
        scale_from_row(G2MAX8[0:1, :], scol_g2)
    if stage >= 4:
        scale_from_row(OMAX8[0:1, :], scol_o)

    rcol = e.sb("rcol", [128, 1])

    def qdq_inplace(ap, scol):
        T("vector", lambda: nc.vector.reciprocal(out=rcol[:], in_=scol[:]),
          after=[("scalar", e.now("scalar"))])
        T("vector", lambda: nc.vector.tensor_scalar(out=ap, in0=ap, scalar1=rcol[:], scalar2=None, op0=ALU.mult))
        T("vector", lambda: nc.vector.tensor_scalar(out=ap, in0=ap, scalar1=MAGIC, scalar2=-MAGIC, op0=ALU.add, op1=ALU.add))
        T("vector", lambda: nc.vector.tensor_scalar(out=ap, in0=ap, scalar1=scol[:], scalar2=None, op0=ALU.mult))

    for t in range(NT):
        dx = e.dma(xa[:], x[t * 128:(t + 1) * 128, :], after=[("scalar", e.now("scalar")), ("vector", e.now("vector"))])
        T("vector", lambda: nc.vector.reduce_sum(out=mcol[:], in_=xa[:], axis=AX.X), after=[("dma", dx)])
        sq = T("scalar", lambda: nc.scalar.activation(out=scr32[:], in_=xa[:], func=AF.Square, accum_out=vcol[:]),
               after=[("dma", dx)])
        T("vector", lambda: nc.vector.tensor_scalar(out=mcol[:], in0=mcol[:], scalar1=1.0 / D, scalar2=None, op0=ALU.mult))
        T("vector", lambda: nc.vector.tensor_scalar(out=vcol[:], in0=vcol[:], scalar1=1.0 / D, scalar2=None, op0=ALU.mult),
          after=[("scalar", sq)])
        T("vector", lambda: nc.vector.tensor_scalar(out=tcol[:], in0=mcol[:], scalar1=mcol[:], scalar2=None, op0=ALU.mult))
        T("vector", lambda: nc.vector.tensor_sub(out=vcol[:], in0=vcol[:], in1=tcol[:]))
        vv = T("vector", lambda: nc.vector.tensor_scalar(out=vcol[:], in0=vcol[:], scalar1=1e-5, scalar2=None, op0=ALU.add))
        sv = T("scalar", lambda: nc.scalar.activation(out=vcol[:], in_=vcol[:], func=AF.Sqrt), after=[("vector", vv)])
        T("vector", lambda: nc.vector.reciprocal(out=vcol[:], in_=vcol[:]), after=[("scalar", sv)])
        T("vector", lambda: nc.vector.tensor_scalar(out=xa[:], in0=xa[:], scalar1=mcol[:], scalar2=vcol[:],
                                                    op0=ALU.subtract, op1=ALU.mult))
        qdq_inplace(xa[:], scol_xn)
        cs = T("scalar", lambda: nc.scalar.copy(out=scr_bf[:], in_=xa[:]),
               after=[("vector", e.now("vector")), ("tensor", e.now("tensor"))])
        for ic in range(NC):
            T("tensor", lambda ic=ic: nc.tensor.transpose(
                out=ps_t[:, ic * 128:(ic + 1) * 128], in_=scr_bf[:, ic * 128:(ic + 1) * 128], identity=identb[:]),
              after=[("scalar", cs)] if ic == 0 else [])
        tc = e.now("tensor")
        for ic in range(NC):
            T("scalar", lambda t=t, ic=ic: nc.scalar.copy(
                out=xqT[:, ic * S + t * 128: ic * S + (t + 1) * 128], in_=ps_t[:, ic * 128:(ic + 1) * 128]),
              after=[("tensor", tc)] if ic == 0 else [])
    T("vector", lambda: nc.vector.memset(mx_up[:], 0.0))
    T("vector", lambda: nc.vector.memset(mx_g[:], 0.0))

    def dequant_to_wT(Wsrc, oc, icols=D):
        nblk = icols // 32
        d = e.dma(w32[:, 0:icols], Wsrc[oc * 128:(oc + 1) * 128, :],
                  after=[("vector", e.now("vector")), ("tensor", e.now("tensor"))])
        T("vector", lambda: nc.vector.tensor_reduce(
            out=bm[:, 0:nblk], in_=w32[:, 0:icols].rearrange("p (b k) -> p b k", k=32),
            axis=AX.X, op=ALU.max, apply_absolute_value=True), after=[("dma", d)])
        T("vector", lambda: nc.vector.tensor_scalar(out=bm[:, 0:nblk], in0=bm[:, 0:nblk], scalar1=1.0 / 7.0,
                                                    scalar2=1e-12, op0=ALU.mult, op1=ALU.max))
        w3 = w32[:, 0:icols].rearrange("p (b k) -> p b k", k=32)
        T("vector", lambda: nc.vector.reciprocal(out=rbm[:, 0:nblk], in_=bm[:, 0:nblk]))
        bmap = bm[:, 0:nblk]
        b3 = bass.AP(bmap.tensor, bmap.offset, [list(bmap.ap[0]), [bmap.ap[1][0], nblk], [0, 32]])
        rmap = rbm[:, 0:nblk]
        r3 = bass.AP(rmap.tensor, rmap.offset, [list(rmap.ap[0]), [rmap.ap[1][0], nblk], [0, 32]])
        T("vector", lambda: nc.vector.tensor_tensor(out=w3, in0=w3, in1=r3, op=ALU.mult))
        T("vector", lambda: nc.vector.tensor_scalar(out=w32[:, 0:icols], in0=w32[:, 0:icols], scalar1=MAGIC,
                                                    scalar2=-MAGIC, op0=ALU.add, op1=ALU.add))
        wv = T("vector", lambda: nc.vector.tensor_tensor(out=w3, in0=w3, in1=b3, op=ALU.mult))
        cb = T("scalar", lambda: nc.scalar.copy(out=wbf[:, 0:icols], in_=w32[:, 0:icols]), after=[("vector", wv)])
        nic = icols // 128
        for ic in range(nic):
            T("tensor", lambda ic=ic: nc.tensor.transpose(
                out=ps_t[:, (ic % 8) * 128:(ic % 8 + 1) * 128], in_=wbf[:, ic * 128:(ic + 1) * 128], identity=identb[:]),
              after=[("scalar", cb), ("scalar", e.now("scalar"))] if ic == 0 else [])
        tc = e.now("tensor")
        for ic in range(nic):
            T("scalar", lambda ic=ic: nc.scalar.copy(
                out=wT[:, ic * 128:(ic + 1) * 128], in_=ps_t[:, (ic % 8) * 128:(ic % 8 + 1) * 128]),
              after=[("tensor", tc)] if ic == 0 else [])
        return e.now("scalar")

    for oc in range(24):
        wld = dequant_to_wT(WGATE, oc)
        for nb in range(4):
            ps = ps_g2[nb % 2]
            deps = [("scalar", e.now("scalar")), ("vector", e.now("vector"))]
            for ic in range(NC):
                T("tensor", lambda ic=ic, nb=nb, ps=ps: nc.tensor.matmul(
                    out=ps[:], lhsT=wT[:, ic * 128:(ic + 1) * 128],
                    rhs=xqT[:, ic * S + nb * 512: ic * S + (nb + 1) * 512],
                    start=(ic == 0), stop=(ic == NC - 1)), after=deps if ic == 0 else [])
            mmc = e.now("tensor")
            if stage == 1:
                T("vector", lambda ps=ps: nc.vector.tensor_reduce(out=tcol[:], in_=ps[:], axis=AX.X,
                                                                  op=ALU.max, apply_absolute_value=True),
                  after=[("tensor", mmc)])
                T("vector", lambda: nc.vector.tensor_max(out=mx_g[:], in0=mx_g[:], in1=tcol[:]))
            else:
                T("scalar", lambda nb=nb, ps=ps: nc.scalar.copy(out=gf[:, nb * 512:(nb + 1) * 512], in_=ps[:]),
                  after=[("tensor", mmc)])
        if stage >= 2:
            qdq_inplace(gf[:], scol_g)
            sg = T("scalar", lambda: nc.scalar.activation(out=sigf[:], in_=gf[:], func=AF.Sigmoid),
                   after=[("vector", e.now("vector"))])
            T("vector", lambda: nc.vector.tensor_scalar(out=sigf[:], in0=sigf[:], scalar1=65536.0, scalar2=MAGIC,
                                                        op0=ALU.mult, op1=ALU.add), after=[("scalar", sg)])
            T("vector", lambda: nc.vector.tensor_scalar(out=sigf[:], in0=sigf[:], scalar1=-MAGIC, scalar2=65535.0,
                                                        op0=ALU.add, op1=ALU.min))
            g2v = T("vector", lambda: nc.vector.scalar_tensor_tensor(out=gf[:], in0=sigf[:], scalar=1.0 / 65536.0,
                                                                     in1=gf[:], op0=ALU.mult, op1=ALU.mult))
            if stage == 2:
                T("vector", lambda: nc.vector.tensor_reduce(out=tcol[:], in_=gf[:], axis=AX.X,
                                                            op=ALU.max, apply_absolute_value=True))
                T("vector", lambda: nc.vector.tensor_max(out=mx_g[:], in0=mx_g[:], in1=tcol[:]))
        if stage >= 3:
            qdq_inplace(gf[:], scol_g2)
        if stage == 1 or stage >= 3:
            wld = dequant_to_wT(WUP, oc)
            for nb in range(4):
                ps = ps_k[nb % 2 + 2]
                deps = [("scalar", e.now("scalar")), ("vector", e.now("vector"))]
                for ic in range(NC):
                    T("tensor", lambda ic=ic, nb=nb, ps=ps: nc.tensor.matmul(
                        out=ps[:], lhsT=wT[:, ic * 128:(ic + 1) * 128],
                        rhs=xqT[:, ic * S + nb * 512: ic * S + (nb + 1) * 512],
                        start=(ic == 0), stop=(ic == NC - 1)), after=deps if ic == 0 else [])
                mmc = e.now("tensor")
                if stage == 1:
                    T("vector", lambda ps=ps: nc.vector.tensor_reduce(out=tcol[:], in_=ps[:], axis=AX.X,
                                                                      op=ALU.max, apply_absolute_value=True),
                      after=[("tensor", mmc)])
                    T("vector", lambda: nc.vector.tensor_max(out=mx_up[:], in0=mx_up[:], in1=tcol[:]))
                else:
                    T("scalar", lambda nb=nb, ps=ps: nc.scalar.copy(out=upf[:, nb * 512:(nb + 1) * 512], in_=ps[:]),
                      after=[("tensor", mmc)])
        if stage >= 3:
            qdq_inplace(upf[:], scol_up)
            if debug:
                dmp = e.dma(UPQ_D[oc * 128:(oc + 1) * 128, :], upf[:], after=[("vector", e.now("vector"))])
                T("vector", lambda: nc.vector.tensor_copy(out=tcol[:], in_=tcol[:]), after=[("dma", dmp)])
            ov = T("vector", lambda: nc.vector.tensor_mul(out=gf[:], in0=gf[:], in1=upf[:]))
            if stage == 3:
                T("vector", lambda: nc.vector.tensor_reduce(out=tcol[:], in_=gf[:], axis=AX.X,
                                                            op=ALU.max, apply_absolute_value=True))
                T("vector", lambda: nc.vector.tensor_max(out=mx_g[:], in0=mx_g[:], in1=tcol[:]))
            if debug:
                dmp = e.dma(G2_D[oc * 128:(oc + 1) * 128, :], gf[:], after=[("vector", e.now("vector"))])
                T("vector", lambda: nc.vector.tensor_copy(out=tcol[:], in_=tcol[:]), after=[("dma", dmp)])
        if stage >= 4:
            qdq_inplace(gf[:], scol_o)
            ob = T("vector", lambda: nc.vector.tensor_copy(out=obf[:], in_=gf[:]), after=[("dma", e.now("dma"))])
            e.dma(ODRAM[oc * 128:(oc + 1) * 128, :], obf[:], after=[("vector", ob)])

    def col_to_scalar_max(col, dst11):
        mm = T("tensor", lambda: nc.tensor.matmul(out=ps_h[0:1, 0:128], lhsT=col[:], rhs=idf32[:],
                                                  start=True, stop=True),
               after=[("vector", e.now("vector")), ("scalar", e.now("scalar"))])
        sc = T("scalar", lambda: nc.scalar.copy(out=mrow[:], in_=ps_h[0:1, 0:128]), after=[("tensor", mm)])
        return T("vector", lambda: nc.vector.tensor_reduce(out=dst11, in_=mrow[:], axis=AX.X, op=ALU.max),
                 after=[("scalar", sc)])

    if stage == 1:
        v1 = col_to_scalar_max(mx_up, s_sc[0:1, 0:1])
        e.dma(UGMAX[0:1, :], s_sc[0:1, 0:1], after=[("vector", v1)])
        v2 = col_to_scalar_max(mx_g, mrow[0:1, 0:1])
        e.dma(UGMAX[1:2, :], mrow[0:1, 0:1], after=[("vector", v2)])
    if stage == 2:
        v2 = col_to_scalar_max(mx_g, s_sc[0:1, 0:1])
        e.dma(G2MAX[:], s_sc[0:1, 0:1], after=[("vector", v2)])
    if stage == 3:
        v2 = col_to_scalar_max(mx_g, s_sc[0:1, 0:1])
        e.dma(OMAX[:], s_sc[0:1, 0:1], after=[("vector", v2)])

    if stage >= 4:
        wdT = e.sb("wdT", [128, 24 * 1024], BF16)
        for c in range(NC):
            d = e.dma(w32[:, 0:D] if False else w32[:], WDOWN[c * 128:(c + 1) * 128, 0:1024],
                      after=[("vector", e.now("vector")), ("tensor", e.now("tensor"))])
            for g3 in range(3):
                dg = e.dma(w32[:], WDOWN[c * 128:(c + 1) * 128, g3 * 1024:(g3 + 1) * 1024],
                           after=[("vector", e.now("vector")), ("tensor", e.now("tensor"))]) if g3 > 0 else d
                nblk = 32
                T("vector", lambda: nc.vector.tensor_reduce(
                    out=bm[:, 0:nblk], in_=w32[:].rearrange("p (b k) -> p b k", k=32),
                    axis=AX.X, op=ALU.max, apply_absolute_value=True), after=[("dma", dg)])
                T("vector", lambda: nc.vector.tensor_scalar(out=bm[:, 0:nblk], in0=bm[:, 0:nblk], scalar1=1.0 / 7.0,
                                                            scalar2=1e-12, op0=ALU.mult, op1=ALU.max))
                w3 = w32[:].rearrange("p (b k) -> p b k", k=32)
                T("vector", lambda: nc.vector.reciprocal(out=rbm[:, 0:nblk], in_=bm[:, 0:nblk]))
                bmap = bm[:, 0:nblk]
                b3 = bass.AP(bmap.tensor, bmap.offset, [list(bmap.ap[0]), [bmap.ap[1][0], nblk], [0, 32]])
                rmap = rbm[:, 0:nblk]
                r3 = bass.AP(rmap.tensor, rmap.offset, [list(rmap.ap[0]), [rmap.ap[1][0], nblk], [0, 32]])
                T("vector", lambda w3=w3, r3=r3: nc.vector.tensor_tensor(out=w3, in0=w3, in1=r3, op=ALU.mult))
                T("vector", lambda: nc.vector.tensor_scalar(out=w32[:], in0=w32[:], scalar1=MAGIC,
                                                            scalar2=-MAGIC, op0=ALU.add, op1=ALU.add))
                wv = T("vector", lambda w3=w3, b3=b3: nc.vector.tensor_tensor(out=w3, in0=w3, in1=b3, op=ALU.mult))
                cb = T("scalar", lambda: nc.scalar.copy(out=wbf[:], in_=w32[:]), after=[("vector", wv)])
                for ic in range(8):
                    T("tensor", lambda ic=ic: nc.tensor.transpose(
                        out=ps_t[:, ic * 128:(ic + 1) * 128], in_=wbf[:, ic * 128:(ic + 1) * 128], identity=identb[:]),
                      after=[("scalar", cb), ("scalar", e.now("scalar"))] if ic == 0 else [])
                tc = e.now("tensor")
                for ic in range(8):
                    icg = g3 * 8 + ic
                    T("scalar", lambda icg=icg, ic=ic, c=c: nc.scalar.copy(
                        out=wdT[:, icg * 1024 + c * 128: icg * 1024 + (c + 1) * 128],
                        in_=ps_t[:, ic * 128:(ic + 1) * 128]),
                      after=[("tensor", tc)] if ic == 0 else [])
        wd_done = e.now("scalar")
        oT_t = e.sb("oT_t", [128, 24 * 128], BF16)
        res32 = e.sb("res32", [128, D])
        for t in range(NT):
            dl = []
            for icg in range(24):
                dl.append(e.dma(oT_t[:, icg * 128:(icg + 1) * 128],
                                ODRAM[icg * 128:(icg + 1) * 128, t * 128:(t + 1) * 128],
                                after=[("tensor", e.now("tensor"))] if icg == 0 else []))
            dx = e.dma(res32[:], x[t * 128:(t + 1) * 128, :], after=[("vector", e.now("vector"))])
            dat = e.dma(xa[:], ATTN0[t * 128:(t + 1) * 128, :])
            av = T("vector", lambda: nc.vector.tensor_add(out=res32[:], in0=res32[:], in1=xa[:]),
                   after=[("dma", e.now("dma"))])
            for nb in range(2):
                ps = ps_k[nb]
                deps = [("scalar", wd_done), ("vector", av), ("dma", dl[-1])]
                for icg in range(24):
                    T("tensor", lambda icg=icg, nb=nb, ps=ps: nc.tensor.matmul(
                        out=ps[:], lhsT=oT_t[:, icg * 128:(icg + 1) * 128],
                        rhs=wdT[:, icg * 1024 + nb * 512: icg * 1024 + (nb + 1) * 512],
                        start=(icg == 0), stop=(icg == 23)), after=deps if icg == 0 else [])
                mmc = e.now("tensor")
                fv = T("vector", lambda t=t, nb=nb, ps=ps: nc.vector.tensor_add(
                    out=res32[:, nb * 512:(nb + 1) * 512], in0=res32[:, nb * 512:(nb + 1) * 512], in1=ps[:]),
                    after=[("tensor", mmc)])
                e.dma(X1[t * 128:(t + 1) * 128, nb * 512:(nb + 1) * 512], res32[:, nb * 512:(nb + 1) * 512],
                      after=[("vector", fv)])

    T("gpsimd", lambda: None, after=e.all_now())
    return e.emit_and_close()


def emit_l1(e, fused=False, debug=False):
    nc = e.nc
    e.phase("C")
    X1I = e.dr("x1", [S, D], F32, kind="Internal" if fused else "ExternalInput")
    AWQ = e.dr("awq", [D, D], F32)
    AWK = e.dr("awk", [D, D], F32)
    AWV = e.dr("awv", [D, D], F32)
    AWO = e.dr("awo", [D, D], F32)
    ABQ = e.dr("abq", [1, D], F32)
    ABK = e.dr("abk", [1, D], F32)
    ABV = e.dr("abv", [1, D], F32)
    ABO = e.dr("abo", [1, D], F32)
    SC1 = e.dr("scale1", [1, D], F32)
    SC2 = e.dr("scale2", [1, D], F32)
    FWQ = e.dr("fwq", [D, D], F32)
    FWK = e.dr("fwk", [D, D], F32)
    FWV = e.dr("fwv", [D, D], F32)
    FWO = e.dr("fwo", [D, D], F32)
    ROUT = e.dr("router", [D, 8], F32)
    W1 = e.dr("w1", [8 * D, D], F32)
    W2 = e.dr("w2", [8 * D, D], F32)
    WOUT = e.dr("wout", [8 * D, D], F32)
    B1 = e.dr("b1", [8, D], F32)
    B2 = e.dr("b2", [8, D], F32)
    BOUT = e.dr("bout", [8, D], F32)
    GN = e.dr("gnoise", [S, 8], F32)
    COS32 = e.dr("cos32", [128, S], BF16)
    SIN32 = e.dr("sin32", [128, S], BF16)
    IDENT = e.dr("identf", [128, 128], BF16)
    IDF32 = e.dr("identf32", [128, 128], F32)
    CAUS = e.dr("caus", [128, 4 * 512], BF16)

    X2O = e.dr("x2", [S, D], F32, kind="ExternalOutput")
    XN1 = e.dr("xn1_d", [S, D], F32, kind="Internal")
    A1 = e.dr("a1_d", [S, D], F32, kind="Internal")
    H1 = e.dr("h1_d", [S, D], F32, kind="Internal")
    MOE = e.dr("moe_d", [S, D], F32, kind="Internal")
    if debug:
        A1_D = e.dr("a1_dbg", [S, D], F32, kind="ExternalOutput")
        H1_D = e.dr("h1_dbg", [S, D], F32, kind="ExternalOutput")
        MOE_D = e.dr("moe_dbg", [S, D], F32, kind="ExternalOutput")
        V38_D = e.dr("v38_dbg", [2, 128], F32, kind="ExternalOutput")
        WN_D = e.dr("wn_dbg", [128, S], BF16, kind="ExternalOutput")
        CTX_D = e.dr("ctx_dbg", [D, S], BF16, kind="ExternalOutput")
        PRB_D = e.dr("prb_dbg", [128, 8], F32, kind="ExternalOutput")
        WTS_D = e.dr("wts_dbg", [128, 128], F32, kind="ExternalOutput")

    xa = e.sb("xa", [128, D])
    xb2 = e.sb("xb2", [128, D])
    scr32 = e.sb("scr32", [128, D])
    scr_bf = e.sb("scr_bf", [128, D], BF16)
    xnT = e.sb("xnT", [128, NC * S], BF16)
    qT = e.sb("qT", [128, NC * S], BF16)
    kT = e.sb("kT", [128, NC * S], BF16)
    vtok = e.sb("vtok", [128, NT * D], BF16)
    ctxT = e.sb("ctxT", [128, NC * S], BF16)
    wfull = e.sb("wfull", [128, NC * D], BF16)
    ssb = e.sb("ssb", [128, S])
    sel = e.sb_at("sel", [128, S], F32, xnT)
    wn = e.sb_at("wn", [128, S], BF16, wfull, 0)
    shared = e.sb_at("shared", [128, S], BF16, wfull, 8192)
    causb = e.sb("causb", [128, 4 * 512], BF16)
    ropeA = e.sb_at("ropeA", [128, S], BF16, wfull, 0)
    ropeB = e.sb_at("ropeB", [128, S], BF16, wfull, 4096)
    ropeC = e.sb_at("ropeC", [128, S], BF16, wfull, 8192)
    ropeD = e.sb_at("ropeD", [128, S], BF16, wfull, 12288)
    cosb = e.sb_at("cosb", [128, S], BF16, ctxT, 0)
    sinb = e.sb_at("sinb", [128, S], BF16, ctxT, 4096)
    m8 = e.sb("m8", [128, 8])
    mcol = e.sb("mcol", [128, 1])
    vcol = e.sb("vcol", [128, 1])
    tcol = e.sb("tcol", [128, 1])
    zb0 = e.sb("zb0", [128, 1])
    zcol = e.sb("zcol", [128, 1])
    rz = e.sb("rz", [128, 1])
    negm = e.sb("negm", [128, 1])
    bqc = e.sb("bqc", [128, NC])
    bkc = e.sb("bkc", [128, NC])
    s1b = e.sb_at("s1b", [128, D], F32, qT, 0)
    wts_all = e.sb("wts_all", [128, NT * 8])
    prb = e.sb("prb", [128, 8])
    gn_t = e.sb("gn_t", [128, 8])
    rowbuf_bf = e.sb_at("rowbuf_bf", [1, D], BF16, scr_bf, 0)
    rowb2 = e.sb_at("rowb2", [1, D], BF16, xb2, 0)
    boutall = e.sb_at("boutall", [8, D], BF16, xb2, 2048)
    wtsT = e.sb("wtsT", [8, 128], BF16)
    a1s = e.sb_at("a1s", [128, D], BF16, ssb, 0)
    a2s = e.sb_at("a2s", [128, D], BF16, ssb, 2048)
    heT = e.sb_at("heT", [128, D], BF16, ssb, 4096)
    ident = e.sb("ident", [128, 128], BF16)
    idf32 = e.sb("idf32", [128, 128])
    ones1 = e.sb("ones1", [1, 128], BF16)
    onef32 = e.sb("onef32", [1, 128])

    ps_k = [e.psum(f"ps_{i}", [128, 512], F32) for i in range(4)]
    ps_t = e.psum("ps_t", [128, 1024], BF16)
    ps_t2 = e.psum("ps_t2", [128, 1024], BF16)
    ps_g2 = [e.psum(f"psg_{i}", [128, 512], F32) for i in range(2)]

    T = e.op
    e.dma(ident[:], IDENT[:])
    e.dma(idf32[:], IDF32[:])
    ld0 = e.now("dma")
    T("vector", lambda: nc.vector.memset(ones1[:], 1.0))
    T("vector", lambda: nc.vector.memset(onef32[:], 1.0))
    T("vector", lambda: nc.vector.memset(zb0[:], 0.0))
    for c in range(NC):
        e.dma(bqc[:, c:c + 1], ABQ[0:1, c * 128:(c + 1) * 128])
        e.dma(bkc[:, c:c + 1], ABK[0:1, c * 128:(c + 1) * 128])
    ld_b = e.now("dma")

    def ln_stream(SRC, DSTDRAM, eps, rms=False, scale_b=None):
        for t in range(NT):
            dx = e.dma(xa[:], SRC[t * 128:(t + 1) * 128, :],
                       after=[("scalar", e.now("scalar")), ("vector", e.now("vector")), ("dma", e.now("dma"))])
            if not rms:
                T("vector", lambda: nc.vector.reduce_sum(out=mcol[:], in_=xa[:], axis=AX.X), after=[("dma", dx)])
            sq = T("scalar", lambda: nc.scalar.activation(out=scr32[:], in_=xa[:], func=AF.Square, accum_out=vcol[:]),
                   after=[("dma", dx)])
            if not rms:
                T("vector", lambda: nc.vector.tensor_scalar(out=mcol[:], in0=mcol[:], scalar1=1.0 / D, scalar2=None, op0=ALU.mult))
            T("vector", lambda: nc.vector.tensor_scalar(out=vcol[:], in0=vcol[:], scalar1=1.0 / D, scalar2=None, op0=ALU.mult),
              after=[("scalar", sq)])
            if not rms:
                T("vector", lambda: nc.vector.tensor_scalar(out=tcol[:], in0=mcol[:], scalar1=mcol[:], scalar2=None, op0=ALU.mult))
                T("vector", lambda: nc.vector.tensor_sub(out=vcol[:], in0=vcol[:], in1=tcol[:]))
            vv = T("vector", lambda: nc.vector.tensor_scalar(out=vcol[:], in0=vcol[:], scalar1=eps, scalar2=None, op0=ALU.add))
            sv = T("scalar", lambda: nc.scalar.activation(out=vcol[:], in_=vcol[:], func=AF.Sqrt), after=[("vector", vv)])
            T("vector", lambda: nc.vector.reciprocal(out=vcol[:], in_=vcol[:]), after=[("scalar", sv)])
            if rms:
                T("vector", lambda: nc.vector.tensor_scalar(out=xa[:], in0=xa[:], scalar1=vcol[:], scalar2=None, op0=ALU.mult))
                T("vector", lambda: nc.vector.tensor_mul(out=xa[:], in0=xa[:], in1=scale_b[:]))
            else:
                T("vector", lambda: nc.vector.tensor_scalar(out=xa[:], in0=xa[:], scalar1=mcol[:], scalar2=vcol[:],
                                                            op0=ALU.subtract, op1=ALU.mult))
            lv = e.now("vector")
            if DSTDRAM is not None:
                e.dma(DSTDRAM[t * 128:(t + 1) * 128, :], xa[:], after=[("vector", lv)])
            cs = T("scalar", lambda: nc.scalar.copy(out=scr_bf[:], in_=xa[:]),
                   after=[("vector", lv), ("tensor", e.now("tensor"))])
            for ic in range(NC):
                T("tensor", lambda ic=ic: nc.tensor.transpose(
                    out=ps_t[:, ic * 128:(ic + 1) * 128], in_=scr_bf[:, ic * 128:(ic + 1) * 128], identity=ident[:]),
                  after=[("scalar", cs)] if ic == 0 else [])
            tc = e.now("tensor")
            for ic in range(NC):
                T("scalar", lambda t=t, ic=ic: nc.scalar.copy(
                    out=xnT[:, ic * S + t * 128: ic * S + (t + 1) * 128], in_=ps_t[:, ic * 128:(ic + 1) * 128]),
                  after=[("tensor", tc)] if ic == 0 else [])

    ln_stream(X1I, XN1, 1e-5, rms=False)

    def load_w_full(W):
        for r in range(NC):
            e.dma(wfull[:, r * D:(r + 1) * D], W[r * 128:(r + 1) * 128, :],
                  after=[("tensor", e.now("tensor"))] if r == 0 else [], queue="gpsimd")
        return e.now("dmag")

    def transpose_w_to(WDST):
        dep = [("dmag", e.now("dmag"))]
        for oc in range(NC):
            for ic in range(NC):
                T("tensor", lambda oc=oc, ic=ic: nc.tensor.transpose(
                    out=ps_t[:, (ic % 8) * 128:(ic % 8 + 1) * 128],
                    in_=wfull[:, oc * D + ic * 128: oc * D + (ic + 1) * 128], identity=ident[:]),
                  after=dep + [("scalar", e.now("scalar"))] if ic == 0 else [])
            tc = e.now("tensor")
            for ic in range(NC):
                T("scalar", lambda oc=oc, ic=ic: nc.scalar.copy(
                    out=WDST[:, ic * D + oc * 128: ic * D + (oc + 1) * 128],
                    in_=ps_t[:, (ic % 8) * 128:(ic % 8 + 1) * 128]),
                  after=[("tensor", tc)] if ic == 0 else [])
        return e.now("scalar")

    def proj_from_pieces(PIECES, outT, bias_cols, ldep):
        for oc in range(NC):
            for nb in range(S // 512):
                ps = ps_k[nb % 4]
                deps = [("scalar", e.now("scalar")), ("vector", e.now("vector"))] + ldep
                for ic in range(NC):
                    T("tensor", lambda ic=ic, oc=oc, nb=nb, ps=ps: nc.tensor.matmul(
                        out=ps[:], lhsT=PIECES[:, ic * D + oc * 128: ic * D + (oc + 1) * 128],
                        rhs=xnT[:, ic * S + nb * 512: ic * S + (nb + 1) * 512],
                        start=(ic == 0), stop=(ic == NC - 1)), after=deps if ic == 0 else [])
                mmc = e.now("tensor")
                if bias_cols is not None:
                    T("scalar", lambda oc=oc, nb=nb, ps=ps, outT=outT: nc.scalar.activation(
                        out=outT[:, oc * S + nb * 512: oc * S + (nb + 1) * 512], in_=ps[:],
                        func=AF.Identity, bias=bias_cols[:, oc:oc + 1], scale=1.0), after=[("tensor", mmc)])
                else:
                    T("scalar", lambda oc=oc, nb=nb, ps=ps, outT=outT: nc.scalar.copy(
                        out=outT[:, oc * S + nb * 512: oc * S + (nb + 1) * 512], in_=ps[:]), after=[("tensor", mmc)])

    def proj_tokmajor(RHS_PIECES, dst_tok, brow_bf, ldep):
        for t in range(NT):
            for nb in range(2):
                ps = ps_k[(2 * t + nb) % 4]
                deps = [("scalar", e.now("scalar")), ("vector", e.now("vector"))] + ldep
                last = NC - 1 if brow_bf is None else -1
                for ic in range(NC):
                    T("tensor", lambda t=t, ic=ic, nb=nb, ps=ps, last=last: nc.tensor.matmul(
                        out=ps[:], lhsT=xnT[:, ic * S + t * 128: ic * S + (t + 1) * 128],
                        rhs=RHS_PIECES[:, ic * D + nb * 512: ic * D + (nb + 1) * 512],
                        start=(ic == 0), stop=(ic == last)), after=deps if ic == 0 else [])
                if brow_bf is not None:
                    T("tensor", lambda nb=nb, ps=ps, brow_bf=brow_bf: nc.tensor.matmul(
                        out=ps[:], lhsT=ones1[:], rhs=brow_bf[0:1, nb * 512:(nb + 1) * 512], start=False, stop=True))
                mmc = e.now("tensor")
                T("scalar", lambda t=t, nb=nb, ps=ps, dst_tok=dst_tok: nc.scalar.copy(
                    out=dst_tok[:, t * D + nb * 512: t * D + (nb + 1) * 512], in_=ps[:]), after=[("tensor", mmc)])

    bqs = e.sb("bqs", [128, NC])
    T("vector", lambda: nc.vector.tensor_scalar(out=bqs[:], in0=bqc[:], scalar1=0.125, scalar2=None, op0=ALU.mult),
      after=[("dma", ld_b)])
    load_w_full(AWQ)
    wdep = transpose_w_to(ctxT)
    for oc in range(NC):
        pass
    for oc in range(NC):
        for nb in range(S // 512):
            ps = ps_k[nb % 4]
            deps = [("scalar", e.now("scalar")), ("vector", e.now("vector"))]
            for ic in range(NC):
                T("tensor", lambda ic=ic, oc=oc, nb=nb, ps=ps: nc.tensor.matmul(
                    out=ps[:], lhsT=ctxT[:, ic * D + oc * 128: ic * D + (oc + 1) * 128],
                    rhs=xnT[:, ic * S + nb * 512: ic * S + (nb + 1) * 512],
                    start=(ic == 0), stop=(ic == NC - 1)), after=deps if ic == 0 else [])
            mmc = e.now("tensor")
            T("scalar", lambda oc=oc, nb=nb, ps=ps: nc.scalar.activation(
                out=qT[:, oc * S + nb * 512: oc * S + (nb + 1) * 512], in_=ps[:],
                func=AF.Identity, bias=bqs[:, oc:oc + 1], scale=0.125), after=[("tensor", mmc)])
    load_w_full(AWK)
    transpose_w_to(ctxT)
    proj_from_pieces(ctxT, kT, bkc, [])
    dbv = e.dma(rowbuf_bf[:], ABV[:], queue="gpsimd")
    load_w_full(AWV)
    transpose_w_to(ctxT)
    proj_tokmajor(ctxT, vtok, rowbuf_bf, [])

    U = 38
    units = [(h, qt) for h in range(H) for qt in range(NT)]
    wnB = [wn, e.sb_at("wn2", [128, S], BF16, wfull, 4096)]
    shB = [shared, e.sb_at("sh2", [128, S], BF16, wfull, 12288)]
    wtB = [causb, e.sb_at("wt2", [128, S], BF16, xb2, 0)]

    def l1a_qk(h, qt):
        c2, pb = h // 2, 64 * (h % 2)
        for kb in range(4):
            T("tensor", lambda c2=c2, pb=pb, qt=qt, kb=kb: nc.tensor.matmul(
                out=ps_k[kb][:], lhsT=qT[pb:pb + 64, c2 * S + qt * 128: c2 * S + (qt + 1) * 128],
                rhs=kT[pb:pb + 64, c2 * S + kb * 512: c2 * S + (kb + 1) * 512], start=True, stop=True),
              after=[("scalar", e.now("scalar")), ("vector", e.now("vector"))] if kb == 0 else [])
        return e.now("tensor")

    def l1a_soft(h, qt, mmc, par):
        wn = wnB[par]
        shared = shB[par]
        for kb in range(4):
            T("scalar", lambda kb=kb: nc.scalar.copy(out=ssb[:, kb * 512:(kb + 1) * 512], in_=ps_k[kb][:]),
              after=[("tensor", mmc), ("vector", e.now("vector"))] if kb == 0 else [])
        sevac = e.now("scalar")
        T("vector", lambda: nc.vector.tensor_copy(out=sel[:], in_=ssb[:]), after=[("scalar", sevac)])
        for rnd in range(5):
            T("vector", lambda: nc.vector.max(out=m8[:], in_=sel[:]))
            if rnd == 0:
                T("vector", lambda: nc.vector.tensor_copy(out=mcol[:], in_=m8[:, 0:1]))
            if rnd < 4:
                T("vector", lambda: nc.vector.match_replace(out=sel[:], in_to_replace=m8[:], in_values=sel[:],
                                                            imm_value=-1e30))
        T("vector", lambda: nc.vector.tensor_copy(out=vcol[:], in_=m8[:, 5:6]))
        T("vector", lambda: nc.vector.tensor_scalar(out=negm[:], in0=mcol[:], scalar1=-1.0, scalar2=None, op0=ALU.mult))
        ex = T("scalar", lambda shared=shared: nc.scalar.activation(out=shared[:], in_=ssb[:], func=AF.Exp, bias=negm[:]),
               after=[("vector", e.now("vector"))])
        T("vector", lambda wn=wn, shared=shared: nc.vector.scalar_tensor_tensor(
            out=wn[:], in0=ssb[:], scalar=vcol[:], in1=shared[:], op0=ALU.is_ge, op1=ALU.mult),
          after=[("scalar", ex)])
        zs = T("scalar", lambda wn=wn, shared=shared: nc.scalar.activation(
            out=shared[:], in_=wn[:], func=AF.Identity, bias=zb0[:], accum_out=zcol[:]),
            after=[("vector", e.now("vector"))])
        T("vector", lambda: nc.vector.reciprocal(out=rz[:], in_=zcol[:]), after=[("scalar", zs)])
        nw = T("vector", lambda: nc.vector.tensor_scalar(out=wn[:], in0=wn[:], scalar1=rz[:], scalar2=None, op0=ALU.mult))
        return nw

    def l1a_pv(h, qt, nw, par):
        wn = wnB[par]
        shared = wtB[par]
        c2, pb = h // 2, 64 * (h % 2)
        for c in range(16):
            pst = ps_t if c < 8 else ps_t2
            T("tensor", lambda c=c, pst=pst: nc.tensor.transpose(
                out=pst[:, (c % 8) * 128:(c % 8 + 1) * 128], in_=wn[:, c * 128:(c + 1) * 128], identity=ident[:]),
              after=[("vector", nw), ("scalar", e.now("scalar"))] if c == 0 else [])
        tcn = e.now("tensor")
        T("scalar", lambda: nc.scalar.copy(out=shared[:, 0:1024], in_=ps_t[:]), after=[("tensor", tcn)])
        ev2 = T("scalar", lambda: nc.scalar.copy(out=shared[:, 1024:2048], in_=ps_t2[:]))
        for c in range(16):
            T("tensor", lambda c=c, h=h: nc.tensor.matmul(
                out=ps_g2[0][0:64, 0:128], lhsT=vtok[:, c * D + h * 64: c * D + h * 64 + 64],
                rhs=shared[:, c * 128:(c + 1) * 128], start=(c == 0), stop=(c == 15)),
              after=[("scalar", ev2)] if c == 0 else [])
        pvc = e.now("tensor")
        T("scalar", lambda c2=c2, pb=pb, qt=qt: nc.scalar.copy(
            out=ctxT[pb:pb + 64, c2 * S + qt * 128: c2 * S + (qt + 1) * 128], in_=ps_g2[0][0:64, 0:128]),
          after=[("tensor", pvc)])

    mmc = l1a_qk(*units[0])
    pend = None
    for i, (h, qt) in enumerate(units):
        nw = l1a_soft(h, qt, mmc, i % 2)
        if i + 1 < len(units):
            mmc = l1a_qk(*units[i + 1])
        if pend is not None:
            l1a_pv(*pend)
        pend = (h, qt, nw, i % 2)
    l1a_pv(*pend)

    dbo = e.dma(rowbuf_bf[:], ABO[:], after=[("tensor", e.now("tensor"))], queue="gpsimd")
    load_w_full(AWO)
    wot = transpose_w_to(qT)
    for t in range(NT):
        for nb in range(2):
            ps = ps_k[(2 * t + nb) % 4]
            deps = [("scalar", e.now("scalar")), ("vector", e.now("vector")), ("dmag", e.now("dmag"))]
            for ic in range(NC):
                T("tensor", lambda t=t, ic=ic, nb=nb, ps=ps: nc.tensor.matmul(
                    out=ps[:], lhsT=ctxT[:, ic * S + t * 128: ic * S + (t + 1) * 128],
                    rhs=qT[:, ic * D + nb * 512: ic * D + (nb + 1) * 512],
                    start=(ic == 0), stop=False), after=deps if ic == 0 else [])
            T("tensor", lambda nb=nb, ps=ps: nc.tensor.matmul(
                out=ps[:], lhsT=ones1[:], rhs=rowbuf_bf[0:1, nb * 512:(nb + 1) * 512], start=False, stop=True))
            mmc = e.now("tensor")
            sv = T("scalar", lambda nb=nb, ps=ps: nc.scalar.copy(out=scr32[:, nb * 512:(nb + 1) * 512], in_=ps[:]),
                   after=[("tensor", mmc), ("dma", e.now("dma"))])
        e.dma(A1[t * 128:(t + 1) * 128, :], scr32[:], after=[("scalar", e.now("scalar"))])
        if debug:
            e.dma(A1_D[t * 128:(t + 1) * 128, :], scr32[:], after=[("scalar", e.now("scalar"))])

    ds1 = e.dma(scr32[0:1, :], SC1[:], after=[("scalar", e.now("scalar")), ("vector", e.now("vector")),
                                              ("dma", e.now("dma"))])
    cs1 = T("vector", lambda: nc.vector.tensor_copy(out=rowb2[:], in_=scr32[0:1, :]), after=[("dma", ds1)])
    for nb in range(2):
        mm = T("tensor", lambda nb=nb: nc.tensor.matmul(
            out=ps_g2[0][:, 0:512], lhsT=ones1[:], rhs=rowb2[0:1, nb * 512:(nb + 1) * 512], start=True, stop=True),
            after=[("vector", cs1), ("scalar", e.now("scalar"))])
        T("scalar", lambda nb=nb: nc.scalar.copy(out=s1b[:, nb * 512:(nb + 1) * 512], in_=ps_g2[0][:, 0:512]),
          after=[("tensor", mm)])
    sb_done = e.now("scalar")
    T("vector", lambda: nc.vector.tensor_copy(out=tcol[:], in_=tcol[:]), after=[("scalar", sb_done)])
    ln_stream(XN1, None, 1e-6, rms=True, scale_b=s1b)

    def rope_chunks(tT):
        for c in range(4):
            x1 = tT[:, (2 * c) * S:(2 * c + 1) * S]
            x2 = tT[:, (2 * c + 1) * S:(2 * c + 2) * S]
            T("vector", lambda x1=x1: nc.vector.tensor_mul(out=ropeA[:], in0=x1, in1=cosb[:]),
              after=[("scalar", e.now("scalar"))])
            T("vector", lambda x1=x1: nc.vector.tensor_mul(out=ropeB[:], in0=x1, in1=sinb[:]))
            T("vector", lambda x2=x2: nc.vector.tensor_mul(out=ropeC[:], in0=x2, in1=sinb[:]))
            T("vector", lambda x2=x2: nc.vector.tensor_mul(out=ropeD[:], in0=x2, in1=cosb[:]))
            T("vector", lambda x1=x1: nc.vector.tensor_sub(out=x1, in0=ropeA[:], in1=ropeC[:]))
            T("vector", lambda x2=x2: nc.vector.tensor_add(out=x2, in0=ropeB[:], in1=ropeD[:]))

    ldq = load_w_full(FWQ)
    for oc in range(NC):
        for nb in range(S // 512):
            ps = ps_k[nb % 4]
            deps = [("scalar", e.now("scalar")), ("vector", e.now("vector")), ("dmag", ldq)]
            for ic in range(NC):
                T("tensor", lambda ic=ic, oc=oc, nb=nb, ps=ps: nc.tensor.matmul(
                    out=ps[:], lhsT=wfull[:, ic * D + oc * 128: ic * D + (oc + 1) * 128],
                    rhs=xnT[:, ic * S + nb * 512: ic * S + (nb + 1) * 512],
                    start=(ic == 0), stop=(ic == NC - 1)), after=deps if ic == 0 else [])
            mmc = e.now("tensor")
            T("scalar", lambda oc=oc, nb=nb, ps=ps: nc.scalar.activation(
                out=qT[:, oc * S + nb * 512: oc * S + (nb + 1) * 512], in_=ps[:],
                func=AF.Identity, scale=0.125), after=[("tensor", mmc)])
    ldk = load_w_full(FWK)
    for oc in range(NC):
        for nb in range(S // 512):
            ps = ps_k[nb % 4]
            deps = [("scalar", e.now("scalar")), ("vector", e.now("vector")), ("dmag", ldk)]
            for ic in range(NC):
                T("tensor", lambda ic=ic, oc=oc, nb=nb, ps=ps: nc.tensor.matmul(
                    out=ps[:], lhsT=wfull[:, ic * D + oc * 128: ic * D + (oc + 1) * 128],
                    rhs=xnT[:, ic * S + nb * 512: ic * S + (nb + 1) * 512],
                    start=(ic == 0), stop=(ic == NC - 1)), after=deps if ic == 0 else [])
            mmc = e.now("tensor")
            T("scalar", lambda oc=oc, nb=nb, ps=ps: nc.scalar.copy(
                out=kT[:, oc * S + nb * 512: oc * S + (nb + 1) * 512], in_=ps[:]), after=[("tensor", mmc)])
    ldv = load_w_full(FWV)
    proj_tokmajor(wfull, vtok, None, [("dmag", ldv)])
    dcs = e.dma(cosb[:], COS32[:], after=[("tensor", e.now("tensor")), ("scalar", e.now("scalar"))])
    dsn = e.dma(sinb[:], SIN32[:])
    T("vector", lambda: nc.vector.tensor_copy(out=tcol[:], in_=tcol[:]), after=[("dma", dsn)])
    rope_chunks(qT)
    rope_chunks(kT)
    dca = e.dma(causb[:], CAUS[:], after=[("tensor", e.now("tensor")), ("scalar", e.now("scalar")),
                                          ("vector", e.now("vector"))])
    T("vector", lambda: nc.vector.tensor_copy(out=tcol[:], in_=tcol[:]), after=[("dma", dca)])

    def l1c_qk(h, qt):
        nkb = ((qt + 1) * 128 + 511) // 512
        cc = h // 4
        rb = 32 * (h % 4)
        for kb in range(nkb):
            for xh in range(2):
                cblk = (2 * cc + xh) * S
                T("tensor", lambda qt=qt, kb=kb, rb=rb, cblk=cblk, xh=xh: nc.tensor.matmul(
                    out=ps_k[kb][:], lhsT=qT[rb:rb + 32, cblk + qt * 128: cblk + (qt + 1) * 128],
                    rhs=kT[rb:rb + 32, cblk + kb * 512: cblk + (kb + 1) * 512],
                    start=(xh == 0), stop=(xh == 1), tile_position=(rb, 0)),
                  after=[("scalar", e.now("scalar")), ("vector", e.now("vector"))] if (kb == 0 and xh == 0) else [])
        return e.now("tensor")

    def l1c_soft(h, qt, mmc, par):
        wnl = wnB[par]
        nkc = qt + 1
        nkb = (nkc * 128 + 511) // 512
        for kb in range(nkb):
            T("scalar", lambda kb=kb: nc.scalar.copy(out=ssb[:, kb * 512:(kb + 1) * 512], in_=ps_k[kb][:]),
              after=[("tensor", mmc)] if kb == 0 else [])
        sevac = e.now("scalar")
        dkb = (qt * 128) // 512
        dofs = qt * 128 - dkb * 512
        T("vector", lambda dkb=dkb, dofs=dofs: nc.vector.tensor_add(
            out=ssb[:, dkb * 512:(dkb + 1) * 512], in0=ssb[:, dkb * 512:(dkb + 1) * 512],
            in1=causb[:, (dofs // 128) * 512:(dofs // 128 + 1) * 512]),
          after=[("scalar", sevac)])
        nv = nkc * 128
        T("vector", lambda nv=nv: nc.vector.tensor_reduce(out=mcol[:], in_=ssb[:, 0:nv], axis=AX.X, op=ALU.max))
        T("vector", lambda: nc.vector.tensor_scalar(out=negm[:], in0=mcol[:], scalar1=-1.0, scalar2=None, op0=ALU.mult))
        ex = T("scalar", lambda nv=nv, wnl=wnl: nc.scalar.activation(out=wnl[:, 0:nv], in_=ssb[:, 0:nv], func=AF.Exp,
                                                                     bias=negm[:], accum_out=zcol[:]),
               after=[("vector", e.now("vector"))])
        T("vector", lambda: nc.vector.reciprocal(out=rz[:], in_=zcol[:]), after=[("scalar", ex)])
        nw = T("vector", lambda nv=nv, wnl=wnl: nc.vector.tensor_scalar(out=wnl[:, 0:nv], in0=wnl[:, 0:nv],
                                                                        scalar1=rz[:], scalar2=None, op0=ALU.mult))
        return nw

    def l1c_pv(h, qt, nw, par):
        wnl = wnB[par]
        c2 = h // 2
        pb = 64 * (h % 2)
        nkc = qt + 1
        for c in range(nkc):
            pst = ps_t if (c % 16) < 8 else ps_t2
            T("tensor", lambda c=c, pst=pst, wnl=wnl: nc.tensor.transpose(
                out=pst[:, (c % 8) * 128:(c % 8 + 1) * 128], in_=wnl[:, c * 128:(c + 1) * 128], identity=ident[:]),
              after=[("vector", nw), ("scalar", e.now("scalar"))] if c == 0 else [])
        tcn = e.now("tensor")
        T("scalar", lambda w=min(nkc, 8): nc.scalar.copy(out=shared[:, 0:w * 128], in_=ps_t[:, 0:w * 128]),
          after=[("tensor", tcn)])
        if nkc > 8:
            T("scalar", lambda w=nkc - 8: nc.scalar.copy(out=shared[:, 1024:1024 + w * 128], in_=ps_t2[:, 0:w * 128]))
        ev2 = e.now("scalar")
        for c in range(nkc):
            T("tensor", lambda c=c, h=h: nc.tensor.matmul(
                out=ps_g2[0][0:64, 0:128], lhsT=vtok[:, c * D + h * 64: c * D + h * 64 + 64],
                rhs=shared[:, c * 128:(c + 1) * 128], start=(c == 0), stop=(c == nkc - 1)),
              after=[("scalar", ev2)] if c == 0 else [])
        pvc = e.now("tensor")
        T("scalar", lambda c2=c2, pb=pb, qt=qt: nc.scalar.copy(
            out=ctxT[pb:pb + 64, c2 * S + qt * 128: c2 * S + (qt + 1) * 128], in_=ps_g2[0][0:64, 0:128]),
          after=[("tensor", pvc)])

    unitsC = [(h, qt) for h in range(H) for qt in range(NT)]
    mmc = l1c_qk(*unitsC[0])
    pend = None
    for i, (h, qt) in enumerate(unitsC):
        nw = l1c_soft(h, qt, mmc, i % 2)
        if i + 1 < len(unitsC):
            mmc = l1c_qk(*unitsC[i + 1])
        if pend is not None:
            l1c_pv(*pend)
        pend = (h, qt, nw, i % 2)
    l1c_pv(*pend)

    ldo = load_w_full(FWO)
    for t in range(NT):
        dxn = e.dma(xb2[:], XN1[t * 128:(t + 1) * 128, :], after=[("vector", e.now("vector"))])
        for nb in range(2):
            ps = ps_k[(2 * t + nb) % 4]
            deps = [("scalar", e.now("scalar")), ("vector", e.now("vector")), ("dmag", ldo), ("dma", dxn)]
            for ic in range(NC):
                T("tensor", lambda t=t, ic=ic, nb=nb, ps=ps: nc.tensor.matmul(
                    out=ps[:], lhsT=ctxT[:, ic * S + t * 128: ic * S + (t + 1) * 128],
                    rhs=wfull[:, ic * D + nb * 512: ic * D + (nb + 1) * 512],
                    start=(ic == 0), stop=(ic == NC - 1)), after=deps if ic == 0 else [])
            mmc = e.now("tensor")
            T("vector", lambda t=t, nb=nb, ps=ps: nc.vector.tensor_add(
                out=scr32[:, nb * 512:(nb + 1) * 512], in0=xb2[:, nb * 512:(nb + 1) * 512], in1=ps[:]),
              after=[("tensor", mmc), ("dma", e.now("dma"))])
        hv = e.now("vector")
        e.dma(H1[t * 128:(t + 1) * 128, :], scr32[:], after=[("vector", hv)])
        if debug:
            e.dma(H1_D[t * 128:(t + 1) * 128, :], scr32[:], after=[("vector", hv)])

    ds2 = e.dma(scr32[0:1, :], SC2[:], after=[("scalar", e.now("scalar")), ("vector", e.now("vector")),
                                              ("dma", e.now("dma"))])
    cs2 = T("vector", lambda: nc.vector.tensor_copy(out=rowb2[:], in_=scr32[0:1, :]), after=[("dma", ds2)])
    for nb in range(2):
        mm = T("tensor", lambda nb=nb: nc.tensor.matmul(
            out=ps_g2[0][:, 0:512], lhsT=ones1[:], rhs=rowb2[0:1, nb * 512:(nb + 1) * 512], start=True, stop=True),
            after=[("vector", cs2), ("scalar", e.now("scalar"))])
        T("scalar", lambda nb=nb: nc.scalar.copy(out=s1b[:, nb * 512:(nb + 1) * 512], in_=ps_g2[0][:, 0:512]),
          after=[("tensor", mm)])
    T("vector", lambda: nc.vector.tensor_copy(out=tcol[:], in_=tcol[:]), after=[("scalar", e.now("scalar"))])
    ln_stream(H1, None, 1e-6, rms=True, scale_b=s1b)

    for ic in range(NC):
        e.dma(wfull[:, ic * 8:(ic + 1) * 8], ROUT[ic * 128:(ic + 1) * 128, :],
              after=[("tensor", e.now("tensor"))] if ic == 0 else [], queue="gpsimd")
    ldr = e.now("dmag")
    for t in range(NT):
        dgn = e.dma(gn_t[:], GN[t * 128:(t + 1) * 128, :], after=[("vector", e.now("vector"))])
        deps = [("scalar", e.now("scalar")), ("vector", e.now("vector")), ("dmag", ldr)]
        for ic in range(NC):
            T("tensor", lambda t=t, ic=ic: nc.tensor.matmul(
                out=ps_g2[1][0:128, 0:8], lhsT=xnT[:, ic * S + t * 128: ic * S + (t + 1) * 128],
                rhs=wfull[:, ic * 8:(ic + 1) * 8], start=(ic == 0), stop=(ic == NC - 1)),
              after=deps if ic == 0 else [])
        mmc = e.now("tensor")
        T("vector", lambda: nc.vector.tensor_add(out=prb[:], in0=ps_g2[1][0:128, 0:8], in1=gn_t[:]),
          after=[("tensor", mmc), ("dma", dgn)])
        T("vector", lambda: nc.vector.tensor_reduce(out=mcol[:], in_=prb[:], axis=AX.X, op=ALU.max))
        T("vector", lambda: nc.vector.tensor_scalar(out=negm[:], in0=mcol[:], scalar1=-1.0, scalar2=None, op0=ALU.mult))
        exr = T("scalar", lambda: nc.scalar.activation(out=prb[:], in_=prb[:], func=AF.Exp, bias=negm[:],
                                                       accum_out=zcol[:]),
                after=[("vector", e.now("vector"))])
        T("vector", lambda: nc.vector.reciprocal(out=rz[:], in_=zcol[:]), after=[("scalar", exr)])
        T("vector", lambda: nc.vector.tensor_scalar(out=prb[:], in0=prb[:], scalar1=rz[:], scalar2=None, op0=ALU.mult))
        T("vector", lambda: nc.vector.max(out=m8[:], in_=prb[:]))
        T("vector", lambda: nc.vector.tensor_copy(out=vcol[:], in_=m8[:, 1:2]))
        T("vector", lambda: nc.vector.tensor_single_scalar(out=m8[:], in_=prb[:], scalar=vcol[:], op=ALU.is_ge))
        T("vector", lambda t=t: nc.vector.tensor_tensor(out=wts_all[:, t * 8:(t + 1) * 8], in0=prb[:],
                                                        in1=m8[:], op=ALU.mult))
        if debug and t == 0:
            e.dma(PRB_D[:], prb[:], after=[("vector", e.now("vector"))])
    if debug:
        e.dma(WTS_D[:], wts_all[:], after=[("vector", e.now("vector"))])

    e.dma(boutall[:], BOUT[:], after=[("tensor", e.now("tensor"))], queue="gpsimd")
    bout_ld = e.now("dmag")

    for ex in range(8):
        for r in range(NC):
            e.dma(qT[:, r * D:(r + 1) * D], W1[ex * D + r * 128: ex * D + (r + 1) * 128, :],
                  after=[("tensor", e.now("tensor"))] if r == 0 else [], queue="gpsimd")
            e.dma(kT[:, r * D:(r + 1) * D], W2[ex * D + r * 128: ex * D + (r + 1) * 128, :], queue="gpsimd")
            e.dma(vtok[:, r * D:(r + 1) * D], WOUT[ex * D + r * 128: ex * D + (r + 1) * 128, :], queue="gpsimd")
        e.dma(rowbuf_bf[:], B1[ex:ex + 1, :], queue="gpsimd")
        e.dma(rowb2[:], B2[ex:ex + 1, :], queue="gpsimd")
        wld = e.now("dmag")
        for t in range(NT):
            for nb in range(2):
                deps = [("scalar", e.now("scalar")), ("vector", e.now("vector")), ("dmag", wld)]
                for ic in range(NC):
                    T("tensor", lambda t=t, ic=ic, nb=nb: nc.tensor.matmul(
                        out=ps_k[nb][:], lhsT=xnT[:, ic * S + t * 128: ic * S + (t + 1) * 128],
                        rhs=qT[:, ic * D + nb * 512: ic * D + (nb + 1) * 512],
                        start=(ic == 0), stop=False), after=deps if ic == 0 else [])
                T("tensor", lambda nb=nb: nc.tensor.matmul(
                    out=ps_k[nb][:], lhsT=ones1[:], rhs=rowbuf_bf[0:1, nb * 512:(nb + 1) * 512],
                    start=False, stop=True))
                for ic in range(NC):
                    T("tensor", lambda t=t, ic=ic, nb=nb: nc.tensor.matmul(
                        out=ps_k[2 + nb][:], lhsT=xnT[:, ic * S + t * 128: ic * S + (t + 1) * 128],
                        rhs=kT[:, ic * D + nb * 512: ic * D + (nb + 1) * 512],
                        start=(ic == 0), stop=False))
                T("tensor", lambda nb=nb: nc.tensor.matmul(
                    out=ps_k[2 + nb][:], lhsT=ones1[:], rhs=rowb2[0:1, nb * 512:(nb + 1) * 512],
                    start=False, stop=True))
            mmc = e.now("tensor")
            for nb in range(2):
                T("scalar", lambda nb=nb: nc.scalar.activation(out=a1s[:, nb * 512:(nb + 1) * 512],
                                                               in_=ps_k[nb][:], func=AF.Silu),
                  after=[("tensor", mmc)] if nb == 0 else [])
                T("scalar", lambda nb=nb: nc.scalar.copy(out=a2s[:, nb * 512:(nb + 1) * 512], in_=ps_k[2 + nb][:]))
            aev = e.now("scalar")
            hv = T("vector", lambda t=t, ex=ex: nc.vector.scalar_tensor_tensor(
                out=a1s[:], in0=a1s[:], scalar=wts_all[:, t * 8 + ex: t * 8 + ex + 1], in1=a2s[:],
                op0=ALU.mult, op1=ALU.mult), after=[("scalar", aev)])
            for ic in range(NC):
                T("tensor", lambda ic=ic: nc.tensor.transpose(
                    out=ps_t[:, ic * 128:(ic + 1) * 128], in_=a1s[:, ic * 128:(ic + 1) * 128], identity=ident[:]),
                  after=[("vector", hv), ("scalar", e.now("scalar"))] if ic == 0 else [])
            tcn = e.now("tensor")
            T("scalar", lambda: nc.scalar.copy(out=heT[:], in_=ps_t[:, 0:1024]), after=[("tensor", tcn)])
            hev = e.now("scalar")
            for nb in range(2):
                deps2 = [("scalar", hev)]
                last = NC - 1 if ex != 7 else -1
                for ic in range(NC):
                    T("tensor", lambda ic=ic, nb=nb, last=last: nc.tensor.matmul(
                        out=ps_g2[nb][:], lhsT=heT[:, ic * 128:(ic + 1) * 128],
                        rhs=vtok[:, ic * D + nb * 512: ic * D + (nb + 1) * 512],
                        start=(ic == 0), stop=(ic == last)), after=deps2 if ic == 0 else [])
                if ex == 7:
                    T("tensor", lambda t=t, nb=nb: nc.tensor.matmul(
                        out=ps_g2[nb][:], lhsT=wtsT[:], rhs=boutall[:, nb * 512:(nb + 1) * 512],
                        start=False, stop=True))
            mmc = e.now("tensor")
            sv = T("scalar", lambda: nc.scalar.copy(out=scr32[:, 0:512], in_=ps_g2[0][:]),
                   after=[("tensor", mmc), ("dmag", e.now("dmag"))])
            sv2 = T("scalar", lambda: nc.scalar.copy(out=scr32[:, 512:1024], in_=ps_g2[1][:]))
            op = ALU.bypass if ex == 0 else ALU.add
            md = e.op("gpsimd", lambda t=t, op=op: nc.gpsimd.dma_start(
                out=MOE[t * 128:(t + 1) * 128, :], in_=scr32[:], accum_op=op), after=[("scalar", sv2)], inc="dmag")
            if ex == 6:
                wm = T("tensor", lambda t=t: nc.tensor.matmul(
                    out=ps_g2[0][0:8, 0:128], lhsT=wts_all[:, t * 8:(t + 1) * 8], rhs=idf32[:],
                    start=True, stop=True),
                    after=[("scalar", e.now("scalar")), ("vector", e.now("vector"))])
                T("scalar", lambda: nc.scalar.copy(out=wtsT[:], in_=ps_g2[0][0:8, 0:128]), after=[("tensor", wm)])

    for t in range(NT):
        d1 = e.dma(xa[:], X1I[t * 128:(t + 1) * 128, :], after=[("vector", e.now("vector")), ("scalar", e.now("scalar"))])
        d2 = e.dma(xb2[:], A1[t * 128:(t + 1) * 128, :], after=[("vector", e.now("vector"))])
        v1 = T("vector", lambda: nc.vector.tensor_add(out=xa[:], in0=xa[:], in1=xb2[:]), after=[("dma", d2)])
        d3 = e.dma(xb2[:], H1[t * 128:(t + 1) * 128, :], after=[("vector", v1)])
        v2 = T("vector", lambda: nc.vector.tensor_add(out=xa[:], in0=xa[:], in1=xb2[:]), after=[("dma", d3)])
        d4 = e.dma(xb2[:], MOE[t * 128:(t + 1) * 128, :], after=[("vector", v2), ("dmag", e.now("dmag"))])
        v3 = T("vector", lambda: nc.vector.tensor_add(out=xa[:], in0=xa[:], in1=xb2[:]), after=[("dma", d4)])
        e.dma(X2O[t * 128:(t + 1) * 128, :], xa[:], after=[("vector", v3)])
        if debug:
            dm = e.dma(xb2[:], MOE[t * 128:(t + 1) * 128, :], after=[("vector", e.now("vector"))])
            vv = T("vector", lambda: nc.vector.tensor_copy(out=scr32[:], in_=xb2[:]), after=[("dma", dm)])
            e.dma(MOE_D[t * 128:(t + 1) * 128, :], scr32[:], after=[("vector", vv)])

    return


def build_l1(debug=False):
    e = Emitter("l6")
    emit_l1(e, fused=False, debug=debug)
    e.op("gpsimd", lambda: None, after=e.all_now())
    return e.emit_and_close()


def emit_ffn_fused(e, debug=False):
    nc = e.nc
    e.phase("B")
    T = e.op
    x = e.dr("x", [S, D])
    WUP = e.dr("wup", [3072, D])
    WGATE = e.dr("wgate", [3072, D])
    WDOWN = e.dr("wdown", [D, 3072])
    IDF32 = e.dr("identf32", [128, 128], F32)
    IDENT = e.dr("identf", [128, 128], BF16)
    ATTN0 = e.dr("attn0", [S, D], F32, kind="Internal")
    X1 = e.dr("x1", [S, D], F32, kind="Internal")
    GD = e.dr("gd", [3072, S], F32, kind="Internal")
    UD = e.dr("ud", [3072, S], F32, kind="Internal")
    OD = e.dr("od", [3072, S], BF16, kind="Internal")

    xa = e.sb("xa", [128, D])
    xqT = e.sb("xqT", [128, NC * S], BF16)
    scr32 = e.sb("scr32", [128, D])
    scr_bf = e.sb("scr_bf", [128, D], BF16)
    w32 = e.sb("w32", [128, 3072])
    wbf = e.sb("wbf", [128, 3072], BF16)
    wT = e.sb("wT", [128, NC * 128], BF16)
    bm = e.sb("bm", [128, 96])
    rbm = e.sb("rbm", [128, 96])
    upf = e.sb("upf", [128, S])
    gf = e.sb("gf", [128, S])
    sigf = e.sb("sigf", [128, S])
    obf = e.sb("obf", [128, S], BF16)
    mcol = e.sb("mcol", [128, 1])
    vcol = e.sb("vcol", [128, 1])
    tcol = e.sb("tcol", [128, 1])
    rcol = e.sb("rcol", [128, 1])
    mx_up = e.sb("mx_up", [128, 1])
    mx_g = e.sb("mx_g", [128, 1])
    s_sc = e.sb("s_sc", [1, 8])
    scol_xn = e.sb("scol_xn", [128, 1])
    scol_up = e.sb("scol_up", [128, 1])
    scol_g = e.sb("scol_g", [128, 1])
    scol_g2 = e.sb("scol_g2", [128, 1])
    scol_o = e.sb("scol_o", [128, 1])
    onef32 = e.sb("onef32", [1, 128])
    idf32 = e.sb("idf32", [128, 128])
    identb = e.sb("identb", [128, 128], BF16)
    mrow = e.sb("mrow", [1, 128])
    wdT = e.sb("wdT", [128, 24 * 1024], BF16)
    oT_t = e.sb("oT_t", [128, 24 * 128], BF16)
    res32 = e.sb("res32", [128, D])

    ps_k = [e.psum(f"psB_{i}", [128, 512], F32) for i in range(4)]
    ps_g2 = [e.psum(f"psgB_{i}", [128, 512], F32) for i in range(2)]
    ps_t = e.psum("ps_tB", [128, 1024], BF16)
    ps_h = e.psum("ps_hB", [128, 512], F32)

    e.dma(idf32[:], IDF32[:])
    e.dma(identb[:], IDENT[:])
    T("vector", lambda: nc.vector.memset(onef32[:], 1.0))

    def bcast_col(src11, dst_col, dep):
        mm = T("tensor", lambda src11=src11: nc.tensor.matmul(
            out=ps_h[0:128, 0:1], lhsT=onef32[:], rhs=src11, start=True, stop=True),
            after=dep + [("scalar", e.now("scalar")), ("dma", e.now("dma"))])
        return T("scalar", lambda dst_col=dst_col: nc.scalar.copy(out=dst_col[:], in_=ps_h[0:128, 0:1]),
                 after=[("tensor", mm)])

    def scale_from_cc(slot, dst_col, cc_dep):
        flat = bass.AP(e.ccbuf.tensor, slot, [[1, 1], [1, 1]])
        d = e.dma(s_sc[0:1, 0:1], flat, after=cc_dep + [("vector", e.now("vector")), ("tensor", e.now("tensor"))],
                  queue="gpsimd")
        T("vector", lambda: nc.vector.tensor_scalar(out=s_sc[0:1, 0:1], in0=s_sc[0:1, 0:1], scalar1=1.0 / 32767.0,
                                                    scalar2=1e-12, op0=ALU.mult, op1=ALU.max),
          after=[("dmag", d)])
        return bcast_col(s_sc[0:1, 0:1], dst_col, [("vector", e.now("vector"))])

    def qdq_inplace(ap, scol):
        T("vector", lambda: nc.vector.reciprocal(out=rcol[:], in_=scol[:]), after=[("scalar", e.now("scalar"))])
        T("vector", lambda ap=ap: nc.vector.tensor_scalar(out=ap, in0=ap, scalar1=rcol[:], scalar2=None, op0=ALU.mult))
        T("vector", lambda ap=ap: nc.vector.tensor_scalar(out=ap, in0=ap, scalar1=MAGIC, scalar2=-MAGIC, op0=ALU.add, op1=ALU.add))
        T("vector", lambda ap=ap, scol=scol: nc.vector.tensor_scalar(out=ap, in0=ap, scalar1=scol[:], scalar2=None, op0=ALU.mult))

    scale_from_cc(0, scol_xn, [("gpsimd", e.now("gpsimd"))])

    for t in range(NT):
        dx = e.dma(xa[:], x[t * 128:(t + 1) * 128, :], after=[("scalar", e.now("scalar")), ("vector", e.now("vector"))])
        T("vector", lambda: nc.vector.reduce_sum(out=mcol[:], in_=xa[:], axis=AX.X), after=[("dma", dx)])
        sq = T("scalar", lambda: nc.scalar.activation(out=scr32[:], in_=xa[:], func=AF.Square, accum_out=vcol[:]),
               after=[("dma", dx)])
        T("vector", lambda: nc.vector.tensor_scalar(out=mcol[:], in0=mcol[:], scalar1=1.0 / D, scalar2=None, op0=ALU.mult))
        T("vector", lambda: nc.vector.tensor_scalar(out=vcol[:], in0=vcol[:], scalar1=1.0 / D, scalar2=None, op0=ALU.mult),
          after=[("scalar", sq)])
        T("vector", lambda: nc.vector.tensor_scalar(out=tcol[:], in0=mcol[:], scalar1=mcol[:], scalar2=None, op0=ALU.mult))
        T("vector", lambda: nc.vector.tensor_sub(out=vcol[:], in0=vcol[:], in1=tcol[:]))
        vv = T("vector", lambda: nc.vector.tensor_scalar(out=vcol[:], in0=vcol[:], scalar1=1e-5, scalar2=None, op0=ALU.add))
        sv = T("scalar", lambda: nc.scalar.activation(out=vcol[:], in_=vcol[:], func=AF.Sqrt), after=[("vector", vv)])
        T("vector", lambda: nc.vector.reciprocal(out=vcol[:], in_=vcol[:]), after=[("scalar", sv)])
        T("vector", lambda: nc.vector.tensor_scalar(out=xa[:], in0=xa[:], scalar1=mcol[:], scalar2=vcol[:],
                                                    op0=ALU.subtract, op1=ALU.mult))
        qdq_inplace(xa[:], scol_xn)
        cs = T("scalar", lambda: nc.scalar.copy(out=scr_bf[:], in_=xa[:]),
               after=[("vector", e.now("vector")), ("tensor", e.now("tensor"))])
        for ic in range(NC):
            T("tensor", lambda ic=ic: nc.tensor.transpose(
                out=ps_t[:, ic * 128:(ic + 1) * 128], in_=scr_bf[:, ic * 128:(ic + 1) * 128], identity=identb[:]),
              after=[("scalar", cs)] if ic == 0 else [])
        tc = e.now("tensor")
        for ic in range(NC):
            T("scalar", lambda t=t, ic=ic: nc.scalar.copy(
                out=xqT[:, ic * S + t * 128: ic * S + (t + 1) * 128], in_=ps_t[:, ic * 128:(ic + 1) * 128]),
              after=[("tensor", tc)] if ic == 0 else [])
    T("vector", lambda: nc.vector.memset(mx_up[:], 0.0))
    T("vector", lambda: nc.vector.memset(mx_g[:], 0.0))

    def dequant_to_wT(Wsrc, oc):
        nblk = 32
        d = e.dma(w32[:, 0:D], Wsrc[oc * 128:(oc + 1) * 128, :],
                  after=[("vector", e.now("vector")), ("tensor", e.now("tensor"))])
        T("vector", lambda: nc.vector.tensor_reduce(
            out=bm[:, 0:nblk], in_=w32[:, 0:D].rearrange("p (b k) -> p b k", k=32),
            axis=AX.X, op=ALU.max, apply_absolute_value=True), after=[("dma", d)])
        T("vector", lambda: nc.vector.tensor_scalar(out=bm[:, 0:nblk], in0=bm[:, 0:nblk], scalar1=1.0 / 7.0,
                                                    scalar2=1e-12, op0=ALU.mult, op1=ALU.max))
        w3 = w32[:, 0:D].rearrange("p (b k) -> p b k", k=32)
        T("vector", lambda: nc.vector.reciprocal(out=rbm[:, 0:nblk], in_=bm[:, 0:nblk]))
        bmap = bm[:, 0:nblk]
        b3 = bass.AP(bmap.tensor, bmap.offset, [list(bmap.ap[0]), [bmap.ap[1][0], nblk], [0, 32]])
        rmap = rbm[:, 0:nblk]
        r3 = bass.AP(rmap.tensor, rmap.offset, [list(rmap.ap[0]), [rmap.ap[1][0], nblk], [0, 32]])
        T("vector", lambda: nc.vector.tensor_tensor(out=w3, in0=w3, in1=r3, op=ALU.mult))
        T("vector", lambda: nc.vector.tensor_scalar(out=w32[:, 0:D], in0=w32[:, 0:D], scalar1=MAGIC,
                                                    scalar2=-MAGIC, op0=ALU.add, op1=ALU.add))
        wv = T("vector", lambda: nc.vector.tensor_tensor(out=w3, in0=w3, in1=b3, op=ALU.mult))
        cb = T("scalar", lambda: nc.scalar.copy(out=wbf[:, 0:D], in_=w32[:, 0:D]), after=[("vector", wv)])
        for ic in range(NC):
            T("tensor", lambda ic=ic: nc.tensor.transpose(
                out=ps_t[:, ic * 128:(ic + 1) * 128], in_=wbf[:, ic * 128:(ic + 1) * 128], identity=identb[:]),
              after=[("scalar", cb), ("scalar", e.now("scalar"))] if ic == 0 else [])
        tc = e.now("tensor")
        for ic in range(NC):
            T("scalar", lambda ic=ic: nc.scalar.copy(
                out=wT[:, ic * 128:(ic + 1) * 128], in_=ps_t[:, ic * 128:(ic + 1) * 128]),
              after=[("tensor", tc)] if ic == 0 else [])
        return e.now("scalar")

    def hidden_matmul(dst_sb, oc, psl):
        for nb in range(4):
            ps = psl[nb % 2]
            deps = [("scalar", e.now("scalar")), ("vector", e.now("vector"))]
            for ic in range(NC):
                T("tensor", lambda ic=ic, nb=nb, ps=ps: nc.tensor.matmul(
                    out=ps[:], lhsT=wT[:, ic * 128:(ic + 1) * 128],
                    rhs=xqT[:, ic * S + nb * 512: ic * S + (nb + 1) * 512],
                    start=(ic == 0), stop=(ic == NC - 1)), after=deps if ic == 0 else [])
            mmc = e.now("tensor")
            T("scalar", lambda nb=nb, ps=ps, dst_sb=dst_sb: nc.scalar.copy(
                out=dst_sb[:, nb * 512:(nb + 1) * 512], in_=ps[:]), after=[("tensor", mmc)])
        return e.now("scalar")

    def maxacc(src_sb, acc_col):
        T("vector", lambda src_sb=src_sb: nc.vector.tensor_reduce(
            out=tcol[:], in_=src_sb[:], axis=AX.X, op=ALU.max, apply_absolute_value=True),
          after=[("scalar", e.now("scalar"))])
        return T("vector", lambda acc_col=acc_col: nc.vector.tensor_max(out=acc_col[:], in0=acc_col[:], in1=tcol[:]))

    def col_to_val(col, dst11):
        mm = T("tensor", lambda col=col: nc.tensor.matmul(out=ps_h[0:1, 0:128], lhsT=col[:], rhs=idf32[:],
                                                          start=True, stop=True),
               after=[("vector", e.now("vector")), ("scalar", e.now("scalar"))])
        sc = T("scalar", lambda: nc.scalar.copy(out=mrow[:], in_=ps_h[0:1, 0:128]), after=[("tensor", mm)])
        return T("vector", lambda dst11=dst11: nc.vector.tensor_reduce(out=dst11, in_=mrow[:], axis=AX.X, op=ALU.max),
                 after=[("scalar", sc)])

    for oc in range(24):
        dequant_to_wT(WGATE, oc)
        gev = hidden_matmul(gf, oc, ps_g2)
        e.dma(GD[oc * 128:(oc + 1) * 128, :], gf[:], after=[("scalar", gev)])
        maxacc(gf, mx_g)
        dequant_to_wT(WUP, oc)
        uev = hidden_matmul(upf, oc, [ps_k[2], ps_k[3]])
        e.dma(UD[oc * 128:(oc + 1) * 128, :], upf[:], after=[("scalar", uev)])
        maxacc(upf, mx_up)
    col_to_val(mx_up, s_sc[0:1, 0:1])
    v2 = col_to_val(mx_g, s_sc[0:1, 1:2])
    _, cc12 = e.allreduce_max(s_sc[0:1, 0:2], 2, after=[("vector", v2)])
    scale_from_cc(1, scol_up, [("gpsimd", cc12)])
    scale_from_cc(2, scol_g, [("gpsimd", cc12)])

    T("vector", lambda: nc.vector.memset(mx_g[:], 0.0))
    for oc in range(24):
        d = e.dma(gf[:], GD[oc * 128:(oc + 1) * 128, :],
                  after=[("vector", e.now("vector")), ("scalar", e.now("scalar")), ("dma", e.now("dma"))])
        T("vector", lambda: nc.vector.tensor_copy(out=tcol[:], in_=tcol[:]), after=[("dma", d)])
        qdq_inplace(gf[:], scol_g)
        sg = T("scalar", lambda: nc.scalar.activation(out=sigf[:], in_=gf[:], func=AF.Sigmoid),
               after=[("vector", e.now("vector"))])
        T("vector", lambda: nc.vector.tensor_scalar(out=sigf[:], in0=sigf[:], scalar1=65536.0, scalar2=MAGIC,
                                                    op0=ALU.mult, op1=ALU.add), after=[("scalar", sg)])
        T("vector", lambda: nc.vector.tensor_scalar(out=sigf[:], in0=sigf[:], scalar1=-MAGIC, scalar2=65535.0,
                                                    op0=ALU.add, op1=ALU.min))
        g2v = T("vector", lambda: nc.vector.scalar_tensor_tensor(out=gf[:], in0=sigf[:], scalar=1.0 / 65536.0,
                                                                 in1=gf[:], op0=ALU.mult, op1=ALU.mult))
        e.dma(GD[oc * 128:(oc + 1) * 128, :], gf[:], after=[("vector", g2v)])
        maxacc(gf, mx_g)
    v3 = col_to_val(mx_g, s_sc[0:1, 0:1])
    _, cc3 = e.allreduce_max(s_sc[0:1, 0:1], 1, after=[("vector", v3)])
    scale_from_cc(3, scol_g2, [("gpsimd", cc3)])

    T("vector", lambda: nc.vector.memset(mx_g[:], 0.0))
    for oc in range(24):
        d1 = e.dma(gf[:], GD[oc * 128:(oc + 1) * 128, :],
                   after=[("vector", e.now("vector")), ("scalar", e.now("scalar")), ("dma", e.now("dma"))])
        d2 = e.dma(upf[:], UD[oc * 128:(oc + 1) * 128, :])
        T("vector", lambda: nc.vector.tensor_copy(out=tcol[:], in_=tcol[:]), after=[("dma", d2)])
        qdq_inplace(gf[:], scol_g2)
        qdq_inplace(upf[:], scol_up)
        ov = T("vector", lambda: nc.vector.tensor_mul(out=gf[:], in0=gf[:], in1=upf[:]))
        e.dma(UD[oc * 128:(oc + 1) * 128, :], gf[:], after=[("vector", ov)])
        maxacc(gf, mx_g)
    v4 = col_to_val(mx_g, s_sc[0:1, 0:1])
    _, cc4 = e.allreduce_max(s_sc[0:1, 0:1], 1, after=[("vector", v4)])
    scale_from_cc(4, scol_o, [("gpsimd", cc4)])

    for oc in range(24):
        d1 = e.dma(gf[:], UD[oc * 128:(oc + 1) * 128, :],
                   after=[("vector", e.now("vector")), ("scalar", e.now("scalar")), ("dma", e.now("dma"))])
        T("vector", lambda: nc.vector.tensor_copy(out=tcol[:], in_=tcol[:]), after=[("dma", d1)])
        qdq_inplace(gf[:], scol_o)
        ob = T("vector", lambda: nc.vector.tensor_copy(out=obf[:], in_=gf[:]), after=[("dma", e.now("dma"))])
        e.dma(OD[oc * 128:(oc + 1) * 128, :], obf[:], after=[("vector", ob)])

    for c in range(NC):
        for g3 in range(3):
            dg = e.dma(w32[:, 0:D], WDOWN[c * 128:(c + 1) * 128, g3 * 1024:(g3 + 1) * 1024],
                       after=[("vector", e.now("vector")), ("tensor", e.now("tensor"))])
            nblk = 32
            T("vector", lambda: nc.vector.tensor_reduce(
                out=bm[:, 0:nblk], in_=w32[:, 0:D].rearrange("p (b k) -> p b k", k=32),
                axis=AX.X, op=ALU.max, apply_absolute_value=True), after=[("dma", dg)])
            T("vector", lambda: nc.vector.tensor_scalar(out=bm[:, 0:nblk], in0=bm[:, 0:nblk], scalar1=1.0 / 7.0,
                                                        scalar2=1e-12, op0=ALU.mult, op1=ALU.max))
            w3 = w32[:, 0:D].rearrange("p (b k) -> p b k", k=32)
            T("vector", lambda: nc.vector.reciprocal(out=rbm[:, 0:nblk], in_=bm[:, 0:nblk]))
            bmap = bm[:, 0:nblk]
            b3 = bass.AP(bmap.tensor, bmap.offset, [list(bmap.ap[0]), [bmap.ap[1][0], nblk], [0, 32]])
            rmap = rbm[:, 0:nblk]
            r3 = bass.AP(rmap.tensor, rmap.offset, [list(rmap.ap[0]), [rmap.ap[1][0], nblk], [0, 32]])
            T("vector", lambda w3=w3, r3=r3: nc.vector.tensor_tensor(out=w3, in0=w3, in1=r3, op=ALU.mult))
            T("vector", lambda: nc.vector.tensor_scalar(out=w32[:, 0:D], in0=w32[:, 0:D], scalar1=MAGIC,
                                                        scalar2=-MAGIC, op0=ALU.add, op1=ALU.add))
            wv = T("vector", lambda w3=w3, b3=b3: nc.vector.tensor_tensor(out=w3, in0=w3, in1=b3, op=ALU.mult))
            cb = T("scalar", lambda: nc.scalar.copy(out=wbf[:, 0:D], in_=w32[:, 0:D]), after=[("vector", wv)])
            for ic in range(8):
                T("tensor", lambda ic=ic: nc.tensor.transpose(
                    out=ps_t[:, ic * 128:(ic + 1) * 128], in_=wbf[:, ic * 128:(ic + 1) * 128], identity=identb[:]),
                  after=[("scalar", cb), ("scalar", e.now("scalar"))] if ic == 0 else [])
            tc = e.now("tensor")
            for ic in range(8):
                icg = g3 * 8 + ic
                T("scalar", lambda icg=icg, ic=ic, c=c: nc.scalar.copy(
                    out=wdT[:, icg * 1024 + c * 128: icg * 1024 + (c + 1) * 128],
                    in_=ps_t[:, ic * 128:(ic + 1) * 128]),
                  after=[("tensor", tc)] if ic == 0 else [])
    wd_done = e.now("scalar")

    for t in range(NT):
        dl = []
        for icg in range(24):
            dl.append(e.dma(oT_t[:, icg * 128:(icg + 1) * 128],
                            OD[icg * 128:(icg + 1) * 128, t * 128:(t + 1) * 128],
                            after=[("tensor", e.now("tensor"))] if icg == 0 else []))
        dx = e.dma(res32[:], x[t * 128:(t + 1) * 128, :], after=[("vector", e.now("vector"))])
        dat = e.dma(xa[:], ATTN0[t * 128:(t + 1) * 128, :])
        av = T("vector", lambda: nc.vector.tensor_add(out=res32[:], in0=res32[:], in1=xa[:]),
               after=[("dma", e.now("dma"))])
        for nb in range(2):
            ps = ps_k[nb]
            deps = [("scalar", wd_done), ("vector", av), ("dma", dl[-1]), ("scalar", e.now("scalar"))]
            for icg in range(24):
                T("tensor", lambda icg=icg, nb=nb, ps=ps: nc.tensor.matmul(
                    out=ps[:], lhsT=oT_t[:, icg * 128:(icg + 1) * 128],
                    rhs=wdT[:, icg * 1024 + nb * 512: icg * 1024 + (nb + 1) * 512],
                    start=(icg == 0), stop=(icg == 23)), after=deps if icg == 0 else [])
            mmc = e.now("tensor")
            fv = T("vector", lambda t=t, nb=nb, ps=ps: nc.vector.tensor_add(
                out=res32[:, nb * 512:(nb + 1) * 512], in0=res32[:, nb * 512:(nb + 1) * 512], in1=ps[:]),
                after=[("tensor", mmc)])
            e.dma(X1[t * 128:(t + 1) * 128, nb * 512:(nb + 1) * 512], res32[:, nb * 512:(nb + 1) * 512],
                  after=[("vector", fv)])
    return


def build_fused(debug=False):
    e = Emitter("fused", fused=True)
    emit_l0attn(e, fused=True, debug=False)
    e.fence()
    emit_ffn_fused(e, debug=False)
    e.fence()
    emit_l1(e, fused=True, debug=debug)
    e.op("gpsimd", lambda: None, after=e.all_now())
    return e.emit_and_close()


def _fused_in_map(inputs, c):
    m = dict(_l1_in_map(inputs, c))
    m.update({"wup": inputs["L0_ffn_w_up"], "wgate": inputs["L0_ffn_w_gate"],
              "wdown": inputs["L0_ffn_w_down"]})
    m6 = _l6_in_map(inputs, c, None)
    m6.pop("x1")
    m.update(m6)
    return {k: np.ascontiguousarray(v) for k, v in m.items()}


def kernel_fused(**inputs):
    inputs = {k: np.asarray(v, dtype=np.float32) for k, v in inputs.items()}
    ncf = build_fused()
    res = run_bass_kernel_spmd(ncf, [_fused_in_map(inputs, c) for c in range(8)], core_ids=list(range(8)))
    return np.stack([np.asarray(res.results[c]["x2"]) for c in range(8)])
